# version 07
# baseline (speedup 1.0000x reference)
import numpy as np
import ml_dtypes
from contextlib import ExitStack

import concourse.bass as bass
import concourse.mybir as mybir
from concourse.bass_utils import run_bass_kernel_spmd
from concourse.aot_env import allow_ici_all_to_all

F32 = mybir.dt.float32
BF16 = mybir.dt.bfloat16
AF = mybir.ActivationFunctionType
ALU = mybir.AluOpType
AX = mybir.AxisListType

D = 4096
T = 8192
SEQ = 4096
HSL = 512
NHL = 4
HD = 128
OWN = 1024
RANK = 1024
INTER = 11008
NFC = D // 128
CHK = 256
NCH = T // CHK
TPC = CHK // 128
OTC = 10
NLT = T // 128
NTILE = NLT + 8
INV_SQRT_HD = float(1.0 / np.sqrt(128.0))
EPS = 1e-5


class SemC:

    def __init__(self, handle):
        self.h = handle
        self.n = 0

    def inc(self, instr, k=1):
        instr.then_inc(self.h, k)
        self.n += k
        return self.n


def build(dump=()):
    nc = bass.Bass("TRN2", target_bir_lowering=False)

    x = nc.dram_tensor("x", [T, D], F32, kind="ExternalInput")
    x_own = nc.dram_tensor("x_own", [OWN, D], F32, kind="ExternalInput")
    x_f = nc.dram_tensor("x_f", [D, T], F32, kind="ExternalInput")
    xf_own = nc.dram_tensor("xf_own", [D, OWN], F32, kind="ExternalInput")
    wq_t = nc.dram_tensor("wq_t", [D, HSL], F32, kind="ExternalInput")
    wk_t = nc.dram_tensor("wk_t", [D, HSL], F32, kind="ExternalInput")
    wv_t = nc.dram_tensor("wv_t", [D, HSL], F32, kind="ExternalInput")
    bq = nc.dram_tensor("bq", [1, HSL], F32, kind="ExternalInput")
    bk = nc.dram_tensor("bk", [1, HSL], F32, kind="ExternalInput")
    bv = nc.dram_tensor("bv", [1, HSL], F32, kind="ExternalInput")
    wo_t = nc.dram_tensor("wo_t", [D, D], F32, kind="ExternalInput")
    bo = nc.dram_tensor("bo", [1, D], F32, kind="ExternalInput")
    u1 = nc.dram_tensor("u1", [D, RANK], F32, kind="ExternalInput")
    v1 = nc.dram_tensor("v1", [RANK, 2 * INTER], F32, kind="ExternalInput")
    u2 = nc.dram_tensor("u2", [INTER, RANK], F32, kind="ExternalInput")
    v2 = nc.dram_tensor("v2", [RANK, D], F32, kind="ExternalInput")
    b1 = nc.dram_tensor("b1", [1, 2 * INTER], F32, kind="ExternalInput")
    b2 = nc.dram_tensor("b2", [1, D], F32, kind="ExternalInput")
    out_own = nc.dram_tensor("out_own", [OWN, D], F32, kind="ExternalOutput")

    def scratch(name, shape, dtype):
        kind = "ExternalOutput" if name in dump else "Internal"
        return nc.dram_tensor(name, shape, dtype, kind=kind)

    q_d = scratch("q_d", [HSL, T], BF16)
    k_d = scratch("k_d", [HSL, T], BF16)
    v_d = scratch("v_d", [T, HSL], BF16)
    wo_b = scratch("wo_b", [D, D], BF16)
    mrown_d = scratch("mrown_d", [1, 2 * OWN], BF16)
    u1_b = scratch("u1_b", [D, RANK], BF16)
    u2_b = scratch("u2_b", [INTER, RANK], BF16)
    v2_b = scratch("v2_b", [RANK, D], BF16)
    v1_p = scratch("v1_p", [2 * (INTER // 128), 128, RANK], BF16)
    u_in = scratch("u_in", [8, HSL, OWN], BF16)
    u_out = scratch("u_out", [8, HSL, OWN], BF16)
    ffn_d = scratch("ffn_d", [OWN, D], BF16)

    ones_bcol = nc.inline_tensor(np.ones((128, 1), dtype=ml_dtypes.bfloat16),
                                 "ones_bcol")
    ones_frow = nc.inline_tensor(np.ones((1, 128), dtype=np.float32),
                                 "ones_frow")
    ones_frb = nc.inline_tensor(np.ones((1, 128), dtype=ml_dtypes.bfloat16),
                                "ones_frb")
    tri = np.triu(np.ones((128, 128), np.float32)).astype(ml_dtypes.bfloat16)
    tri_t = nc.inline_tensor(tri, "tri_t")
    ident_f32 = nc.inline_tensor(np.eye(128, dtype=np.float32), "ident_f32")

    with ExitStack() as top:
        sem = lambda name: SemC(top.enter_context(nc.semaphore(name)))

        sb_ones_b = nc.alloc_sbuf_tensor("sb_ones_b", [128, 1], BF16)
        sb_ones_f = nc.alloc_sbuf_tensor("sb_ones_f", [1, 128], F32)
        sb_tri = nc.alloc_sbuf_tensor("sb_tri", [128, 128], BF16)
        sb_bqkv = nc.alloc_sbuf_tensor("sb_bqkv", [128, 12], F32)
        sb_id32 = nc.alloc_sbuf_tensor("sb_id32", [128, 128], F32)
        sb_onesrb = nc.alloc_sbuf_tensor("sb_onesrb", [1, 128], BF16)

        s_const = sem("s_const")
        s_id32 = sem("s_id32")

        s_wt = [sem("s_wtq"), sem("s_wtk"), sem("s_wtv")]
        s_bgc = sem("s_bgc")
        s_bgv = sem("s_bgv")
        s_bgu1 = sem("s_bgu1")
        s_bgu2 = sem("s_bgu2")
        s_bgv2 = sem("s_bgv2")
        s_bv = sem("s_bv")
        s_bprep = sem("s_bprep")
        s_bcols = sem("s_bcols")
        s_ldx = [sem("s_ldx0"), sem("s_ldx1")]
        s_ln1 = sem("s_ln1")
        s_ln2 = sem("s_ln2")
        s_mrt = sem("s_mrt")
        s_mre = sem("s_mre")
        s_bcmm = sem("s_bcmm")
        s_bce = sem("s_bce")
        s_xf = [sem("s_xf0"), sem("s_xf1")]
        s_mrst = sem("s_mrst")
        s_xn = sem("s_xn")
        s_mm1 = sem("s_mm1")
        s_evq = sem("s_evq")
        s_evv = sem("s_evv")
        s_qstk = sem("s_qstk")
        s_qstv = sem("s_qstv")

        NCHA = NCH + (OWN // CHK)

        with ExitStack() as ph1:
            sb = lambda n, s, d: ph1.enter_context(nc.sbuf_tensor(n, s, d))
            pt = lambda n, s: ph1.enter_context(nc.psum_tensor(n, s, F32))
            w_qT = sb("w_qT", [128, NFC * 512], BF16)
            w_kT = sb("w_kT", [128, NFC * 512], BF16)
            w_vT = sb("w_vT", [128, NFC * 512], BF16)
            xfb = [sb(f"xfb{i}", [128, NFC * 128], F32) for i in range(2)]
            xnb = [sb(f"xnb{i}", [128, NFC * CHK], BF16) for i in range(2)]
            xb = [sb(f"xb{i}", [128, D], F32) for i in range(2)]
            stg = [sb(f"stg{i}", [128, CHK], BF16) for i in range(8)]
            stgv = [sb(f"stgv{i}", [128, 512], BF16) for i in range(2)]
            stats = [sb(f"stats{i}", [128, 48], F32) for i in range(2)]
            mv = [sb(f"mv{i}", [128, 2], F32) for i in range(4)]
            veps = [sb(f"veps{i}", [128, 1], F32) for i in range(2)]
            rstd = [sb(f"rstd{i}", [128, 1], F32) for i in range(2)]
            mr_sb = [sb(f"mr_sb{i}", [1, 512], BF16) for i in range(2)]
            mbc = [sb(f"mbc1_{i}", [128, 256], BF16) for i in range(2)]
            rbc1 = [sb(f"rbc1_{i}", [128, 256], BF16) for i in range(2)]
            sb_bv3 = sb("sb_bv3", [1, HSL], F32)
            ps = [pt(f"p1_{i}", [128, 512]) for i in range(4)]
            ps_b = pt("p1_b", [128, 512])
            ps_mr = ph1.enter_context(nc.psum_tensor("ps_mr", [1, 512], F32))
            ps_bc = pt("ps_bc", [128, 512])

            def tile_src(i):
                if i < NLT:
                    return x[i * 128:(i + 1) * 128, :]
                j = i - NLT
                return x_own[j * 128:(j + 1) * 128, :]

            evac_engine = []
            for c in range(NCH):
                evac_engine += ['A'] * 8 + ['V'] * 2
            actc = np.cumsum([1 if t == 'A' else 0 for t in evac_engine])
            dvec = np.cumsum([1 if t == 'V' else 0 for t in evac_engine])

            with nc.Block() as blk:
                @blk.gpsimd
                def _(g):
                    s_const.inc(g.dma_start(out=sb_ones_b[:, :],
                                            in_=ones_bcol.ap()), 16)
                    s_const.inc(g.dma_start(out=sb_ones_f[:, :],
                                            in_=ones_frow.ap()), 16)
                    s_const.inc(g.dma_start(out=sb_tri[:, :],
                                            in_=tri_t.ap()), 16)
                    s_id32.inc(g.dma_start(out=sb_id32[:, :],
                                           in_=ident_f32.ap()), 16)
                    s_id32.inc(g.dma_start(out=sb_onesrb[:, :],
                                           in_=ones_frb.ap()), 16)
                    for wi, (wsb, wsrc) in enumerate(
                            ((w_qT, wq_t), (w_kT, wk_t), (w_vT, wv_t))):
                        for f in range(NFC):
                            s_wt[wi].inc(g.dma_start(
                                out=wsb[:, f * 512:(f + 1) * 512],
                                in_=wsrc[f * 128:(f + 1) * 128, :]), 16)
                    bg = []
                    for i in range(8):
                        bg.append((s_bgc, wo_b[i * 512:(i + 1) * 512, :],
                                   wo_t[i * 512:(i + 1) * 512, :]))
                    for i in range(2):
                        bg.append((s_bgu1, u1_b[i * 2048:(i + 1) * 2048, :],
                                   u1[i * 2048:(i + 1) * 2048, :]))
                    for i in range(4):
                        bg.append((s_bgu2, u2_b[i * 2752:(i + 1) * 2752, :],
                                   u2[i * 2752:(i + 1) * 2752, :]))
                    for i in range(2):
                        bg.append((s_bgv2, v2_b[i * 512:(i + 1) * 512, :],
                                   v2[i * 512:(i + 1) * 512, :]))
                    for gu in range(2):
                        for ig in range(INTER // 128):
                            gidx = ig * 2 + gu
                            i0 = gu * INTER + ig * 128
                            for half in range(2):
                                srcv = v1[half * 512:(half + 1) * 512,
                                          i0:i0 + 128]
                                srcv = srcv.rearrange("(rc p) i -> p rc i",
                                                      p=128)
                                dstv = v1_p[gidx, :,
                                            half * 512:(half + 1) * 512]
                                dstv = dstv.rearrange("p (rc i) -> p rc i",
                                                      i=128)
                                bg.append((s_bgv, dstv, srcv))
                    bgi = [0]

                    def bg_pump(k):
                        for _ in range(k):
                            if bgi[0] < len(bg):
                                s_, dst, srcb = bg[bgi[0]]
                                s_.inc(g.dma_start(out=dst, in_=srcb), 16)
                                bgi[0] += 1

                    for c in range(NCH):
                        g.wait_ge(s_bce.h, 2 * (c + 1))
                        for h in range(2):
                            gix = 2 * c + h
                            g.wait_ge(s_xf[gix % 2].h, (gix // 2 + 1) * 16 * 16)
                            xf3 = xfb[gix % 2][:, :].rearrange(
                                "p (f t) -> p f t", t=128)
                            mslice = mbc[c % 2][:, h * 128:(h + 1) * 128]
                            m3 = bass.AP(mslice.tensor, mslice.offset,
                                         [list(mslice.ap[0]), [0, NFC],
                                          [1, 128]])
                            rslice = rbc1[c % 2][:, h * 128:(h + 1) * 128]
                            r3 = bass.AP(rslice.tensor, rslice.offset,
                                         [list(rslice.ap[0]), [0, NFC],
                                          [1, 128]])
                            xn3 = bass.AP(
                                xnb[c % 2][:, :].tensor,
                                xnb[c % 2][:, :].offset + h * 128,
                                [list(xnb[c % 2][:, :].ap[0]),
                                 [CHK, NFC], [1, 128]])
                            g.tensor_tensor(xf3, xf3, m3, op=ALU.subtract)
                            g.drain()
                            s_xn.inc(g.tensor_tensor(xn3, xf3, r3,
                                                     op=ALU.mult))
                        bg_pump(12)
                    bg_pump(len(bg))
                    g.wait_ge(s_bgc.h, s_bgc.n)
                    g.wait_ge(s_bgu1.h, s_bgu1.n)
                    g.wait_ge(s_bgu2.h, s_bgu2.n)
                    g.wait_ge(s_bgv2.h, s_bgv2.n)
                    g.wait_ge(s_bgv.h, s_bgv.n)

                @blk.sync
                def _(sy):
                    s_bv.inc(sy.dma_start(out=sb_bv3[:, :], in_=bq[:, :]), 16)
                    sy.wait_ge(s_bprep.h, 1)
                    s_bv.inc(sy.dma_start(out=sb_bv3[:, :], in_=bk[:, :]), 16)
                    sy.wait_ge(s_bprep.h, 2)
                    s_bv.inc(sy.dma_start(out=sb_bv3[:, :], in_=bv[:, :]), 16)
                    for i in range(2):
                        s_ldx[i % 2].inc(sy.dma_start(out=xb[i % 2][:, :],
                                                      in_=tile_src(i)), 16)
                    for c in range(NCHA):
                        if c < NCH:
                            for h in range(2):
                                gix = 2 * c + h
                                if gix >= 2:
                                    sy.wait_ge(s_xn.h, gix - 1)
                                t0 = c * CHK + h * 128
                                for fq in range(NFC // 2):
                                    srcf = x_f[fq * 256:(fq + 1) * 256,
                                               t0:t0 + 128]
                                    srcf = srcf.rearrange(
                                        "(fc p) t -> p fc t", p=128)
                                    dstf = xfb[gix % 2][:, fq * 256:
                                                        (fq + 1) * 256]
                                    dstf = dstf.rearrange(
                                        "p (fc t) -> p fc t", t=128)
                                    s_xf[gix % 2].inc(
                                        sy.dma_start(out=dstf, in_=srcf), 16)
                        for i in range(TPC * c + 2, TPC * c + 2 + TPC):
                            if i < NTILE:
                                sy.wait_ge(s_ln1.h, max(i - 1, 0))
                                s_ldx[i % 2].inc(sy.dma_start(
                                    out=xb[i % 2][:, :], in_=tile_src(i)), 16)

                @blk.vector
                def _(v):
                    def v_evacs(c):
                        if c >= 1:
                            v.wait_ge(s_qstv.h, 2 * c * 16)
                        for j in range(2):
                            gt = c * OTC + 8 + j
                            v.wait_ge(s_mm1.h, gt + 1)
                            s_evv.inc(v.tensor_copy(stgv[j][:, :],
                                                    ps[gt % 4][:, :]))

                    for p in range(NCHA):
                        i0, i1 = 2 * p, 2 * p + 1
                        for i in (i0, i1):
                            v.wait_ge(s_ldx[i % 2].h, (i // 2 + 1) * 16)
                        if p >= 1:
                            v.wait_ge(s_ln2.h, i0)
                        if p >= 2:
                            v.wait_ge(s_mrt.h, p - 1)
                        for i in (i0, i1):
                            sl = i % 2
                            for j in range(8):
                                v.bn_stats(stats[sl][:, 6 * j:6 * j + 6],
                                           xb[sl][:, 512 * j:512 * (j + 1)])
                        v.drain()
                        for i in (i0, i1):
                            v.bn_aggr(mv[i % 4][:, :], stats[i % 2][:, :])
                        v.drain()
                        for i in (i0, i1):
                            v.tensor_scalar(veps[i % 2][:, :],
                                            mv[i % 4][:, 1:2],
                                            EPS, None, ALU.add)
                        v.drain()
                        s_ln1.inc(v.reciprocal(veps[i0 % 2][:, :],
                                               veps[i0 % 2][:, :]))
                        s_ln1.inc(v.reciprocal(veps[i1 % 2][:, :],
                                               veps[i1 % 2][:, :]))
                        if p >= 2 and p - 2 < NCH:
                            v_evacs(p - 2)
                    for c in range(NCHA - 2, NCH):
                        v_evacs(c)

                @blk.tensor
                def _(t):
                    t.wait_ge(s_const.h, 48)
                    for wi in range(3):
                        t.wait_ge(s_bv.h, (wi + 1) * 16)
                        ins = None
                        for dc in range(4):
                            ins = t.matmul(
                                ps_b[:, wi * 4 + dc:wi * 4 + dc + 1],
                                sb_bv3[:, dc * 128:(dc + 1) * 128],
                                sb_ones_f[:, 0:1],
                                start=True, stop=True)
                        s_bprep.inc(ins)
                    t.wait_ge(s_id32.h, 32)

                    def mr_transp(c):
                        t.wait_ge(s_ln1.h, 2 * c + 2)
                        t.wait_ge(s_ln2.h, 2 * c + 2)
                        if c >= 1:
                            t.wait_ge(s_mre.h, c)
                        t.transpose(ps_mr[0:1, 0:128], mv[(2 * c) % 4][:, 0:1],
                                    sb_id32[:, :])
                        t.transpose(ps_mr[0:1, 128:256],
                                    mv[(2 * c + 1) % 4][:, 0:1], sb_id32[:, :])
                        t.transpose(ps_mr[0:1, 256:384], rstd[0][:, :],
                                    sb_id32[:, :])
                        ins = t.transpose(ps_mr[0:1, 384:512], rstd[1][:, :],
                                          sb_id32[:, :])
                        s_mrt.inc(ins)

                    def mr_bcast(c):
                        t.wait_ge(s_mre.h, c + 1)
                        if c >= 1:
                            t.wait_ge(s_bce.h, 2 * c)
                        t.matmul(ps_bc[:, 0:256], sb_onesrb[:, :],
                                 mr_sb[c % 2][0:1, 0:256],
                                 start=True, stop=True)
                        ins = t.matmul(ps_bc[:, 256:512], sb_onesrb[:, :],
                                       mr_sb[c % 2][0:1, 256:512],
                                       start=True, stop=True)
                        s_bcmm.inc(ins)

                    def qkv(c):
                        t.wait_ge(s_xn.h, 2 * (c + 1))
                        for j in range(OTC):
                            gt = c * OTC + j
                            b = gt % 4
                            if c == 0 and j in (0, 4, 8):
                                t.wait_ge(s_wt[j // 4].h, 32 * 16)
                            if gt >= 4:
                                pt_ = gt - 4
                                if evac_engine[pt_] == 'A':
                                    t.wait_ge(s_evq.h, int(actc[pt_]))
                                else:
                                    t.wait_ge(s_evv.h, int(dvec[pt_]))
                            if j < 8:
                                wsb = w_qT if j < 4 else w_kT
                                dc = j % 4
                                for f in range(NFC):
                                    ins = t.matmul(
                                        ps[b][:, 0:CHK],
                                        wsb[:, f * 512 + dc * 128:
                                            f * 512 + (dc + 1) * 128],
                                        xnb[c % 2][:, f * CHK:(f + 1) * CHK],
                                        start=(f == 0), stop=(f == NFC - 1))
                            else:
                                ts = j - 8
                                for f in range(NFC):
                                    ins = t.matmul(
                                        ps[b][:, :],
                                        xnb[c % 2][:, f * CHK + ts * 128:
                                                   f * CHK + (ts + 1) * 128],
                                        w_vT[:, f * 512:(f + 1) * 512],
                                        start=(f == 0), stop=(f == NFC - 1))
                            s_mm1.inc(ins)

                    for c in range(NCHA):
                        mr_transp(c)
                        if c >= 1 and c - 1 < NCH:
                            mr_bcast(c - 1)
                        if c >= 2 and c - 2 < NCH:
                            qkv(c - 2)
                    for c in range(NCHA - 2, NCH):
                        qkv(c)

                @blk.scalar
                def _(sc):
                    def qk_out(c):
                        sc.wait_ge(s_bcols.h, 1)
                        if c >= 1:
                            sc.wait_ge(s_qstk.h, 8 * c * 16)
                        for j in range(8):
                            gt = c * OTC + j
                            sc.wait_ge(s_mm1.h, gt + 1)
                            bias = sb_bqkv[:, j:j + 1]
                            n = s_evq.inc(sc.activation(
                                stg[j][:, 0:CHK], ps[gt % 4][:, 0:CHK],
                                AF.Identity, bias=bias))
                            sc.wait_ge(s_evq.h, n)
                            dst = (q_d if j < 4 else k_d)
                            dc = j % 4
                            s_qstk.inc(sc.dma_start(
                                out=dst[dc * 128:(dc + 1) * 128,
                                        c * CHK:(c + 1) * CHK],
                                in_=stg[j][:, 0:CHK]), 16)

                    def v_out(c):
                        for j in range(2):
                            gt = c * OTC + 8 + j
                            sc.wait_ge(s_evv.h, int(dvec[gt]))
                            s_qstv.inc(sc.dma_start(
                                out=v_d[c * CHK + j * 128:
                                        c * CHK + (j + 1) * 128, :],
                                in_=stgv[j][:, :]), 16)

                    sc.wait_ge(s_bprep.h, 3)
                    s_bcols.inc(sc.copy(sb_bqkv[:, :], ps_b[:, 0:12]))
                    for c in range(NCHA):
                        for i in (2 * c, 2 * c + 1):
                            sl = i % 2
                            sc.wait_ge(s_ln1.h, i + 1)
                            if i >= 4:
                                sc.wait_ge(s_mrt.h, i // 2)
                            s_ln2.inc(sc.activation(rstd[sl][:, :],
                                                    veps[sl][:, :], AF.Sqrt))
                        sc.wait_ge(s_mrt.h, c + 1)
                        if c < NCH:
                            n = s_mre.inc(sc.copy(mr_sb[c % 2][:, :],
                                                  ps_mr[0:1, :]))
                            if c >= 1:
                                sc.wait_ge(s_bcmm.h, c)
                                if c >= 3:
                                    sc.wait_ge(s_xn.h, 2 * (c - 2))
                                s_bce.inc(sc.copy(mbc[(c - 1) % 2][:, :],
                                                  ps_bc[:, 0:256]))
                                s_bce.inc(sc.copy(rbc1[(c - 1) % 2][:, :],
                                                  ps_bc[:, 256:512]))
                        else:
                            co = c - NCH
                            sc.wait_ge(s_bcmm.h, NCH)
                            if co == 0:
                                sc.wait_ge(s_xn.h, 2 * (c - 2))
                                s_bce.inc(sc.copy(mbc[(NCH - 1) % 2][:, :],
                                                  ps_bc[:, 0:256]))
                                s_bce.inc(sc.copy(rbc1[(NCH - 1) % 2][:, :],
                                                  ps_bc[:, 256:512]))
                            if co >= 2:
                                sc.wait_ge(s_mrst.h, co * 32)
                            n = s_mre.inc(sc.copy(mr_sb[c % 2][:, :],
                                                  ps_mr[0:1, :]))
                            sc.wait_ge(s_mre.h, n)
                            s_mrst.inc(sc.dma_start(
                                out=mrown_d[0:1, co * 256:(co + 1) * 256],
                                in_=mr_sb[c % 2][0:1, 0:256]), 16)
                            s_mrst.inc(sc.dma_start(
                                out=mrown_d[0:1, OWN + co * 256:
                                            OWN + (co + 1) * 256],
                                in_=mr_sb[c % 2][0:1, 256:512]), 16)
                        if c >= 3 and c - 3 < NCH:
                            v_out(c - 3)
                        if c >= 2 and c - 2 < NCH:
                            qk_out(c - 2)
                    for c in range(NCHA - 3, NCH):
                        v_out(c)
                    for c in range(NCHA - 2, NCH):
                        qk_out(c)
                    sc.wait_ge(s_qstk.h, s_qstk.n)
                    sc.wait_ge(s_qstv.h, s_qstv.n)
                    sc.wait_ge(s_mrst.h, s_mrst.n)

        s_qkvld = [sem("s_qkvld0"), sem("s_qkvld1")]
        s_mmA = sem("s_mmA")
        s_maxA = sem("s_maxA")
        s_cred = sem("s_cred")
        s_ctr = sem("s_ctr")
        s_cneg = sem("s_cneg")
        s_cbc = sem("s_cbc")
        s_cbias = sem("s_cbias")
        s_mmB = sem("s_mmB")
        s_expB = sem("s_expB")
        s_mskB = sem("s_mskB")
        s_pvmm = sem("s_pvmm")
        s_rcp = sem("s_rcp")
        s_rbcp = sem("s_rbcp")
        s_rbcs = sem("s_rbcs")
        s_mul = sem("s_mul")
        s_ustg = sem("s_ustg")
        s_uin = [sem("s_uin0"), sem("s_uin1")]
        s_a2a = sem("s_a2a")


        NIT = 8
        NQG = SEQ // 512
        KBS = SEQ // 128
        CPI = sum(4 * qg + 4 for qg in range(NQG))
        _chk_tot = []
        _acc = 0
        for _it in range(NIT):
            row = []
            for _qg in range(NQG):
                _acc += 4 * _qg + 4
                row.append(_acc)
            _chk_tot.append(row)

        _n_passA = sum(4 * (qg + 1) for qg in range(NQG))
        with ExitStack() as ph2:
            sb = lambda n, s, d: ph2.enter_context(nc.sbuf_tensor(n, s, d))
            qh = [sb(f"qh{i}", [128, SEQ], BF16) for i in range(2)]
            kh = [sb(f"kh{i}", [128, SEQ], BF16) for i in range(2)]
            vh = [sb(f"vh{i}", [128, SEQ], BF16) for i in range(2)]
            tmpmax = [sb(f"tmpmax{i}", [128, 160], F32) for i in range(2)]
            maxred = [sb(f"maxred{i}", [128, 1], F32) for i in range(2)]
            csc = [sb(f"csc{i}", [1, 2], F32) for i in range(2)]
            cb = [sb(f"cb{i}", [128, 1], F32) for i in range(2)]
            pbuf = [sb(f"pbuf{i}", [128, 512], BF16) for i in range(4)]
            rsb = [sb(f"rsb{i}", [1, 512], F32) for i in range(2)]
            rbc = [sb(f"rbc{i}", [128, 512], F32) for i in range(2)]
            umul = [sb(f"umul{i}", [128, 512], F32) for i in range(2)]
            ustg = [sb(f"ustg{i}", [128, 512], BF16) for i in range(2)]
            SA = ph2.enter_context(nc.psum_tensor("SA", [128, 1024], F32))
            SB = ph2.enter_context(nc.psum_tensor("SBp", [128, 1024], F32))
            PV = [ph2.enter_context(nc.psum_tensor(f"PV{i}", [128, 512], F32))
                  for i in range(2)]
            SUM = ph2.enter_context(nc.psum_tensor("SUMp", [1, 512], F32))
            RBC = ph2.enter_context(nc.psum_tensor("RBCp", [128, 512], F32))

            _msk_tot = {}
            _mcnt = 0
            _gc = 0
            for _it in range(NIT):
                for _qg in range(NQG):
                    for _r in range(4 * _qg + 4):
                        if _r >= 4 * _qg:
                            _j = _r - 4 * _qg
                            _mcnt += _j + 1
                            _msk_tot[_gc] = _mcnt
                        _gc += 1

            with nc.Block() as blk:
                @blk.sync
                def _(sy):
                    for it in range(NIT):
                        s, h = it // 4, it % 4
                        if it >= 2:
                            sy.wait_ge(s_pvmm.h, CPI * (it - 1))
                        sl = it % 2
                        s_qkvld[sl].inc(sy.dma_start(
                            out=qh[sl][:, :],
                            in_=q_d[h * 128:(h + 1) * 128,
                                    s * SEQ:(s + 1) * SEQ]), 16)
                        s_qkvld[sl].inc(sy.dma_start(
                            out=kh[sl][:, :],
                            in_=k_d[h * 128:(h + 1) * 128,
                                    s * SEQ:(s + 1) * SEQ]), 16)
                        for kb in range(KBS):
                            s_qkvld[sl].inc(sy.dma_start(
                                out=vh[sl][:, kb * 128:(kb + 1) * 128],
                                in_=v_d[s * SEQ + kb * 128:
                                        s * SEQ + (kb + 1) * 128,
                                        h * 128:(h + 1) * 128]), 16)

                @blk.tensor
                def _(t):
                    tA = 0
                    gc = 0
                    for it in range(NIT):
                        sl = it % 2
                        t.wait_ge(s_qkvld[sl].h, (it // 2 + 1) * (2 + KBS) * 16)
                        for qg in range(NQG):
                            for kb5 in range(qg + 1):
                                for qs in range(4):
                                    if tA >= 2:
                                        t.wait_ge(s_maxA.h, tA - 1)
                                    ins = t.matmul(
                                        SA[:, (tA % 2) * 512:(tA % 2 + 1) * 512],
                                        qh[sl][:, qg * 512 + qs * 128:
                                               qg * 512 + (qs + 1) * 128],
                                        kh[sl][:, kb5 * 512:(kb5 + 1) * 512],
                                        start=True, stop=True)
                                    s_mmA.inc(ins)
                                    tA += 1
                        t.wait_ge(s_id32.h, 32)
                        t.wait_ge(s_cred.h, it + 1)
                        s_ctr.inc(t.transpose(RBC[0:1, 0:128],
                                              maxred[sl][:, :], sb_id32[:, :]))
                        t.wait_ge(s_cneg.h, it + 1)
                        if it >= 1:
                            t.wait_ge(s_rbcs.h, NQG * it)
                        s_cbc.inc(t.matmul(RBC[:, 0:1], sb_ones_f[:, :],
                                           csc[sl][0:1, 1:2],
                                           start=True, stop=True))
                        for qg in range(NQG):
                            qgg = it * NQG + qg
                            nkb = 4 * qg + 4

                            def s_mm(r, gc_r):
                                if gc_r >= 2:
                                    t.wait_ge(s_expB.h, gc_r - 1)
                                ins = t.matmul(
                                    SB[:, (gc_r % 2) * 512:(gc_r % 2 + 1) * 512],
                                    kh[sl][:, r * 128:(r + 1) * 128],
                                    qh[sl][:, qg * 512:(qg + 1) * 512],
                                    start=True, stop=True)
                                s_mmB.inc(ins)

                            def pv_sum(r, gc_r):
                                if r >= 4 * qg:
                                    t.wait_ge(s_mskB.h, _msk_tot[gc_r])
                                else:
                                    t.wait_ge(s_expB.h, gc_r + 1)
                                if r == 0 and qgg >= 2:
                                    t.wait_ge(s_mul.h, qgg - 1)
                                t.matmul(PV[qg % 2][:, :],
                                         vh[sl][:, r * 128:(r + 1) * 128],
                                         pbuf[gc_r % 4][:, :],
                                         start=(r == 0), stop=(r == nkb - 1))
                                if r == 0 and qgg >= 1:
                                    t.wait_ge(s_rcp.h, qgg)
                                ins = t.matmul(SUM[0:1, :], sb_ones_b[:, :],
                                               pbuf[gc_r % 4][:, :],
                                               start=(r == 0), stop=(r == nkb - 1))
                                s_pvmm.inc(ins)

                            for r in range(nkb):
                                s_mm(r, gc + r)
                                if r >= 1:
                                    pv_sum(r - 1, gc + r - 1)
                            pv_sum(nkb - 1, gc + nkb - 1)
                            gc += nkb
                            t.wait_ge(s_rcp.h, qgg + 1)
                            t.wait_ge(s_cbias.h, it + 1)
                            if qgg >= 1:
                                t.wait_ge(s_rbcs.h, qgg)
                            s_rbcp.inc(t.matmul(RBC[:, :], sb_ones_f[:, :],
                                                rsb[qgg % 2][:, :],
                                                start=True, stop=True))

                @blk.vector
                def _(v):
                    tA = 0
                    for it in range(NIT):
                        sl = it % 2
                        for ti in range(_n_passA):
                            v.wait_ge(s_mmA.h, tA + 1)
                            s_maxA.inc(v.reduce_max(
                                tmpmax[sl][:, ti:ti + 1],
                                SA[:, (tA % 2) * 512:(tA % 2 + 1) * 512],
                                axis=AX.X))
                            tA += 1
                        v.drain()
                        s_cred.inc(v.reduce_max(maxred[sl][:, :],
                                                tmpmax[sl][:, 0:_n_passA],
                                                axis=AX.X))
                        v.drain()
                        v.wait_ge(s_ctr.h, it + 1)
                        v.reduce_max(csc[sl][0:1, 0:1], RBC[0:1, 0:128],
                                     axis=AX.X)
                        v.drain()
                        s_cneg.inc(v.tensor_scalar(
                            csc[sl][0:1, 1:2], csc[sl][0:1, 0:1],
                            -INV_SQRT_HD, None, ALU.mult))
                        v.drain()
                        for qg in range(NQG):
                            qgg = it * NQG + qg
                            v.wait_ge(s_pvmm.h, _chk_tot[it][qg])
                            s_rcp.inc(v.reciprocal(rsb[qgg % 2][:, :],
                                                   SUM[0:1, :]))
                            v.drain()
                            v.wait_ge(s_rbcs.h, qgg + 1)
                            if qgg >= 2:
                                v.wait_ge(s_ustg.h, qgg - 1)
                            s_mul.inc(v.tensor_tensor(
                                umul[qgg % 2][:, :], PV[qg % 2][:, :],
                                rbc[qgg % 2][:, :], op=ALU.mult))

                @blk.scalar
                def _(sc):
                    gc = 0
                    for it in range(NIT):
                        sl = it % 2
                        s, h = it // 4, it % 4
                        sc.wait_ge(s_cbc.h, it + 1)
                        n = s_cbias.inc(sc.copy(cb[sl][:, :], RBC[:, 0:1]))
                        sc.wait_ge(s_cbias.h, n)
                        for qg in range(NQG):
                            qgg = it * NQG + qg
                            nkb = 4 * qg + 4
                            for r in range(nkb):
                                sc.wait_ge(s_mmB.h, gc + 1)
                                if gc >= 4:
                                    sc.wait_ge(s_pvmm.h, gc - 3)
                                s_expB.inc(sc.activation(
                                    pbuf[gc % 4][:, :],
                                    SB[:, (gc % 2) * 512:(gc % 2 + 1) * 512],
                                    AF.Exp, bias=cb[sl][:, :],
                                    scale=INV_SQRT_HD))
                                gc += 1
                            sc.wait_ge(s_rbcp.h, qgg + 1)
                            s_rbcs.inc(sc.copy(rbc[qgg % 2][:, :], RBC[:, :]))
                            sc.wait_ge(s_mul.h, qgg + 1)
                            if qgg >= 2:
                                sc.wait_ge(s_uin[qgg % 2].h,
                                           (qgg // 2) * 16)
                            n = s_ustg.inc(sc.activation(
                                ustg[qgg % 2][:, :], umul[qgg % 2][:, :],
                                AF.Identity, bias=sb_bqkv[:, 8 + h:9 + h]))
                            sc.wait_ge(s_ustg.h, n)
                            tok0 = s * SEQ + qg * 512
                            jj = tok0 // OWN
                            off = tok0 % OWN
                            s_uin[qgg % 2].inc(sc.dma_start(
                                out=u_in[jj, h * 128:(h + 1) * 128,
                                         off:off + 512],
                                in_=ustg[qgg % 2][:, :]), 16)
                    sc.wait_ge(s_uin[0].h, s_uin[0].n)
                    sc.wait_ge(s_uin[1].h, s_uin[1].n)

                @blk.gpsimd
                def _(g):
                    gc = 0
                    mcnt = 0
                    for it in range(NIT):
                        for qg in range(NQG):
                            for r in range(4 * qg + 4):
                                if r >= 4 * qg:
                                    j = r - 4 * qg
                                    g.wait_ge(s_expB.h, gc + 1)
                                    for i in range(j):
                                        s_mskB.inc(g.memset(
                                            pbuf[gc % 4][:, i * 128:(i + 1) * 128],
                                            0.0))
                                        mcnt += 1
                                    s_mskB.inc(g.tensor_tensor(
                                        pbuf[gc % 4][:, j * 128:(j + 1) * 128],
                                        pbuf[gc % 4][:, j * 128:(j + 1) * 128],
                                        sb_tri[:, :], op=ALU.mult))
                                    mcnt += 1
                                    assert _msk_tot[gc] == mcnt
                                gc += 1
                    g.wait_ge(s_uin[0].h, s_uin[0].n)
                    g.wait_ge(s_uin[1].h, s_uin[1].n)
                    with allow_ici_all_to_all():
                        s_a2a.inc(g.collective_compute(
                            "AllToAll", ALU.bypass,
                            replica_groups=[[0, 1, 2, 3, 4, 5, 6, 7]],
                            ins=[u_in.ap().opt()],
                            outs=[u_out.ap().opt()]))

        NTH = OWN // 512
        NIC = INTER // 128
        NRC = RANK // 128
        s3_b1l = [sem("s3_b1l0"), sem("s3_b1l1")]
        s3_b1m = sem("s3_b1m")
        s3_b1e = sem("s3_b1e")
        s3_xno = sem("s3_xno")
        s3_xfo = [sem("s3_xfo0"), sem("s3_xfo1")]
        s3_mr = sem("s3_mr")
        s3_mrb = sem("s3_mrb")
        s3_mrbe = sem("s3_mrbe")
        s3_u1 = [sem("s3_u1a"), sem("s3_u1b")]
        s3_v1 = [sem(f"s3_v1{i}") for i in range(4)]
        s3_u2 = [sem("s3_u2a"), sem("s3_u2b")]
        s3_v2 = [sem("s3_v2a"), sem("s3_v2b")]
        s3_mm = sem("s3_mm")
        s3_evd = sem("s3_evd")
        s3_eva = sem("s3_eva")
        s3_glu = sem("s3_glu")
        s3_fst = [sem(f"s3_fst{i}") for i in range(4)]

        _b1loads = []
        _o = 0
        while _o < 2 * INTER:
            _w = min(2048, 2 * INTER - _o)
            _b1loads.append((_o, _w))
            _o += _w

        _cnt = 0
        _p_last = []
        for _th in range(NTH):
            for _f in range(NFC):
                _cnt += 1
        _cnt = NTH * NFC
        _hgrp_flat = []
        _h_mm = []
        _u2grp = []
        _z_mm = []
        _f_mm = []
        for _th in range(NTH):
            for _ig in range(NIC):
                _cnt += 1
                _g = _cnt
                _cnt += 1
                _u = _cnt
                _h_mm.append((_g, _u))
                _hgrp_flat += [_g, _u]
            for _i in range(NIC):
                _cnt += 1
                _u2grp.append(_cnt)
            _z_mm.append(_cnt)
            _row = []
            for _j in range(8):
                for _ts in range(4):
                    _cnt += 1
                    _row.append(_cnt)
            _f_mm.append(_row)
        _v1gate = {g: _hgrp_flat[g - 4] for g in range(4, 2 * NTH * NIC)}
        _u2gate = {g: _u2grp[g - 2] for g in range(2, NTH * NIC)}
        _v2gate = {g: _f_mm[(g - 2) // 8][((g - 2) % 8) * 4 + 3]
                   for g in range(2, NTH * 8)}
        _fevd = {}
        for _m in range(32 * NTH):
            _tm = _m // 32
            _fevd[_m] = 8 * NTH + 40 * _tm + 8 + (_m - 32 * _tm) + 1

        with ExitStack() as ph3:
            sb = lambda n, s, d: ph3.enter_context(nc.sbuf_tensor(n, s, d))
            pp = [ph3.enter_context(nc.psum_tensor(f"pp{i}", [128, 512], F32))
                  for i in range(8)]
            b1row = [sb(f"b1row{i}", [1, 2048], F32) for i in range(2)]
            b1c = sb("b1c", [128, 2 * NIC], F32)
            xnoT = [sb(f"xnoT{i}", [128, 512], BF16) for i in range(2)]
            xfo = [sb(f"xfo{i}", [128, 512], F32) for i in range(2)]
            mr3row = sb("mr3row", [1, 2 * OWN], BF16)
            mbc3 = sb("mbc3", [128, 512], BF16)
            rbc3 = sb("rbc3", [128, 512], BF16)
            u1sb = [sb(f"u1sb{i}", [128, RANK], BF16) for i in range(2)]
            v1sb = [sb(f"v1sb{i}", [128, RANK], BF16) for i in range(4)]
            u2sb = [sb(f"u2sb{i}", [128, RANK], BF16) for i in range(2)]
            v2sb = [sb(f"v2sb{i}", [128, RANK * 4], BF16) for i in range(2)]
            g_sb = [sb(f"g_sb{i}", [128, 512], F32) for i in range(2)]
            up_sb = [sb(f"up_sb{i}", [128, 512], F32) for i in range(2)]
            P_sb = sb("P_sb", [128, NRC * OWN], BF16)
            act_sb = sb("act_sb", [128, NIC * 512], BF16)
            Z_sb = sb("Z_sb", [128, NRC * 512], BF16)
            ffn_stg = [sb(f"ffn_stg{i}", [128, 512], BF16) for i in range(4)]

            with nc.Block() as blk:
                @blk.gpsimd
                def _(g):
                    for th in range(NTH):
                        g.wait_ge(s3_mrbe.h, 2 * (th + 1))
                        for f in range(NFC):
                            gi = th * NFC + f
                            g.wait_ge(s3_xfo[gi % 2].h, (gi // 2 + 1) * 16)
                            if gi >= 2:
                                g.wait_ge(s3_mm.h, gi - 1)
                            g.tensor_tensor(xfo[gi % 2][:, :],
                                            xfo[gi % 2][:, :],
                                            mbc3[:, :], op=ALU.subtract)
                            g.drain()
                            s3_xno.inc(g.tensor_tensor(
                                xnoT[gi % 2][:, :], xfo[gi % 2][:, :],
                                rbc3[:, :], op=ALU.mult))

                @blk.sync
                def _(sy):
                    s3_mr.inc(sy.dma_start(out=mr3row[:, :],
                                           in_=mrown_d[:, :]), 16)
                    nch_done = 0
                    for li, (off, wid) in enumerate(_b1loads):
                        if li >= 2:
                            sy.wait_ge(s3_b1m.h, nch_done)
                        s3_b1l[li % 2].inc(sy.dma_start(
                            out=b1row[li % 2][:, 0:wid],
                            in_=b1[:, off:off + wid]), 16)
                        if li >= 1:
                            nch_done += _b1loads[li - 1][1] // 128
                    sy.wait_ge(s_bgu1.h, 32)
                    for th in range(NTH):
                        for f in range(NFC):
                            gi = th * NFC + f
                            if gi >= 2:
                                sy.wait_ge(s3_xno.h, gi - 1)
                                sy.wait_ge(s3_mm.h, gi - 1)
                            s3_xfo[gi % 2].inc(sy.dma_start(
                                out=xfo[gi % 2][:, :],
                                in_=xf_own[f * 128:(f + 1) * 128,
                                           th * 512:(th + 1) * 512]), 16)
                            s3_u1[gi % 2].inc(sy.dma_start(
                                out=u1sb[gi % 2][:, :],
                                in_=u1_b[f * 128:(f + 1) * 128, :]), 16)
                    sy.wait_ge(s_bgv.h, s_bgv.n)
                    sy.wait_ge(s_bgu2.h, 64)
                    sy.wait_ge(s_bgv2.h, 32)
                    for th in range(NTH):
                        for ig in range(NIC):
                            for gu in range(2):
                                gidx = (th * NIC + ig) * 2 + gu
                                if gidx >= 4:
                                    sy.wait_ge(s3_mm.h, _v1gate[gidx])
                                pidx = ig * 2 + gu
                                s3_v1[gidx % 4].inc(sy.dma_start(
                                    out=v1sb[gidx % 4][:, :],
                                    in_=v1_p[pidx, :, :]), 16)
                        for i in range(NIC):
                            gi = th * NIC + i
                            if gi >= 2:
                                sy.wait_ge(s3_mm.h, _u2gate[gi])
                            s3_u2[gi % 2].inc(sy.dma_start(
                                out=u2sb[gi % 2][:, :],
                                in_=u2_b[i * 128:(i + 1) * 128, :]), 16)
                        for j in range(8):
                            gi = th * 8 + j
                            if gi >= 2:
                                sy.wait_ge(s3_mm.h, _v2gate[gi])
                            for rc in range(NRC):
                                s3_v2[gi % 2].inc(sy.dma_start(
                                    out=v2sb[gi % 2][:, rc * 512:(rc + 1) * 512],
                                    in_=v2_b[rc * 128:(rc + 1) * 128,
                                             j * 512:(j + 1) * 512]), 16)

                @blk.tensor
                def _(t):
                    mmn = 0

                    def bcast(th):
                        if th >= 1:
                            t.wait_ge(s3_mrbe.h, 2 * th)
                            t.wait_ge(s3_xno.h, th * NFC)
                        t.matmul(pp[6][:, :], sb_onesrb[:, :],
                                 mr3row[0:1, th * 512:(th + 1) * 512],
                                 start=True, stop=True)
                        ins = t.matmul(pp[7][:, :], sb_onesrb[:, :],
                                       mr3row[0:1, OWN + th * 512:
                                              OWN + (th + 1) * 512],
                                       start=True, stop=True)
                        s3_mrb.inc(ins)

                    t.wait_ge(s3_mr.h, 16)
                    bcast(0)
                    nch = 0
                    for li, (off, wid) in enumerate(_b1loads):
                        t.wait_ge(s3_b1l[li % 2].h, (li // 2 + 1) * 16)
                        ins = None
                        for cc in range(wid // 128):
                            ins = t.matmul(
                                pp[0][:, nch:nch + 1],
                                b1row[li % 2][0:1, cc * 128:(cc + 1) * 128],
                                sb_ones_f[:, 0:1], start=True, stop=True)
                            nch += 1
                        s3_b1m.inc(ins, wid // 128)
                    t.wait_ge(s3_b1e.h, 1)
                    for th in range(NTH):
                        if th >= 1:
                            t.wait_ge(s3_evd.h, 8 * th)
                            bcast(th)
                        for f in range(NFC):
                            gi = th * NFC + f
                            t.wait_ge(s3_xno.h, gi + 1)
                            t.wait_ge(s3_u1[gi % 2].h, (gi // 2 + 1) * 16)
                            ins = None
                            for rc in range(NRC):
                                ins = t.matmul(
                                    pp[rc][:, :],
                                    u1sb[gi % 2][:, rc * 128:(rc + 1) * 128],
                                    xnoT[gi % 2][:, :],
                                    start=(f == 0), stop=(f == NFC - 1))
                            mmn = s3_mm.inc(ins)
                    _Pev = 8 * NTH
                    for th in range(NTH):
                        for ig in range(NIC):
                            gidx2 = (th * NIC + ig) * 2
                            for gu in range(2):
                                gidx = gidx2 + gu
                                slot = gidx % 4
                                bank = pp[2 * (ig % 2) + gu]
                                t.wait_ge(s3_v1[gidx % 4].h,
                                          (gidx // 4 + 1) * 16)
                                if ig < 2:
                                    t.wait_ge(s3_evd.h, 8 * NTH + 40 * th)
                                if gidx >= 4:
                                    t.wait_ge(s3_eva.h, gidx - 3)
                                ins = None
                                for rc in range(NRC):
                                    ins = t.matmul(
                                        bank[:, :],
                                        v1sb[slot][:, rc * 128:(rc + 1) * 128],
                                        P_sb[:, rc * OWN + th * 512:
                                             rc * OWN + (th + 1) * 512],
                                        start=(rc == 0), stop=(rc == NRC - 1))
                                mmn = s3_mm.inc(ins)
                        for i in range(NIC):
                            gi = th * NIC + i
                            t.wait_ge(s3_u2[gi % 2].h, (gi // 2 + 1) * 16)
                            t.wait_ge(s3_glu.h, th * NIC + i + 1)
                            if i == 0:
                                t.wait_ge(s3_eva.h, 2 * NIC * (th + 1))
                            ins = None
                            for rc in range(NRC):
                                ins = t.matmul(
                                    pp[rc][:, :],
                                    u2sb[gi % 2][:, rc * 128:(rc + 1) * 128],
                                    act_sb[:, i * 512:(i + 1) * 512],
                                    start=(i == 0), stop=(i == NIC - 1))
                            mmn = s3_mm.inc(ins)
                        t.wait_ge(s3_evd.h, 8 * NTH + 40 * th + 8)
                        for j in range(8):
                            gi = th * 8 + j
                            t.wait_ge(s3_v2[gi % 2].h, (gi // 2 + 1) * 8 * 16)
                            for ts in range(4):
                                m = gi * 4 + ts
                                if m >= 4:
                                    t.wait_ge(s3_evd.h, _fevd[m - 4])
                                ins = None
                                for rc in range(NRC):
                                    ins = t.matmul(
                                        pp[ts][:, :],
                                        Z_sb[:, rc * 512 + ts * 128:
                                             rc * 512 + (ts + 1) * 128],
                                        v2sb[gi % 2][:, rc * 512:(rc + 1) * 512],
                                        start=(rc == 0), stop=(rc == NRC - 1))
                                mmn = s3_mm.inc(ins)

                @blk.scalar
                def _(sc):
                    sc.wait_ge(s3_mrb.h, 1)
                    s3_mrbe.inc(sc.copy(mbc3[:, :], pp[6][:, :]))
                    s3_mrbe.inc(sc.copy(rbc3[:, :], pp[7][:, :]))
                    sc.wait_ge(s3_b1m.h, 2 * NIC)
                    s3_b1e.inc(sc.copy(b1c[:, :], pp[0][:, 0:2 * NIC]))
                    for th in range(1, NTH):
                        sc.wait_ge(s3_mrb.h, th + 1)
                        s3_mrbe.inc(sc.copy(mbc3[:, :], pp[6][:, :]))
                        s3_mrbe.inc(sc.copy(rbc3[:, :], pp[7][:, :]))
                    for th in range(NTH):
                        for ig in range(NIC):
                            sc.wait_ge(s3_mm.h, _h_mm[th * NIC + ig][0])
                            if th * NIC + ig >= 2:
                                sc.wait_ge(s3_glu.h, th * NIC + ig - 1)
                            s3_eva.inc(sc.activation(
                                g_sb[ig % 2][:, :], pp[2 * (ig % 2)][:, :],
                                AF.Silu, bias=b1c[:, ig:ig + 1]))
                            sc.wait_ge(s3_mm.h, _h_mm[th * NIC + ig][1])
                            s3_eva.inc(sc.activation(
                                up_sb[ig % 2][:, :], pp[2 * (ig % 2) + 1][:, :],
                                AF.Identity, bias=b1c[:, NIC + ig:NIC + ig + 1]))
                        for j in range(8):
                            for ts in range(4):
                                m = (th * 8 + j) * 4 + ts
                                sc.wait_ge(s3_evd.h, _fevd[m])
                                s3_fst[m % 4].inc(sc.dma_start(
                                    out=ffn_d[th * 512 + ts * 128:
                                              th * 512 + (ts + 1) * 128,
                                              j * 512:(j + 1) * 512],
                                    in_=ffn_stg[m % 4][:, :]), 16)
                    for i in range(4):
                        sc.wait_ge(s3_fst[i].h, s3_fst[i].n)

                @blk.vector
                def _(v):
                    for th in range(NTH):
                        v.wait_ge(s3_mm.h, NFC * (th + 1))
                        for rc in range(NRC):
                            s3_evd.inc(v.tensor_copy(
                                P_sb[:, rc * OWN + th * 512:
                                     rc * OWN + (th + 1) * 512],
                                pp[rc][:, :]))
                    for th in range(NTH):
                        for ig in range(NIC):
                            gidx2 = (th * NIC + ig) * 2
                            v.wait_ge(s3_eva.h, gidx2 + 2)
                            s3_glu.inc(v.tensor_tensor(
                                act_sb[:, ig * 512:(ig + 1) * 512],
                                g_sb[ig % 2][:, :], up_sb[ig % 2][:, :],
                                op=ALU.mult))
                        v.wait_ge(s3_mm.h, _z_mm[th])
                        for rc in range(NRC):
                            s3_evd.inc(v.tensor_copy(
                                Z_sb[:, rc * 512:(rc + 1) * 512], pp[rc][:, :]))
                        for j in range(8):
                            for ts in range(4):
                                m = (th * 8 + j) * 4 + ts
                                v.wait_ge(s3_mm.h, _f_mm[th][j * 4 + ts])
                                if m >= 4:
                                    v.wait_ge(s3_fst[m % 4].h,
                                              (m // 4) * 16)
                                s3_evd.inc(v.tensor_copy(
                                    ffn_stg[m % 4][:, :], pp[ts][:, :]))

        s4_u = sem("s4_u")
        s4_wo = [sem(f"s4_wo{i}") for i in range(3)]
        s4_bb = sem("s4_bb")
        s4_bbv = sem("s4_bbv")
        s4_bbm = sem("s4_bbm")
        s4_bbe = sem("s4_bbe")
        s4_mm = sem("s4_mm")
        s4_add = sem("s4_add")
        s4_ffn = [sem("s4_ffn0"), sem("s4_ffn1")]
        s4_x = [sem("s4_x0"), sem("s4_x1")]
        s4_out = [sem("s4_out0"), sem("s4_out1")]
        NTS = OWN // 128

        with ExitStack() as ph4:
            sb = lambda n, s, d: ph4.enter_context(nc.sbuf_tensor(n, s, d))
            pq = [ph4.enter_context(nc.psum_tensor(f"pq{i}", [128, 512], F32))
                  for i in range(5)]
            u_sb = sb("u_sb", [128, 32 * OWN], BF16)
            wo_sb = [sb(f"wo_sb{i}", [128, 32 * 512], BF16) for i in range(3)]
            bbbc = sb("bbbc", [128, D], F32)
            b2_r = sb("b2_r", [1, D], F32)
            t1b = [sb(f"t1b{i}", [128, 512], F32) for i in range(2)]
            toutb = [sb(f"toutb{i}", [128, 512], F32) for i in range(2)]
            ffn_rd = [sb(f"ffn_rd{i}", [128, 512], BF16) for i in range(2)]
            x_rd = [sb(f"x_rd{i}", [128, 512], F32) for i in range(2)]

            with nc.Block() as blk:
                @blk.sync
                def _(sy):
                    s4_bb.inc(sy.dma_start(out=bbbc[0:1, :], in_=bo[:, :]), 16)
                    s4_bb.inc(sy.dma_start(out=b2_r[:, :], in_=b2[:, :]), 16)
                    sy.wait_ge(s_a2a.h, 1)
                    for dc in range(32):
                        src = u_out[dc // 4, (dc % 4) * 128:(dc % 4 + 1) * 128, :]
                        s4_u.inc(sy.dma_start(
                            out=u_sb[:, dc * OWN:(dc + 1) * OWN], in_=src), 16)
                    def wo_load(j):
                        if j >= 3:
                            sy.wait_ge(s4_mm.h, NTS * (j - 2))
                        sy.wait_ge(s_bgc.h, 128)
                        for dc in range(32):
                            s4_wo[j % 3].inc(sy.dma_start(
                                out=wo_sb[j % 3][:, dc * 512:(dc + 1) * 512],
                                in_=wo_b[dc * 128:(dc + 1) * 128,
                                         j * 512:(j + 1) * 512]), 16)

                    wo_load(0)
                    wo_load(1)
                    wo_load(2)
                    for j in range(8):
                        for ts in range(NTS):
                            m = j * NTS + ts
                            if m >= 2:
                                sy.wait_ge(s4_add.h, m - 1)
                            s4_x[m % 2].inc(sy.dma_start(
                                out=x_rd[m % 2][:, :],
                                in_=x_own[ts * 128:(ts + 1) * 128,
                                          j * 512:(j + 1) * 512]), 16)
                            s4_ffn[m % 2].inc(sy.dma_start(
                                out=ffn_rd[m % 2][:, :],
                                in_=ffn_d[ts * 128:(ts + 1) * 128,
                                          j * 512:(j + 1) * 512]), 16)
                        if j + 3 < 8:
                            wo_load(j + 3)

                @blk.vector
                def _(v):
                    for j in range(8):
                        for ts in range(NTS):
                            m = j * NTS + ts
                            v.wait_ge(s4_mm.h, m + 1)
                            v.wait_ge(s4_bbe.h, 8)
                            v.tensor_tensor(t1b[m % 2][:, :], pq[ts % 4][:, :],
                                            bbbc[:, j * 512:(j + 1) * 512],
                                            op=ALU.add)
                            v.drain()
                            v.wait_ge(s4_ffn[m % 2].h, (m // 2 + 1) * 16)
                            v.tensor_tensor(t1b[m % 2][:, :], t1b[m % 2][:, :],
                                            ffn_rd[m % 2][:, :], op=ALU.add)
                            v.drain()
                            v.wait_ge(s4_x[m % 2].h, (m // 2 + 1) * 16)
                            if m >= 2:
                                v.wait_ge(s4_out[m % 2].h, (m // 2) * 16)
                            s4_add.inc(v.tensor_tensor(
                                toutb[m % 2][:, :], t1b[m % 2][:, :],
                                x_rd[m % 2][:, :], op=ALU.add))
                            v.drain()

                @blk.tensor
                def _(t):
                    t.wait_ge(s4_bb.h, 32)
                    for j in range(8):
                        t.matmul(pq[4][:, :], sb_ones_f[:, :],
                                 bbbc[0:1, j * 512:(j + 1) * 512],
                                 start=True, stop=False)
                        ins = t.matmul(pq[4][:, :], sb_ones_f[:, :],
                                       b2_r[0:1, j * 512:(j + 1) * 512],
                                       start=False, stop=True)
                        s4_bbm.inc(ins)
                        t.wait_ge(s4_bbe.h, j + 1)
                    t.wait_ge(s4_u.h, 32 * 16)
                    for j in range(8):
                        t.wait_ge(s4_wo[j % 3].h, (j // 3 + 1) * 32 * 16)
                        for ts in range(NTS):
                            m = j * NTS + ts
                            if m >= 4:
                                t.wait_ge(s4_add.h, m - 3)
                            ins = None
                            for dc in range(32):
                                ins = t.matmul(
                                    pq[ts % 4][:, :],
                                    u_sb[:, dc * OWN + ts * 128:
                                         dc * OWN + (ts + 1) * 128],
                                    wo_sb[j % 3][:, dc * 512:(dc + 1) * 512],
                                    start=(dc == 0), stop=(dc == 31))
                            s4_mm.inc(ins)

                @blk.scalar
                def _(sc):
                    for j in range(8):
                        sc.wait_ge(s4_bbm.h, j + 1)
                        s4_bbe.inc(sc.copy(bbbc[:, j * 512:(j + 1) * 512],
                                           pq[4][:, :]))
                    for j in range(8):
                        for ts in range(NTS):
                            m = j * NTS + ts
                            sc.wait_ge(s4_add.h, m + 1)
                            s4_out[m % 2].inc(sc.dma_start(
                                out=out_own[ts * 128:(ts + 1) * 128,
                                            j * 512:(j + 1) * 512],
                                in_=toutb[m % 2][:, :]), 16)
                    sc.wait_ge(s4_out[0].h, s4_out[0].n)
                    sc.wait_ge(s4_out[1].h, s4_out[1].n)

    return nc


def _shard_inputs(inputs):
    f = lambda a: np.ascontiguousarray(np.asarray(a, dtype=np.float32))
    x = f(inputs["L0_attn_x"]).reshape(T, D)
    Wq, Wk, Wv = f(inputs["L0_attn_Wq"]), f(inputs["L0_attn_Wk"]), f(inputs["L0_attn_Wv"])
    bq, bk, bv = f(inputs["L0_attn_bq"]), f(inputs["L0_attn_bk"]), f(inputs["L0_attn_bv"])
    Wo, bo = f(inputs["L0_attn_Wo"]), f(inputs["L0_attn_bo"])
    U1, V1 = f(inputs["L0_ffn_U1"]), f(inputs["L0_ffn_V1"])
    U2, V2 = f(inputs["L0_ffn_U2"]), f(inputs["L0_ffn_V2"])
    b1, b2 = f(inputs["L0_ffn_b1"]), f(inputs["L0_ffn_b2"])
    x_f = np.ascontiguousarray(x.T)
    Wo_t = np.ascontiguousarray(Wo.T)
    in_maps = []
    for c in range(8):
        sl = slice(c * HSL, (c + 1) * HSL)
        in_maps.append({
            "x": x,
            "x_own": np.ascontiguousarray(x[c * OWN:(c + 1) * OWN]),
            "x_f": x_f,
            "xf_own": np.ascontiguousarray(x_f[:, c * OWN:(c + 1) * OWN]),
            "wq_t": np.ascontiguousarray(Wq[sl].T),
            "wk_t": np.ascontiguousarray(Wk[sl].T),
            "wv_t": np.ascontiguousarray(Wv[sl].T),
            "bq": np.ascontiguousarray(bq[sl]).reshape(1, HSL),
            "bk": np.ascontiguousarray(bk[sl]).reshape(1, HSL),
            "bv": np.ascontiguousarray(bv[sl]).reshape(1, HSL),
            "wo_t": Wo_t,
            "bo": bo.reshape(1, D),
            "u1": U1, "v1": V1, "u2": U2, "v2": V2,
            "b1": b1.reshape(1, 2 * INTER), "b2": b2.reshape(1, D),
        })
    return in_maps


def kernel(**inputs):
    nc = build()
    in_maps = _shard_inputs(inputs)
    res = run_bass_kernel_spmd(nc, in_maps, core_ids=list(range(8)))
    out = np.concatenate(
        [np.asarray(res.results[c]["out_own"], dtype=np.float32)
         for c in range(8)], axis=0)
    return np.ascontiguousarray(out.reshape(2, SEQ, D))


# version 08
# speedup vs baseline: 1.0487x; 1.0487x over previous
import numpy as np
import ml_dtypes
from contextlib import ExitStack

import concourse.bass as bass
import concourse.mybir as mybir
from concourse.bass_utils import run_bass_kernel_spmd
from concourse.aot_env import allow_ici_all_to_all

F32 = mybir.dt.float32
BF16 = mybir.dt.bfloat16
AF = mybir.ActivationFunctionType
ALU = mybir.AluOpType
AX = mybir.AxisListType

D = 4096
T = 8192
SEQ = 4096
HSL = 512
NHL = 4
HD = 128
OWN = 1024
RANK = 1024
INTER = 11008
NFC = D // 128
CHK = 256
NCH = T // CHK
TPC = CHK // 128
OTC = 10
NLT = T // 128
NTILE = NLT + 8
INV_SQRT_HD = float(1.0 / np.sqrt(128.0))
EPS = 1e-5


class SemC:

    def __init__(self, handle):
        self.h = handle
        self.n = 0

    def inc(self, instr, k=1):
        instr.then_inc(self.h, k)
        self.n += k
        return self.n


def build(dump=()):
    nc = bass.Bass("TRN2", target_bir_lowering=False)

    x = nc.dram_tensor("x", [T, D], F32, kind="ExternalInput")
    x_own = nc.dram_tensor("x_own", [OWN, D], F32, kind="ExternalInput")
    x_f = nc.dram_tensor("x_f", [D, T], F32, kind="ExternalInput")
    xf_own = nc.dram_tensor("xf_own", [D, OWN], F32, kind="ExternalInput")
    wq_t = nc.dram_tensor("wq_t", [D, HSL], F32, kind="ExternalInput")
    wk_t = nc.dram_tensor("wk_t", [D, HSL], F32, kind="ExternalInput")
    wv_t = nc.dram_tensor("wv_t", [D, HSL], F32, kind="ExternalInput")
    bq = nc.dram_tensor("bq", [1, HSL], F32, kind="ExternalInput")
    bk = nc.dram_tensor("bk", [1, HSL], F32, kind="ExternalInput")
    bv = nc.dram_tensor("bv", [1, HSL], F32, kind="ExternalInput")
    wo_t = nc.dram_tensor("wo_t", [D, D], F32, kind="ExternalInput")
    bo = nc.dram_tensor("bo", [1, D], F32, kind="ExternalInput")
    u1 = nc.dram_tensor("u1", [D, RANK], F32, kind="ExternalInput")
    v1 = nc.dram_tensor("v1", [RANK, 2 * INTER], F32, kind="ExternalInput")
    u2 = nc.dram_tensor("u2", [INTER, RANK], F32, kind="ExternalInput")
    v2 = nc.dram_tensor("v2", [RANK, D], F32, kind="ExternalInput")
    b1 = nc.dram_tensor("b1", [1, 2 * INTER], F32, kind="ExternalInput")
    b2 = nc.dram_tensor("b2", [1, D], F32, kind="ExternalInput")
    out_own = nc.dram_tensor("out_own", [OWN, D], F32, kind="ExternalOutput")

    def scratch(name, shape, dtype):
        kind = "ExternalOutput" if name in dump else "Internal"
        return nc.dram_tensor(name, shape, dtype, kind=kind)

    q_d = scratch("q_d", [HSL, T], BF16)
    k_d = scratch("k_d", [HSL, T], BF16)
    v_d = scratch("v_d", [T, HSL], BF16)
    mrown_d = scratch("mrown_d", [1, 2 * OWN], BF16)
    v1_p = scratch("v1_p", [2 * (INTER // 128), 128, RANK], BF16)
    u_in = scratch("u_in", [8, HSL, OWN], BF16)
    u_out = scratch("u_out", [8, HSL, OWN], BF16)
    ffn_d = scratch("ffn_d", [OWN, D], BF16)

    ones_bcol = nc.inline_tensor(np.ones((128, 1), dtype=ml_dtypes.bfloat16),
                                 "ones_bcol")
    ones_frow = nc.inline_tensor(np.ones((1, 128), dtype=np.float32),
                                 "ones_frow")
    ones_frb = nc.inline_tensor(np.ones((1, 128), dtype=ml_dtypes.bfloat16),
                                "ones_frb")
    tri = np.triu(np.ones((128, 128), np.float32)).astype(ml_dtypes.bfloat16)
    tri_t = nc.inline_tensor(tri, "tri_t")
    ident_f32 = nc.inline_tensor(np.eye(128, dtype=np.float32), "ident_f32")

    with ExitStack() as top:
        sem = lambda name: SemC(top.enter_context(nc.semaphore(name)))

        sb_ones_b = nc.alloc_sbuf_tensor("sb_ones_b", [128, 1], BF16)
        sb_ones_f = nc.alloc_sbuf_tensor("sb_ones_f", [1, 128], F32)
        sb_tri = nc.alloc_sbuf_tensor("sb_tri", [128, 128], BF16)
        sb_bqkv = nc.alloc_sbuf_tensor("sb_bqkv", [128, 12], F32)
        sb_id32 = nc.alloc_sbuf_tensor("sb_id32", [128, 128], F32)
        sb_onesrb = nc.alloc_sbuf_tensor("sb_onesrb", [1, 128], BF16)

        s_const = sem("s_const")
        s_id32 = sem("s_id32")

        s_wt = [sem("s_wtq"), sem("s_wtk"), sem("s_wtv")]
        s_bgv = sem("s_bgv")
        s_bv = sem("s_bv")
        s_bprep = sem("s_bprep")
        s_bcols = sem("s_bcols")
        s_ldx = [sem("s_ldx0"), sem("s_ldx1")]
        s_ln1 = sem("s_ln1")
        s_ln2 = sem("s_ln2")
        s_mrt = sem("s_mrt")
        s_mre = sem("s_mre")
        s_bcmm = sem("s_bcmm")
        s_bce = sem("s_bce")
        s_xf = [sem("s_xf0"), sem("s_xf1")]
        s_mrst = sem("s_mrst")
        s_xn = sem("s_xn")
        s_mm1 = sem("s_mm1")
        s_evq = sem("s_evq")
        s_evv = sem("s_evv")
        s_qstk = sem("s_qstk")
        s_qstv = sem("s_qstv")

        NCHA = NCH + (OWN // CHK)

        with ExitStack() as ph1:
            sb = lambda n, s, d: ph1.enter_context(nc.sbuf_tensor(n, s, d))
            pt = lambda n, s: ph1.enter_context(nc.psum_tensor(n, s, F32))
            w_qT = sb("w_qT", [128, NFC * 512], BF16)
            w_kT = sb("w_kT", [128, NFC * 512], BF16)
            w_vT = sb("w_vT", [128, NFC * 512], BF16)
            xfb = [sb(f"xfb{i}", [128, NFC * 128], F32) for i in range(2)]
            xnb = [sb(f"xnb{i}", [128, NFC * CHK], BF16) for i in range(2)]
            xb = [sb(f"xb{i}", [128, D], F32) for i in range(2)]
            stg = [sb(f"stg{i}", [128, CHK], BF16) for i in range(8)]
            stgv = [sb(f"stgv{i}", [128, 512], BF16) for i in range(2)]
            stats = [sb(f"stats{i}", [128, 48], F32) for i in range(2)]
            mv = [sb(f"mv{i}", [128, 2], F32) for i in range(4)]
            veps = [sb(f"veps{i}", [128, 1], F32) for i in range(2)]
            rstd = [sb(f"rstd{i}", [128, 1], F32) for i in range(2)]
            mr_sb = [sb(f"mr_sb{i}", [1, 512], BF16) for i in range(2)]
            mbc = [sb(f"mbc1_{i}", [128, 256], BF16) for i in range(2)]
            rbc1 = [sb(f"rbc1_{i}", [128, 256], BF16) for i in range(2)]
            sb_bv3 = sb("sb_bv3", [1, HSL], F32)
            ps = [pt(f"p1_{i}", [128, 512]) for i in range(4)]
            ps_b = pt("p1_b", [128, 512])
            ps_mr = ph1.enter_context(nc.psum_tensor("ps_mr", [1, 512], F32))
            ps_bc = pt("ps_bc", [128, 512])

            def tile_src(i):
                if i < NLT:
                    return x[i * 128:(i + 1) * 128, :]
                j = i - NLT
                return x_own[j * 128:(j + 1) * 128, :]

            evac_engine = []
            for c in range(NCH):
                evac_engine += ['A'] * 8 + ['V'] * 2
            actc = np.cumsum([1 if t == 'A' else 0 for t in evac_engine])
            dvec = np.cumsum([1 if t == 'V' else 0 for t in evac_engine])

            with nc.Block() as blk:
                @blk.gpsimd
                def _(g):
                    s_const.inc(g.dma_start(out=sb_ones_b[:, :],
                                            in_=ones_bcol.ap()), 16)
                    s_const.inc(g.dma_start(out=sb_ones_f[:, :],
                                            in_=ones_frow.ap()), 16)
                    s_const.inc(g.dma_start(out=sb_tri[:, :],
                                            in_=tri_t.ap()), 16)
                    s_id32.inc(g.dma_start(out=sb_id32[:, :],
                                           in_=ident_f32.ap()), 16)
                    s_id32.inc(g.dma_start(out=sb_onesrb[:, :],
                                           in_=ones_frb.ap()), 16)
                    for wi, (wsb, wsrc) in enumerate(
                            ((w_qT, wq_t), (w_kT, wk_t), (w_vT, wv_t))):
                        for f in range(NFC):
                            s_wt[wi].inc(g.dma_start(
                                out=wsb[:, f * 512:(f + 1) * 512],
                                in_=wsrc[f * 128:(f + 1) * 128, :]), 16)
                    bg = []
                    for gu in range(2):
                        for ig in range(INTER // 128):
                            gidx = ig * 2 + gu
                            i0 = gu * INTER + ig * 128
                            for half in range(2):
                                srcv = v1[half * 512:(half + 1) * 512,
                                          i0:i0 + 128]
                                srcv = srcv.rearrange("(rc p) i -> p rc i",
                                                      p=128)
                                dstv = v1_p[gidx, :,
                                            half * 512:(half + 1) * 512]
                                dstv = dstv.rearrange("p (rc i) -> p rc i",
                                                      i=128)
                                bg.append((s_bgv, dstv, srcv))
                    bgi = [0]

                    def bg_pump(k):
                        for _ in range(k):
                            if bgi[0] < len(bg):
                                s_, dst, srcb = bg[bgi[0]]
                                s_.inc(g.dma_start(out=dst, in_=srcb), 16)
                                bgi[0] += 1

                    for c in range(NCH):
                        g.wait_ge(s_bce.h, 2 * (c + 1))
                        for h in range(2):
                            gix = 2 * c + h
                            g.wait_ge(s_xf[gix % 2].h, (gix // 2 + 1) * 16 * 16)
                            xf3 = xfb[gix % 2][:, :].rearrange(
                                "p (f t) -> p f t", t=128)
                            mslice = mbc[c % 2][:, h * 128:(h + 1) * 128]
                            m3 = bass.AP(mslice.tensor, mslice.offset,
                                         [list(mslice.ap[0]), [0, NFC],
                                          [1, 128]])
                            rslice = rbc1[c % 2][:, h * 128:(h + 1) * 128]
                            r3 = bass.AP(rslice.tensor, rslice.offset,
                                         [list(rslice.ap[0]), [0, NFC],
                                          [1, 128]])
                            xn3 = bass.AP(
                                xnb[c % 2][:, :].tensor,
                                xnb[c % 2][:, :].offset + h * 128,
                                [list(xnb[c % 2][:, :].ap[0]),
                                 [CHK, NFC], [1, 128]])
                            g.tensor_tensor(xf3, xf3, m3, op=ALU.subtract)
                            g.drain()
                            s_xn.inc(g.tensor_tensor(xn3, xf3, r3,
                                                     op=ALU.mult))
                        bg_pump(12)
                    bg_pump(len(bg))
                    g.wait_ge(s_bgv.h, s_bgv.n)

                @blk.sync
                def _(sy):
                    s_bv.inc(sy.dma_start(out=sb_bv3[:, :], in_=bq[:, :]), 16)
                    sy.wait_ge(s_bprep.h, 1)
                    s_bv.inc(sy.dma_start(out=sb_bv3[:, :], in_=bk[:, :]), 16)
                    sy.wait_ge(s_bprep.h, 2)
                    s_bv.inc(sy.dma_start(out=sb_bv3[:, :], in_=bv[:, :]), 16)
                    for i in range(2):
                        s_ldx[i % 2].inc(sy.dma_start(out=xb[i % 2][:, :],
                                                      in_=tile_src(i)), 16)
                    for c in range(NCHA):
                        if c < NCH:
                            for h in range(2):
                                gix = 2 * c + h
                                if gix >= 2:
                                    sy.wait_ge(s_xn.h, gix - 1)
                                t0 = c * CHK + h * 128
                                for fq in range(NFC // 2):
                                    srcf = x_f[fq * 256:(fq + 1) * 256,
                                               t0:t0 + 128]
                                    srcf = srcf.rearrange(
                                        "(fc p) t -> p fc t", p=128)
                                    dstf = xfb[gix % 2][:, fq * 256:
                                                        (fq + 1) * 256]
                                    dstf = dstf.rearrange(
                                        "p (fc t) -> p fc t", t=128)
                                    s_xf[gix % 2].inc(
                                        sy.dma_start(out=dstf, in_=srcf), 16)
                        for i in range(TPC * c + 2, TPC * c + 2 + TPC):
                            if i < NTILE:
                                sy.wait_ge(s_ln1.h, max(i - 1, 0))
                                s_ldx[i % 2].inc(sy.dma_start(
                                    out=xb[i % 2][:, :], in_=tile_src(i)), 16)

                @blk.vector
                def _(v):
                    def v_evacs(c):
                        if c >= 1:
                            v.wait_ge(s_qstv.h, 2 * c * 16)
                        for j in range(2):
                            gt = c * OTC + 8 + j
                            v.wait_ge(s_mm1.h, gt + 1)
                            s_evv.inc(v.tensor_copy(stgv[j][:, :],
                                                    ps[gt % 4][:, :]))

                    for p in range(NCHA):
                        i0, i1 = 2 * p, 2 * p + 1
                        for i in (i0, i1):
                            v.wait_ge(s_ldx[i % 2].h, (i // 2 + 1) * 16)
                        if p >= 1:
                            v.wait_ge(s_ln2.h, i0)
                        if p >= 2:
                            v.wait_ge(s_mrt.h, p - 1)
                        for i in (i0, i1):
                            sl = i % 2
                            for j in range(8):
                                v.bn_stats(stats[sl][:, 6 * j:6 * j + 6],
                                           xb[sl][:, 512 * j:512 * (j + 1)])
                        v.drain()
                        for i in (i0, i1):
                            v.bn_aggr(mv[i % 4][:, :], stats[i % 2][:, :])
                        v.drain()
                        for i in (i0, i1):
                            v.tensor_scalar(veps[i % 2][:, :],
                                            mv[i % 4][:, 1:2],
                                            EPS, None, ALU.add)
                        v.drain()
                        s_ln1.inc(v.reciprocal(veps[i0 % 2][:, :],
                                               veps[i0 % 2][:, :]))
                        s_ln1.inc(v.reciprocal(veps[i1 % 2][:, :],
                                               veps[i1 % 2][:, :]))
                        if p >= 2 and p - 2 < NCH:
                            v_evacs(p - 2)
                    for c in range(NCHA - 2, NCH):
                        v_evacs(c)

                @blk.tensor
                def _(t):
                    t.wait_ge(s_const.h, 48)
                    for wi in range(3):
                        t.wait_ge(s_bv.h, (wi + 1) * 16)
                        ins = None
                        for dc in range(4):
                            ins = t.matmul(
                                ps_b[:, wi * 4 + dc:wi * 4 + dc + 1],
                                sb_bv3[:, dc * 128:(dc + 1) * 128],
                                sb_ones_f[:, 0:1],
                                start=True, stop=True)
                        s_bprep.inc(ins)
                    t.wait_ge(s_id32.h, 32)

                    def mr_transp(c):
                        t.wait_ge(s_ln1.h, 2 * c + 2)
                        t.wait_ge(s_ln2.h, 2 * c + 2)
                        if c >= 1:
                            t.wait_ge(s_mre.h, c)
                        t.transpose(ps_mr[0:1, 0:128], mv[(2 * c) % 4][:, 0:1],
                                    sb_id32[:, :])
                        t.transpose(ps_mr[0:1, 128:256],
                                    mv[(2 * c + 1) % 4][:, 0:1], sb_id32[:, :])
                        t.transpose(ps_mr[0:1, 256:384], rstd[0][:, :],
                                    sb_id32[:, :])
                        ins = t.transpose(ps_mr[0:1, 384:512], rstd[1][:, :],
                                          sb_id32[:, :])
                        s_mrt.inc(ins)

                    def mr_bcast(c):
                        t.wait_ge(s_mre.h, c + 1)
                        if c >= 1:
                            t.wait_ge(s_bce.h, 2 * c)
                        t.matmul(ps_bc[:, 0:256], sb_onesrb[:, :],
                                 mr_sb[c % 2][0:1, 0:256],
                                 start=True, stop=True)
                        ins = t.matmul(ps_bc[:, 256:512], sb_onesrb[:, :],
                                       mr_sb[c % 2][0:1, 256:512],
                                       start=True, stop=True)
                        s_bcmm.inc(ins)

                    def qkv(c):
                        t.wait_ge(s_xn.h, 2 * (c + 1))
                        for j in range(OTC):
                            gt = c * OTC + j
                            b = gt % 4
                            if c == 0 and j in (0, 4, 8):
                                t.wait_ge(s_wt[j // 4].h, 32 * 16)
                            if gt >= 4:
                                pt_ = gt - 4
                                if evac_engine[pt_] == 'A':
                                    t.wait_ge(s_evq.h, int(actc[pt_]))
                                else:
                                    t.wait_ge(s_evv.h, int(dvec[pt_]))
                            if j < 8:
                                wsb = w_qT if j < 4 else w_kT
                                dc = j % 4
                                for f in range(NFC):
                                    ins = t.matmul(
                                        ps[b][:, 0:CHK],
                                        wsb[:, f * 512 + dc * 128:
                                            f * 512 + (dc + 1) * 128],
                                        xnb[c % 2][:, f * CHK:(f + 1) * CHK],
                                        start=(f == 0), stop=(f == NFC - 1))
                            else:
                                ts = j - 8
                                for f in range(NFC):
                                    ins = t.matmul(
                                        ps[b][:, :],
                                        xnb[c % 2][:, f * CHK + ts * 128:
                                                   f * CHK + (ts + 1) * 128],
                                        w_vT[:, f * 512:(f + 1) * 512],
                                        start=(f == 0), stop=(f == NFC - 1))
                            s_mm1.inc(ins)

                    for c in range(NCHA):
                        mr_transp(c)
                        if c >= 1 and c - 1 < NCH:
                            mr_bcast(c - 1)
                        if c >= 2 and c - 2 < NCH:
                            qkv(c - 2)
                    for c in range(NCHA - 2, NCH):
                        qkv(c)

                @blk.scalar
                def _(sc):
                    def qk_out(c):
                        sc.wait_ge(s_bcols.h, 1)
                        if c >= 1:
                            sc.wait_ge(s_qstk.h, 8 * c * 16)
                        for j in range(8):
                            gt = c * OTC + j
                            sc.wait_ge(s_mm1.h, gt + 1)
                            bias = sb_bqkv[:, j:j + 1]
                            n = s_evq.inc(sc.activation(
                                stg[j][:, 0:CHK], ps[gt % 4][:, 0:CHK],
                                AF.Identity, bias=bias))
                            sc.wait_ge(s_evq.h, n)
                            dst = (q_d if j < 4 else k_d)
                            dc = j % 4
                            s_qstk.inc(sc.dma_start(
                                out=dst[dc * 128:(dc + 1) * 128,
                                        c * CHK:(c + 1) * CHK],
                                in_=stg[j][:, 0:CHK]), 16)

                    def v_out(c):
                        for j in range(2):
                            gt = c * OTC + 8 + j
                            sc.wait_ge(s_evv.h, int(dvec[gt]))
                            s_qstv.inc(sc.dma_start(
                                out=v_d[c * CHK + j * 128:
                                        c * CHK + (j + 1) * 128, :],
                                in_=stgv[j][:, :]), 16)

                    sc.wait_ge(s_bprep.h, 3)
                    s_bcols.inc(sc.copy(sb_bqkv[:, :], ps_b[:, 0:12]))
                    for c in range(NCHA):
                        for i in (2 * c, 2 * c + 1):
                            sl = i % 2
                            sc.wait_ge(s_ln1.h, i + 1)
                            if i >= 4:
                                sc.wait_ge(s_mrt.h, i // 2)
                            s_ln2.inc(sc.activation(rstd[sl][:, :],
                                                    veps[sl][:, :], AF.Sqrt))
                        sc.wait_ge(s_mrt.h, c + 1)
                        if c < NCH:
                            n = s_mre.inc(sc.copy(mr_sb[c % 2][:, :],
                                                  ps_mr[0:1, :]))
                            if c >= 1:
                                sc.wait_ge(s_bcmm.h, c)
                                if c >= 3:
                                    sc.wait_ge(s_xn.h, 2 * (c - 2))
                                s_bce.inc(sc.copy(mbc[(c - 1) % 2][:, :],
                                                  ps_bc[:, 0:256]))
                                s_bce.inc(sc.copy(rbc1[(c - 1) % 2][:, :],
                                                  ps_bc[:, 256:512]))
                        else:
                            co = c - NCH
                            sc.wait_ge(s_bcmm.h, NCH)
                            if co == 0:
                                sc.wait_ge(s_xn.h, 2 * (c - 2))
                                s_bce.inc(sc.copy(mbc[(NCH - 1) % 2][:, :],
                                                  ps_bc[:, 0:256]))
                                s_bce.inc(sc.copy(rbc1[(NCH - 1) % 2][:, :],
                                                  ps_bc[:, 256:512]))
                            if co >= 2:
                                sc.wait_ge(s_mrst.h, co * 32)
                            n = s_mre.inc(sc.copy(mr_sb[c % 2][:, :],
                                                  ps_mr[0:1, :]))
                            sc.wait_ge(s_mre.h, n)
                            s_mrst.inc(sc.dma_start(
                                out=mrown_d[0:1, co * 256:(co + 1) * 256],
                                in_=mr_sb[c % 2][0:1, 0:256]), 16)
                            s_mrst.inc(sc.dma_start(
                                out=mrown_d[0:1, OWN + co * 256:
                                            OWN + (co + 1) * 256],
                                in_=mr_sb[c % 2][0:1, 256:512]), 16)
                        if c >= 3 and c - 3 < NCH:
                            v_out(c - 3)
                        if c >= 2 and c - 2 < NCH:
                            qk_out(c - 2)
                    for c in range(NCHA - 3, NCH):
                        v_out(c)
                    for c in range(NCHA - 2, NCH):
                        qk_out(c)
                    sc.wait_ge(s_qstk.h, s_qstk.n)
                    sc.wait_ge(s_qstv.h, s_qstv.n)
                    sc.wait_ge(s_mrst.h, s_mrst.n)

        s_qkvld = [sem("s_qkvld0"), sem("s_qkvld1")]
        s_mmA = sem("s_mmA")
        s_maxA = sem("s_maxA")
        s_cred = sem("s_cred")
        s_ctr = sem("s_ctr")
        s_cneg = sem("s_cneg")
        s_cbc = sem("s_cbc")
        s_cbias = sem("s_cbias")
        s_mmB = sem("s_mmB")
        s_expB = sem("s_expB")
        s_mskB = sem("s_mskB")
        s_pvmm = sem("s_pvmm")
        s_rcp = sem("s_rcp")
        s_rbcp = sem("s_rbcp")
        s_rbcs = sem("s_rbcs")
        s_mul = sem("s_mul")
        s_ustg = sem("s_ustg")
        s_uin = [sem("s_uin0"), sem("s_uin1")]
        s_a2a = sem("s_a2a")


        NIT = 8
        NQG = SEQ // 512
        KBS = SEQ // 128
        CPI = sum(4 * qg + 4 for qg in range(NQG))
        _chk_tot = []
        _acc = 0
        for _it in range(NIT):
            row = []
            for _qg in range(NQG):
                _acc += 4 * _qg + 4
                row.append(_acc)
            _chk_tot.append(row)

        _n_passA = sum(4 * (qg + 1) for qg in range(NQG))
        with ExitStack() as ph2:
            sb = lambda n, s, d: ph2.enter_context(nc.sbuf_tensor(n, s, d))
            qh = [sb(f"qh{i}", [128, SEQ], BF16) for i in range(2)]
            kh = [sb(f"kh{i}", [128, SEQ], BF16) for i in range(2)]
            vh = [sb(f"vh{i}", [128, SEQ], BF16) for i in range(2)]
            tmpmax = [sb(f"tmpmax{i}", [128, 160], F32) for i in range(2)]
            maxred = [sb(f"maxred{i}", [128, 1], F32) for i in range(2)]
            csc = [sb(f"csc{i}", [1, 2], F32) for i in range(2)]
            cb = [sb(f"cb{i}", [128, 1], F32) for i in range(2)]
            pbuf = [sb(f"pbuf{i}", [128, 512], BF16) for i in range(4)]
            rsb = [sb(f"rsb{i}", [1, 512], F32) for i in range(2)]
            rbc = [sb(f"rbc{i}", [128, 512], F32) for i in range(2)]
            umul = [sb(f"umul{i}", [128, 512], F32) for i in range(2)]
            ustg = [sb(f"ustg{i}", [128, 512], BF16) for i in range(2)]
            SA = ph2.enter_context(nc.psum_tensor("SA", [128, 1024], F32))
            SB = ph2.enter_context(nc.psum_tensor("SBp", [128, 1024], F32))
            PV = [ph2.enter_context(nc.psum_tensor(f"PV{i}", [128, 512], F32))
                  for i in range(2)]
            SUM = ph2.enter_context(nc.psum_tensor("SUMp", [1, 512], F32))
            RBC = ph2.enter_context(nc.psum_tensor("RBCp", [128, 512], F32))

            _msk_tot = {}
            _mcnt = 0
            _gc = 0
            for _it in range(NIT):
                for _qg in range(NQG):
                    for _r in range(4 * _qg + 4):
                        if _r >= 4 * _qg:
                            _j = _r - 4 * _qg
                            _mcnt += _j + 1
                            _msk_tot[_gc] = _mcnt
                        _gc += 1

            with nc.Block() as blk:
                @blk.sync
                def _(sy):
                    for it in range(NIT):
                        s, h = it // 4, it % 4
                        if it >= 2:
                            sy.wait_ge(s_pvmm.h, CPI * (it - 1))
                        sl = it % 2
                        s_qkvld[sl].inc(sy.dma_start(
                            out=qh[sl][:, :],
                            in_=q_d[h * 128:(h + 1) * 128,
                                    s * SEQ:(s + 1) * SEQ]), 16)
                        s_qkvld[sl].inc(sy.dma_start(
                            out=kh[sl][:, :],
                            in_=k_d[h * 128:(h + 1) * 128,
                                    s * SEQ:(s + 1) * SEQ]), 16)
                        for kb in range(KBS):
                            s_qkvld[sl].inc(sy.dma_start(
                                out=vh[sl][:, kb * 128:(kb + 1) * 128],
                                in_=v_d[s * SEQ + kb * 128:
                                        s * SEQ + (kb + 1) * 128,
                                        h * 128:(h + 1) * 128]), 16)

                @blk.tensor
                def _(t):
                    tA = 0
                    gc = 0
                    for it in range(NIT):
                        sl = it % 2
                        t.wait_ge(s_qkvld[sl].h, (it // 2 + 1) * (2 + KBS) * 16)
                        for qg in range(NQG):
                            for kb5 in range(qg + 1):
                                for qs in range(4):
                                    if tA >= 2:
                                        t.wait_ge(s_maxA.h, tA - 1)
                                    ins = t.matmul(
                                        SA[:, (tA % 2) * 512:(tA % 2 + 1) * 512],
                                        qh[sl][:, qg * 512 + qs * 128:
                                               qg * 512 + (qs + 1) * 128],
                                        kh[sl][:, kb5 * 512:(kb5 + 1) * 512],
                                        start=True, stop=True)
                                    s_mmA.inc(ins)
                                    tA += 1
                        t.wait_ge(s_id32.h, 32)
                        t.wait_ge(s_cred.h, it + 1)
                        s_ctr.inc(t.transpose(RBC[0:1, 0:128],
                                              maxred[sl][:, :], sb_id32[:, :]))
                        t.wait_ge(s_cneg.h, it + 1)
                        if it >= 1:
                            t.wait_ge(s_rbcs.h, NQG * it)
                        s_cbc.inc(t.matmul(RBC[:, 0:1], sb_ones_f[:, :],
                                           csc[sl][0:1, 1:2],
                                           start=True, stop=True))
                        for qg in range(NQG):
                            qgg = it * NQG + qg
                            nkb = 4 * qg + 4

                            def s_mm(r, gc_r):
                                if gc_r >= 2:
                                    t.wait_ge(s_expB.h, gc_r - 1)
                                ins = t.matmul(
                                    SB[:, (gc_r % 2) * 512:(gc_r % 2 + 1) * 512],
                                    kh[sl][:, r * 128:(r + 1) * 128],
                                    qh[sl][:, qg * 512:(qg + 1) * 512],
                                    start=True, stop=True)
                                s_mmB.inc(ins)

                            def pv_sum(r, gc_r):
                                if r >= 4 * qg:
                                    t.wait_ge(s_mskB.h, _msk_tot[gc_r])
                                else:
                                    t.wait_ge(s_expB.h, gc_r + 1)
                                if r == 0 and qgg >= 2:
                                    t.wait_ge(s_mul.h, qgg - 1)
                                t.matmul(PV[qg % 2][:, :],
                                         vh[sl][:, r * 128:(r + 1) * 128],
                                         pbuf[gc_r % 4][:, :],
                                         start=(r == 0), stop=(r == nkb - 1))
                                if r == 0 and qgg >= 1:
                                    t.wait_ge(s_rcp.h, qgg)
                                ins = t.matmul(SUM[0:1, :], sb_ones_b[:, :],
                                               pbuf[gc_r % 4][:, :],
                                               start=(r == 0), stop=(r == nkb - 1))
                                s_pvmm.inc(ins)

                            for r in range(nkb):
                                s_mm(r, gc + r)
                                if r >= 1:
                                    pv_sum(r - 1, gc + r - 1)
                            pv_sum(nkb - 1, gc + nkb - 1)
                            gc += nkb
                            t.wait_ge(s_rcp.h, qgg + 1)
                            t.wait_ge(s_cbias.h, it + 1)
                            if qgg >= 1:
                                t.wait_ge(s_rbcs.h, qgg)
                            s_rbcp.inc(t.matmul(RBC[:, :], sb_ones_f[:, :],
                                                rsb[qgg % 2][:, :],
                                                start=True, stop=True))

                @blk.vector
                def _(v):
                    tA = 0
                    for it in range(NIT):
                        sl = it % 2
                        for ti in range(_n_passA):
                            v.wait_ge(s_mmA.h, tA + 1)
                            s_maxA.inc(v.reduce_max(
                                tmpmax[sl][:, ti:ti + 1],
                                SA[:, (tA % 2) * 512:(tA % 2 + 1) * 512],
                                axis=AX.X))
                            tA += 1
                        v.drain()
                        s_cred.inc(v.reduce_max(maxred[sl][:, :],
                                                tmpmax[sl][:, 0:_n_passA],
                                                axis=AX.X))
                        v.drain()
                        v.wait_ge(s_ctr.h, it + 1)
                        v.reduce_max(csc[sl][0:1, 0:1], RBC[0:1, 0:128],
                                     axis=AX.X)
                        v.drain()
                        s_cneg.inc(v.tensor_scalar(
                            csc[sl][0:1, 1:2], csc[sl][0:1, 0:1],
                            -INV_SQRT_HD, None, ALU.mult))
                        v.drain()
                        for qg in range(NQG):
                            qgg = it * NQG + qg
                            v.wait_ge(s_pvmm.h, _chk_tot[it][qg])
                            s_rcp.inc(v.reciprocal(rsb[qgg % 2][:, :],
                                                   SUM[0:1, :]))
                            v.drain()
                            v.wait_ge(s_rbcs.h, qgg + 1)
                            if qgg >= 2:
                                v.wait_ge(s_ustg.h, qgg - 1)
                            s_mul.inc(v.tensor_tensor(
                                umul[qgg % 2][:, :], PV[qg % 2][:, :],
                                rbc[qgg % 2][:, :], op=ALU.mult))

                @blk.scalar
                def _(sc):
                    gc = 0
                    for it in range(NIT):
                        sl = it % 2
                        s, h = it // 4, it % 4
                        sc.wait_ge(s_cbc.h, it + 1)
                        n = s_cbias.inc(sc.copy(cb[sl][:, :], RBC[:, 0:1]))
                        sc.wait_ge(s_cbias.h, n)
                        for qg in range(NQG):
                            qgg = it * NQG + qg
                            nkb = 4 * qg + 4
                            for r in range(nkb):
                                sc.wait_ge(s_mmB.h, gc + 1)
                                if gc >= 4:
                                    sc.wait_ge(s_pvmm.h, gc - 3)
                                s_expB.inc(sc.activation(
                                    pbuf[gc % 4][:, :],
                                    SB[:, (gc % 2) * 512:(gc % 2 + 1) * 512],
                                    AF.Exp, bias=cb[sl][:, :],
                                    scale=INV_SQRT_HD))
                                gc += 1
                            sc.wait_ge(s_rbcp.h, qgg + 1)
                            s_rbcs.inc(sc.copy(rbc[qgg % 2][:, :], RBC[:, :]))
                            sc.wait_ge(s_mul.h, qgg + 1)
                            if qgg >= 2:
                                sc.wait_ge(s_uin[qgg % 2].h,
                                           (qgg // 2) * 16)
                            n = s_ustg.inc(sc.activation(
                                ustg[qgg % 2][:, :], umul[qgg % 2][:, :],
                                AF.Identity, bias=sb_bqkv[:, 8 + h:9 + h]))
                            sc.wait_ge(s_ustg.h, n)
                            tok0 = s * SEQ + qg * 512
                            jj = tok0 // OWN
                            off = tok0 % OWN
                            s_uin[qgg % 2].inc(sc.dma_start(
                                out=u_in[jj, h * 128:(h + 1) * 128,
                                         off:off + 512],
                                in_=ustg[qgg % 2][:, :]), 16)
                    sc.wait_ge(s_uin[0].h, s_uin[0].n)
                    sc.wait_ge(s_uin[1].h, s_uin[1].n)

                @blk.gpsimd
                def _(g):
                    gc = 0
                    mcnt = 0
                    for it in range(NIT):
                        for qg in range(NQG):
                            for r in range(4 * qg + 4):
                                if r >= 4 * qg:
                                    j = r - 4 * qg
                                    g.wait_ge(s_expB.h, gc + 1)
                                    for i in range(j):
                                        s_mskB.inc(g.memset(
                                            pbuf[gc % 4][:, i * 128:(i + 1) * 128],
                                            0.0))
                                        mcnt += 1
                                    s_mskB.inc(g.tensor_tensor(
                                        pbuf[gc % 4][:, j * 128:(j + 1) * 128],
                                        pbuf[gc % 4][:, j * 128:(j + 1) * 128],
                                        sb_tri[:, :], op=ALU.mult))
                                    mcnt += 1
                                    assert _msk_tot[gc] == mcnt
                                gc += 1
                    g.wait_ge(s_uin[0].h, s_uin[0].n)
                    g.wait_ge(s_uin[1].h, s_uin[1].n)
                    with allow_ici_all_to_all():
                        s_a2a.inc(g.collective_compute(
                            "AllToAll", ALU.bypass,
                            replica_groups=[[0, 1, 2, 3, 4, 5, 6, 7]],
                            ins=[u_in.ap().opt()],
                            outs=[u_out.ap().opt()]))

        NTH = OWN // 512
        NIC = INTER // 128
        NRC = RANK // 128
        s3_b1l = [sem("s3_b1l0"), sem("s3_b1l1")]
        s3_b1m = sem("s3_b1m")
        s3_b1e = sem("s3_b1e")
        s3_xno = sem("s3_xno")
        s3_xfo = [sem("s3_xfo0"), sem("s3_xfo1")]
        s3_mr = sem("s3_mr")
        s3_mrb = sem("s3_mrb")
        s3_mrbe = sem("s3_mrbe")
        s3_u1 = [sem("s3_u1a"), sem("s3_u1b")]
        s3_v1 = [sem(f"s3_v1{i}") for i in range(4)]
        s3_u2 = [sem("s3_u2a"), sem("s3_u2b")]
        s3_v2 = [sem("s3_v2a"), sem("s3_v2b")]
        s3_mm = sem("s3_mm")
        s3_evd = sem("s3_evd")
        s3_eva = sem("s3_eva")
        s3_glu = sem("s3_glu")
        s3_fst = [sem(f"s3_fst{i}") for i in range(4)]

        _b1loads = []
        _o = 0
        while _o < 2 * INTER:
            _w = min(2048, 2 * INTER - _o)
            _b1loads.append((_o, _w))
            _o += _w

        _cnt = 0
        _p_last = []
        for _th in range(NTH):
            for _f in range(NFC):
                _cnt += 1
        _cnt = NTH * NFC
        _hgrp_flat = []
        _h_mm = []
        _u2grp = []
        _z_mm = []
        _f_mm = []
        for _th in range(NTH):
            for _ig in range(NIC):
                _cnt += 1
                _g = _cnt
                _cnt += 1
                _u = _cnt
                _h_mm.append((_g, _u))
                _hgrp_flat += [_g, _u]
            for _i in range(NIC):
                _cnt += 1
                _u2grp.append(_cnt)
            _z_mm.append(_cnt)
            _row = []
            for _j in range(8):
                for _ts in range(4):
                    _cnt += 1
                    _row.append(_cnt)
            _f_mm.append(_row)
        _v1gate = {g: _hgrp_flat[g - 4] for g in range(4, 2 * NTH * NIC)}
        _u2gate = {g: _u2grp[g - 2] for g in range(2, NTH * NIC)}
        _v2gate = {g: _f_mm[(g - 2) // 8][((g - 2) % 8) * 4 + 3]
                   for g in range(2, NTH * 8)}
        _fevd = {}
        for _m in range(32 * NTH):
            _tm = _m // 32
            _fevd[_m] = 8 * NTH + 40 * _tm + 8 + (_m - 32 * _tm) + 1

        with ExitStack() as ph3:
            sb = lambda n, s, d: ph3.enter_context(nc.sbuf_tensor(n, s, d))
            pp = [ph3.enter_context(nc.psum_tensor(f"pp{i}", [128, 512], F32))
                  for i in range(8)]
            b1row = [sb(f"b1row{i}", [1, 2048], F32) for i in range(2)]
            b1c = sb("b1c", [128, 2 * NIC], F32)
            xnoT = [sb(f"xnoT{i}", [128, 512], BF16) for i in range(2)]
            xfo = [sb(f"xfo{i}", [128, 512], F32) for i in range(2)]
            mr3row = sb("mr3row", [1, 2 * OWN], BF16)
            mbc3 = sb("mbc3", [128, 512], BF16)
            rbc3 = sb("rbc3", [128, 512], BF16)
            u1sb = [sb(f"u1sb{i}", [128, RANK], BF16) for i in range(2)]
            v1sb = [sb(f"v1sb{i}", [128, RANK], BF16) for i in range(4)]
            u2sb = [sb(f"u2sb{i}", [128, RANK], BF16) for i in range(2)]
            v2sb = [sb(f"v2sb{i}", [128, RANK * 4], BF16) for i in range(2)]
            g_sb = [sb(f"g_sb{i}", [128, 512], F32) for i in range(2)]
            up_sb = [sb(f"up_sb{i}", [128, 512], F32) for i in range(2)]
            P_sb = sb("P_sb", [128, NRC * OWN], BF16)
            act_sb = sb("act_sb", [128, NIC * 512], BF16)
            Z_sb = sb("Z_sb", [128, NRC * 512], BF16)
            ffn_stg = [sb(f"ffn_stg{i}", [128, 512], BF16) for i in range(4)]

            with nc.Block() as blk:
                @blk.gpsimd
                def _(g):
                    for th in range(NTH):
                        g.wait_ge(s3_mrbe.h, 2 * (th + 1))
                        for f in range(NFC):
                            gi = th * NFC + f
                            g.wait_ge(s3_xfo[gi % 2].h, (gi // 2 + 1) * 16)
                            if gi >= 2:
                                g.wait_ge(s3_mm.h, gi - 1)
                            s3_u1[gi % 2].inc(g.dma_start(
                                out=u1sb[gi % 2][:, :],
                                in_=u1[f * 128:(f + 1) * 128, :]), 16)
                            g.tensor_tensor(xfo[gi % 2][:, :],
                                            xfo[gi % 2][:, :],
                                            mbc3[:, :], op=ALU.subtract)
                            g.drain()
                            s3_xno.inc(g.tensor_tensor(
                                xnoT[gi % 2][:, :], xfo[gi % 2][:, :],
                                rbc3[:, :], op=ALU.mult))
                    for th in range(NTH):
                        for i in range(NIC):
                            gi = th * NIC + i
                            if gi >= 2:
                                g.wait_ge(s3_mm.h, _u2gate[gi])
                            s3_u2[gi % 2].inc(g.dma_start(
                                out=u2sb[gi % 2][:, :],
                                in_=u2[i * 128:(i + 1) * 128, :]), 16)
                        for j in range(8):
                            gi = th * 8 + j
                            if gi >= 2:
                                g.wait_ge(s3_mm.h, _v2gate[gi])
                            for rc in range(NRC):
                                s3_v2[gi % 2].inc(g.dma_start(
                                    out=v2sb[gi % 2][:, rc * 512:(rc + 1) * 512],
                                    in_=v2[rc * 128:(rc + 1) * 128,
                                           j * 512:(j + 1) * 512]), 16)

                @blk.sync
                def _(sy):
                    s3_mr.inc(sy.dma_start(out=mr3row[:, :],
                                           in_=mrown_d[:, :]), 16)
                    nch_done = 0
                    for li, (off, wid) in enumerate(_b1loads):
                        if li >= 2:
                            sy.wait_ge(s3_b1m.h, nch_done)
                        s3_b1l[li % 2].inc(sy.dma_start(
                            out=b1row[li % 2][:, 0:wid],
                            in_=b1[:, off:off + wid]), 16)
                        if li >= 1:
                            nch_done += _b1loads[li - 1][1] // 128
                    for th in range(NTH):
                        for f in range(NFC):
                            gi = th * NFC + f
                            if gi >= 2:
                                sy.wait_ge(s3_xno.h, gi - 1)
                                sy.wait_ge(s3_mm.h, gi - 1)
                            s3_xfo[gi % 2].inc(sy.dma_start(
                                out=xfo[gi % 2][:, :],
                                in_=xf_own[f * 128:(f + 1) * 128,
                                           th * 512:(th + 1) * 512]), 16)
                    sy.wait_ge(s_bgv.h, s_bgv.n)
                    for th in range(NTH):
                        for ig in range(NIC):
                            for gu in range(2):
                                gidx = (th * NIC + ig) * 2 + gu
                                if gidx >= 4:
                                    sy.wait_ge(s3_mm.h, _v1gate[gidx])
                                pidx = ig * 2 + gu
                                s3_v1[gidx % 4].inc(sy.dma_start(
                                    out=v1sb[gidx % 4][:, :],
                                    in_=v1_p[pidx, :, :]), 16)

                @blk.tensor
                def _(t):
                    mmn = 0

                    def bcast(th):
                        if th >= 1:
                            t.wait_ge(s3_mrbe.h, 2 * th)
                            t.wait_ge(s3_xno.h, th * NFC)
                        t.matmul(pp[6][:, :], sb_onesrb[:, :],
                                 mr3row[0:1, th * 512:(th + 1) * 512],
                                 start=True, stop=True)
                        ins = t.matmul(pp[7][:, :], sb_onesrb[:, :],
                                       mr3row[0:1, OWN + th * 512:
                                              OWN + (th + 1) * 512],
                                       start=True, stop=True)
                        s3_mrb.inc(ins)

                    t.wait_ge(s3_mr.h, 16)
                    bcast(0)
                    nch = 0
                    for li, (off, wid) in enumerate(_b1loads):
                        t.wait_ge(s3_b1l[li % 2].h, (li // 2 + 1) * 16)
                        ins = None
                        for cc in range(wid // 128):
                            ins = t.matmul(
                                pp[0][:, nch:nch + 1],
                                b1row[li % 2][0:1, cc * 128:(cc + 1) * 128],
                                sb_ones_f[:, 0:1], start=True, stop=True)
                            nch += 1
                        s3_b1m.inc(ins, wid // 128)
                    t.wait_ge(s3_b1e.h, 1)
                    for th in range(NTH):
                        if th >= 1:
                            t.wait_ge(s3_evd.h, 8 * th)
                            bcast(th)
                        for f in range(NFC):
                            gi = th * NFC + f
                            t.wait_ge(s3_xno.h, gi + 1)
                            t.wait_ge(s3_u1[gi % 2].h, (gi // 2 + 1) * 16)
                            ins = None
                            for rc in range(NRC):
                                ins = t.matmul(
                                    pp[rc][:, :],
                                    u1sb[gi % 2][:, rc * 128:(rc + 1) * 128],
                                    xnoT[gi % 2][:, :],
                                    start=(f == 0), stop=(f == NFC - 1))
                            mmn = s3_mm.inc(ins)
                    _Pev = 8 * NTH
                    for th in range(NTH):
                        for ig in range(NIC):
                            gidx2 = (th * NIC + ig) * 2
                            for gu in range(2):
                                gidx = gidx2 + gu
                                slot = gidx % 4
                                bank = pp[2 * (ig % 2) + gu]
                                t.wait_ge(s3_v1[gidx % 4].h,
                                          (gidx // 4 + 1) * 16)
                                if ig < 2:
                                    t.wait_ge(s3_evd.h, 8 * NTH + 40 * th)
                                if gidx >= 4:
                                    t.wait_ge(s3_eva.h, gidx - 3)
                                ins = None
                                for rc in range(NRC):
                                    ins = t.matmul(
                                        bank[:, :],
                                        v1sb[slot][:, rc * 128:(rc + 1) * 128],
                                        P_sb[:, rc * OWN + th * 512:
                                             rc * OWN + (th + 1) * 512],
                                        start=(rc == 0), stop=(rc == NRC - 1))
                                mmn = s3_mm.inc(ins)
                        for i in range(NIC):
                            gi = th * NIC + i
                            t.wait_ge(s3_u2[gi % 2].h, (gi // 2 + 1) * 16)
                            t.wait_ge(s3_glu.h, th * NIC + i + 1)
                            if i == 0:
                                t.wait_ge(s3_eva.h, 2 * NIC * (th + 1))
                            ins = None
                            for rc in range(NRC):
                                ins = t.matmul(
                                    pp[rc][:, :],
                                    u2sb[gi % 2][:, rc * 128:(rc + 1) * 128],
                                    act_sb[:, i * 512:(i + 1) * 512],
                                    start=(i == 0), stop=(i == NIC - 1))
                            mmn = s3_mm.inc(ins)
                        t.wait_ge(s3_evd.h, 8 * NTH + 40 * th + 8)
                        for j in range(8):
                            gi = th * 8 + j
                            t.wait_ge(s3_v2[gi % 2].h, (gi // 2 + 1) * 8 * 16)
                            for ts in range(4):
                                m = gi * 4 + ts
                                if m >= 4:
                                    t.wait_ge(s3_evd.h, _fevd[m - 4])
                                ins = None
                                for rc in range(NRC):
                                    ins = t.matmul(
                                        pp[ts][:, :],
                                        Z_sb[:, rc * 512 + ts * 128:
                                             rc * 512 + (ts + 1) * 128],
                                        v2sb[gi % 2][:, rc * 512:(rc + 1) * 512],
                                        start=(rc == 0), stop=(rc == NRC - 1))
                                mmn = s3_mm.inc(ins)

                @blk.scalar
                def _(sc):
                    sc.wait_ge(s3_mrb.h, 1)
                    s3_mrbe.inc(sc.copy(mbc3[:, :], pp[6][:, :]))
                    s3_mrbe.inc(sc.copy(rbc3[:, :], pp[7][:, :]))
                    sc.wait_ge(s3_b1m.h, 2 * NIC)
                    s3_b1e.inc(sc.copy(b1c[:, :], pp[0][:, 0:2 * NIC]))
                    for th in range(1, NTH):
                        sc.wait_ge(s3_mrb.h, th + 1)
                        s3_mrbe.inc(sc.copy(mbc3[:, :], pp[6][:, :]))
                        s3_mrbe.inc(sc.copy(rbc3[:, :], pp[7][:, :]))
                    for th in range(NTH):
                        for ig in range(NIC):
                            sc.wait_ge(s3_mm.h, _h_mm[th * NIC + ig][0])
                            if th * NIC + ig >= 2:
                                sc.wait_ge(s3_glu.h, th * NIC + ig - 1)
                            s3_eva.inc(sc.activation(
                                g_sb[ig % 2][:, :], pp[2 * (ig % 2)][:, :],
                                AF.Silu, bias=b1c[:, ig:ig + 1]))
                            sc.wait_ge(s3_mm.h, _h_mm[th * NIC + ig][1])
                            s3_eva.inc(sc.activation(
                                up_sb[ig % 2][:, :], pp[2 * (ig % 2) + 1][:, :],
                                AF.Identity, bias=b1c[:, NIC + ig:NIC + ig + 1]))
                        for j in range(8):
                            for ts in range(4):
                                m = (th * 8 + j) * 4 + ts
                                sc.wait_ge(s3_evd.h, _fevd[m])
                                s3_fst[m % 4].inc(sc.dma_start(
                                    out=ffn_d[th * 512 + ts * 128:
                                              th * 512 + (ts + 1) * 128,
                                              j * 512:(j + 1) * 512],
                                    in_=ffn_stg[m % 4][:, :]), 16)
                    for i in range(4):
                        sc.wait_ge(s3_fst[i].h, s3_fst[i].n)

                @blk.vector
                def _(v):
                    for th in range(NTH):
                        v.wait_ge(s3_mm.h, NFC * (th + 1))
                        for rc in range(NRC):
                            s3_evd.inc(v.tensor_copy(
                                P_sb[:, rc * OWN + th * 512:
                                     rc * OWN + (th + 1) * 512],
                                pp[rc][:, :]))
                    for th in range(NTH):
                        for ig in range(NIC):
                            gidx2 = (th * NIC + ig) * 2
                            v.wait_ge(s3_eva.h, gidx2 + 2)
                            s3_glu.inc(v.tensor_tensor(
                                act_sb[:, ig * 512:(ig + 1) * 512],
                                g_sb[ig % 2][:, :], up_sb[ig % 2][:, :],
                                op=ALU.mult))
                        v.wait_ge(s3_mm.h, _z_mm[th])
                        for rc in range(NRC):
                            s3_evd.inc(v.tensor_copy(
                                Z_sb[:, rc * 512:(rc + 1) * 512], pp[rc][:, :]))
                        for j in range(8):
                            for ts in range(4):
                                m = (th * 8 + j) * 4 + ts
                                v.wait_ge(s3_mm.h, _f_mm[th][j * 4 + ts])
                                if m >= 4:
                                    v.wait_ge(s3_fst[m % 4].h,
                                              (m // 4) * 16)
                                s3_evd.inc(v.tensor_copy(
                                    ffn_stg[m % 4][:, :], pp[ts][:, :]))

        s4_u = sem("s4_u")
        s4_wo = [sem(f"s4_wo{i}") for i in range(3)]
        s4_bb = sem("s4_bb")
        s4_bbv = sem("s4_bbv")
        s4_bbm = sem("s4_bbm")
        s4_bbe = sem("s4_bbe")
        s4_mm = sem("s4_mm")
        s4_add = sem("s4_add")
        s4_ffn = [sem("s4_ffn0"), sem("s4_ffn1")]
        s4_x = [sem("s4_x0"), sem("s4_x1")]
        s4_out = [sem("s4_out0"), sem("s4_out1")]
        NTS = OWN // 128

        with ExitStack() as ph4:
            sb = lambda n, s, d: ph4.enter_context(nc.sbuf_tensor(n, s, d))
            pq = [ph4.enter_context(nc.psum_tensor(f"pq{i}", [128, 512], F32))
                  for i in range(5)]
            u_sb = sb("u_sb", [128, 32 * OWN], BF16)
            wo_sb = [sb(f"wo_sb{i}", [128, 32 * 512], BF16) for i in range(3)]
            bbbc = sb("bbbc", [128, D], F32)
            b2_r = sb("b2_r", [1, D], F32)
            t1b = [sb(f"t1b{i}", [128, 512], F32) for i in range(2)]
            toutb = [sb(f"toutb{i}", [128, 512], F32) for i in range(2)]
            ffn_rd = [sb(f"ffn_rd{i}", [128, 512], BF16) for i in range(2)]
            x_rd = [sb(f"x_rd{i}", [128, 512], F32) for i in range(2)]

            with nc.Block() as blk:
                @blk.sync
                def _(sy):
                    s4_bb.inc(sy.dma_start(out=bbbc[0:1, :], in_=bo[:, :]), 16)
                    s4_bb.inc(sy.dma_start(out=b2_r[:, :], in_=b2[:, :]), 16)
                    sy.wait_ge(s_a2a.h, 1)
                    for dc in range(32):
                        src = u_out[dc // 4, (dc % 4) * 128:(dc % 4 + 1) * 128, :]
                        s4_u.inc(sy.dma_start(
                            out=u_sb[:, dc * OWN:(dc + 1) * OWN], in_=src), 16)
                    for j in range(8):
                        for ts in range(NTS):
                            m = j * NTS + ts
                            if m >= 2:
                                sy.wait_ge(s4_add.h, m - 1)
                            s4_x[m % 2].inc(sy.dma_start(
                                out=x_rd[m % 2][:, :],
                                in_=x_own[ts * 128:(ts + 1) * 128,
                                          j * 512:(j + 1) * 512]), 16)
                            s4_ffn[m % 2].inc(sy.dma_start(
                                out=ffn_rd[m % 2][:, :],
                                in_=ffn_d[ts * 128:(ts + 1) * 128,
                                          j * 512:(j + 1) * 512]), 16)

                @blk.gpsimd
                def _(g):
                    for j in range(8):
                        if j >= 3:
                            g.wait_ge(s4_mm.h, NTS * (j - 2))
                        for dc in range(32):
                            s4_wo[j % 3].inc(g.dma_start(
                                out=wo_sb[j % 3][:, dc * 512:(dc + 1) * 512],
                                in_=wo_t[dc * 128:(dc + 1) * 128,
                                         j * 512:(j + 1) * 512]), 16)

                @blk.vector
                def _(v):
                    for j in range(8):
                        for ts in range(NTS):
                            m = j * NTS + ts
                            v.wait_ge(s4_mm.h, m + 1)
                            v.wait_ge(s4_bbe.h, 8)
                            v.tensor_tensor(t1b[m % 2][:, :], pq[ts % 4][:, :],
                                            bbbc[:, j * 512:(j + 1) * 512],
                                            op=ALU.add)
                            v.drain()
                            v.wait_ge(s4_ffn[m % 2].h, (m // 2 + 1) * 16)
                            v.tensor_tensor(t1b[m % 2][:, :], t1b[m % 2][:, :],
                                            ffn_rd[m % 2][:, :], op=ALU.add)
                            v.drain()
                            v.wait_ge(s4_x[m % 2].h, (m // 2 + 1) * 16)
                            if m >= 2:
                                v.wait_ge(s4_out[m % 2].h, (m // 2) * 16)
                            s4_add.inc(v.tensor_tensor(
                                toutb[m % 2][:, :], t1b[m % 2][:, :],
                                x_rd[m % 2][:, :], op=ALU.add))
                            v.drain()

                @blk.tensor
                def _(t):
                    t.wait_ge(s4_bb.h, 32)
                    for j in range(8):
                        t.matmul(pq[4][:, :], sb_ones_f[:, :],
                                 bbbc[0:1, j * 512:(j + 1) * 512],
                                 start=True, stop=False)
                        ins = t.matmul(pq[4][:, :], sb_ones_f[:, :],
                                       b2_r[0:1, j * 512:(j + 1) * 512],
                                       start=False, stop=True)
                        s4_bbm.inc(ins)
                        t.wait_ge(s4_bbe.h, j + 1)
                    t.wait_ge(s4_u.h, 32 * 16)
                    for j in range(8):
                        t.wait_ge(s4_wo[j % 3].h, (j // 3 + 1) * 32 * 16)
                        for ts in range(NTS):
                            m = j * NTS + ts
                            if m >= 4:
                                t.wait_ge(s4_add.h, m - 3)
                            ins = None
                            for dc in range(32):
                                ins = t.matmul(
                                    pq[ts % 4][:, :],
                                    u_sb[:, dc * OWN + ts * 128:
                                         dc * OWN + (ts + 1) * 128],
                                    wo_sb[j % 3][:, dc * 512:(dc + 1) * 512],
                                    start=(dc == 0), stop=(dc == 31))
                            s4_mm.inc(ins)

                @blk.scalar
                def _(sc):
                    for j in range(8):
                        sc.wait_ge(s4_bbm.h, j + 1)
                        s4_bbe.inc(sc.copy(bbbc[:, j * 512:(j + 1) * 512],
                                           pq[4][:, :]))
                    for j in range(8):
                        for ts in range(NTS):
                            m = j * NTS + ts
                            sc.wait_ge(s4_add.h, m + 1)
                            s4_out[m % 2].inc(sc.dma_start(
                                out=out_own[ts * 128:(ts + 1) * 128,
                                            j * 512:(j + 1) * 512],
                                in_=toutb[m % 2][:, :]), 16)
                    sc.wait_ge(s4_out[0].h, s4_out[0].n)
                    sc.wait_ge(s4_out[1].h, s4_out[1].n)

    return nc


def _shard_inputs(inputs):
    f = lambda a: np.ascontiguousarray(np.asarray(a, dtype=np.float32))
    x = f(inputs["L0_attn_x"]).reshape(T, D)
    Wq, Wk, Wv = f(inputs["L0_attn_Wq"]), f(inputs["L0_attn_Wk"]), f(inputs["L0_attn_Wv"])
    bq, bk, bv = f(inputs["L0_attn_bq"]), f(inputs["L0_attn_bk"]), f(inputs["L0_attn_bv"])
    Wo, bo = f(inputs["L0_attn_Wo"]), f(inputs["L0_attn_bo"])
    U1, V1 = f(inputs["L0_ffn_U1"]), f(inputs["L0_ffn_V1"])
    U2, V2 = f(inputs["L0_ffn_U2"]), f(inputs["L0_ffn_V2"])
    b1, b2 = f(inputs["L0_ffn_b1"]), f(inputs["L0_ffn_b2"])
    x_f = np.ascontiguousarray(x.T)
    Wo_t = np.ascontiguousarray(Wo.T)
    in_maps = []
    for c in range(8):
        sl = slice(c * HSL, (c + 1) * HSL)
        in_maps.append({
            "x": x,
            "x_own": np.ascontiguousarray(x[c * OWN:(c + 1) * OWN]),
            "x_f": x_f,
            "xf_own": np.ascontiguousarray(x_f[:, c * OWN:(c + 1) * OWN]),
            "wq_t": np.ascontiguousarray(Wq[sl].T),
            "wk_t": np.ascontiguousarray(Wk[sl].T),
            "wv_t": np.ascontiguousarray(Wv[sl].T),
            "bq": np.ascontiguousarray(bq[sl]).reshape(1, HSL),
            "bk": np.ascontiguousarray(bk[sl]).reshape(1, HSL),
            "bv": np.ascontiguousarray(bv[sl]).reshape(1, HSL),
            "wo_t": Wo_t,
            "bo": bo.reshape(1, D),
            "u1": U1, "v1": V1, "u2": U2, "v2": V2,
            "b1": b1.reshape(1, 2 * INTER), "b2": b2.reshape(1, D),
        })
    return in_maps


def kernel(**inputs):
    nc = build()
    in_maps = _shard_inputs(inputs)
    res = run_bass_kernel_spmd(nc, in_maps, core_ids=list(range(8)))
    out = np.concatenate(
        [np.asarray(res.results[c]["out_own"], dtype=np.float32)
         for c in range(8)], axis=0)
    return np.ascontiguousarray(out.reshape(2, SEQ, D))


# version 10
# speedup vs baseline: 1.1009x; 1.0498x over previous
import numpy as np
import ml_dtypes
from contextlib import ExitStack

import concourse.bass as bass
import concourse.mybir as mybir
from concourse.bass_utils import run_bass_kernel_spmd
from concourse.aot_env import allow_ici_all_to_all

F32 = mybir.dt.float32
BF16 = mybir.dt.bfloat16
AF = mybir.ActivationFunctionType
ALU = mybir.AluOpType
AX = mybir.AxisListType

D = 4096
T = 8192
SEQ = 4096
HSL = 512
NHL = 4
HD = 128
OWN = 1024
RANK = 1024
INTER = 11008
NFC = D // 128
CHK = 256
NCH = T // CHK
TPC = CHK // 128
OTC = 10
NLT = T // 128
NTILE = NLT + 8
INV_SQRT_HD = float(1.0 / np.sqrt(128.0))
EPS = 1e-5


class SemC:

    def __init__(self, handle):
        self.h = handle
        self.n = 0

    def inc(self, instr, k=1):
        instr.then_inc(self.h, k)
        self.n += k
        return self.n


def build(dump=()):
    nc = bass.Bass("TRN2", target_bir_lowering=False)

    x = nc.dram_tensor("x", [T, D], F32, kind="ExternalInput")
    x_own = nc.dram_tensor("x_own", [OWN, D], F32, kind="ExternalInput")
    x_q = nc.dram_tensor("x_q", [NCH, D, CHK], F32, kind="ExternalInput")
    xf_own = nc.dram_tensor("xf_own", [D, OWN], F32, kind="ExternalInput")
    wq_t = nc.dram_tensor("wq_t", [D, HSL], F32, kind="ExternalInput")
    wk_t = nc.dram_tensor("wk_t", [D, HSL], F32, kind="ExternalInput")
    wv_t = nc.dram_tensor("wv_t", [D, HSL], F32, kind="ExternalInput")
    bq = nc.dram_tensor("bq", [1, HSL], F32, kind="ExternalInput")
    bk = nc.dram_tensor("bk", [1, HSL], F32, kind="ExternalInput")
    bv = nc.dram_tensor("bv", [1, HSL], F32, kind="ExternalInput")
    wo_t = nc.dram_tensor("wo_t", [D, D], F32, kind="ExternalInput")
    bo = nc.dram_tensor("bo", [1, D], F32, kind="ExternalInput")
    u1 = nc.dram_tensor("u1", [D, RANK], F32, kind="ExternalInput")
    v1 = nc.dram_tensor("v1", [RANK, 2 * INTER], F32, kind="ExternalInput")
    u2 = nc.dram_tensor("u2", [INTER, RANK], F32, kind="ExternalInput")
    v2 = nc.dram_tensor("v2", [RANK, D], F32, kind="ExternalInput")
    b1 = nc.dram_tensor("b1", [1, 2 * INTER], F32, kind="ExternalInput")
    b2 = nc.dram_tensor("b2", [1, D], F32, kind="ExternalInput")
    out_own = nc.dram_tensor("out_own", [OWN, D], F32, kind="ExternalOutput")

    def scratch(name, shape, dtype):
        kind = "ExternalOutput" if name in dump else "Internal"
        return nc.dram_tensor(name, shape, dtype, kind=kind)

    q_d = scratch("q_d", [HSL, T], BF16)
    k_d = scratch("k_d", [HSL, T], BF16)
    v_d = scratch("v_d", [T, HSL], BF16)
    mrown_d = scratch("mrown_d", [1, 2 * OWN], BF16)
    v1_p = scratch("v1_p", [2 * (INTER // 128), 128, RANK], BF16)
    u_in = scratch("u_in", [8, HSL, OWN], BF16)
    u_out = scratch("u_out", [8, HSL, OWN], BF16)
    ffn_d = scratch("ffn_d", [OWN, D], BF16)

    ones_bcol = nc.inline_tensor(np.ones((128, 1), dtype=ml_dtypes.bfloat16),
                                 "ones_bcol")
    ones_frow = nc.inline_tensor(np.ones((1, 128), dtype=np.float32),
                                 "ones_frow")
    ones_frb = nc.inline_tensor(np.ones((1, 128), dtype=ml_dtypes.bfloat16),
                                "ones_frb")
    tri = np.triu(np.ones((128, 128), np.float32)).astype(ml_dtypes.bfloat16)
    tri_t = nc.inline_tensor(tri, "tri_t")
    ident_f32 = nc.inline_tensor(np.eye(128, dtype=np.float32), "ident_f32")

    with ExitStack() as top:
        sem = lambda name: SemC(top.enter_context(nc.semaphore(name)))

        sb_ones_b = nc.alloc_sbuf_tensor("sb_ones_b", [128, 1], BF16)
        sb_ones_f = nc.alloc_sbuf_tensor("sb_ones_f", [1, 128], F32)
        sb_tri = nc.alloc_sbuf_tensor("sb_tri", [128, 128], BF16)
        sb_bqkv = nc.alloc_sbuf_tensor("sb_bqkv", [128, 12], F32)
        sb_id32 = nc.alloc_sbuf_tensor("sb_id32", [128, 128], F32)
        sb_onesrb = nc.alloc_sbuf_tensor("sb_onesrb", [1, 128], BF16)

        s_const = sem("s_const")
        s_id32 = sem("s_id32")

        s_wt = [sem("s_wtq"), sem("s_wtk"), sem("s_wtv")]
        s_bgv = sem("s_bgv")
        s_bv = sem("s_bv")
        s_bprep = sem("s_bprep")
        s_bcols = sem("s_bcols")
        s_ldx = [sem(f"s_ldx{i}") for i in range(4)]
        s_st = sem("s_st")
        s_ln1 = sem("s_ln1")
        s_ln2 = sem("s_ln2")
        s_mrt = sem("s_mrt")
        s_mre = sem("s_mre")
        s_bcmm = sem("s_bcmm")
        s_bce = sem("s_bce")
        s_xf = [sem("s_xf0"), sem("s_xf1")]
        s_mrst = sem("s_mrst")
        s_xn = sem("s_xn")
        s_mm1 = sem("s_mm1")
        s_evq = sem("s_evq")
        s_evv = sem("s_evv")
        s_qstk = sem("s_qstk")
        s_qstv = sem("s_qstv")

        NCHA = NCH + (OWN // CHK)

        with ExitStack() as ph1:
            sb = lambda n, s, d: ph1.enter_context(nc.sbuf_tensor(n, s, d))
            pt = lambda n, s: ph1.enter_context(nc.psum_tensor(n, s, F32))
            w_qT = sb("w_qT", [128, NFC * 512], BF16)
            w_kT = sb("w_kT", [128, NFC * 512], BF16)
            w_vT = sb("w_vT", [128, NFC * 512], BF16)
            xfb = [sb(f"xfb{i}", [128, NFC * 128], F32) for i in range(2)]
            xnb = [sb(f"xnb{i}", [128, NFC * CHK], BF16) for i in range(2)]
            xb = [sb(f"xb{i}", [128, D // 2], F32) for i in range(4)]
            stg = [sb(f"stg{i}", [128, CHK], BF16) for i in range(8)]
            stgv = [sb(f"stgv{i}", [128, 512], BF16) for i in range(2)]
            stats = [sb(f"stats{i}", [128, 48], F32) for i in range(2)]
            mv = [sb(f"mv{i}", [128, 2], F32) for i in range(4)]
            veps = [sb(f"veps{i}", [128, 1], F32) for i in range(2)]
            rstd = [sb(f"rstd{i}", [128, 1], F32) for i in range(2)]
            mr_sb = [sb(f"mr_sb{i}", [1, 512], BF16) for i in range(2)]
            mbc = [sb(f"mbc1_{i}", [128, 256], BF16) for i in range(2)]
            rbc1 = [sb(f"rbc1_{i}", [128, 256], BF16) for i in range(2)]
            sb_bv3 = sb("sb_bv3", [1, HSL], F32)
            ps = [pt(f"p1_{i}", [128, 512]) for i in range(4)]
            ps_b = pt("p1_b", [128, 512])
            ps_mr = ph1.enter_context(nc.psum_tensor("ps_mr", [1, 512], F32))
            ps_bc = pt("ps_bc", [128, 512])

            def half_src(q):
                i, hh = q // 2, q % 2
                cs = slice(hh * (D // 2), (hh + 1) * (D // 2))
                if i < NLT:
                    return x[i * 128:(i + 1) * 128, cs]
                j = i - NLT
                return x_own[j * 128:(j + 1) * 128, cs]

            evac_engine = []
            for c in range(NCH):
                evac_engine += ['A'] * 8 + ['V'] * 2
            actc = np.cumsum([1 if t == 'A' else 0 for t in evac_engine])
            dvec = np.cumsum([1 if t == 'V' else 0 for t in evac_engine])

            with nc.Block() as blk:
                @blk.gpsimd
                def _(g):
                    s_const.inc(g.dma_start(out=sb_ones_b[:, :],
                                            in_=ones_bcol.ap()), 16)
                    s_const.inc(g.dma_start(out=sb_ones_f[:, :],
                                            in_=ones_frow.ap()), 16)
                    s_const.inc(g.dma_start(out=sb_tri[:, :],
                                            in_=tri_t.ap()), 16)
                    s_id32.inc(g.dma_start(out=sb_id32[:, :],
                                           in_=ident_f32.ap()), 16)
                    s_id32.inc(g.dma_start(out=sb_onesrb[:, :],
                                           in_=ones_frb.ap()), 16)
                    for wi, (wsb, wsrc) in enumerate(
                            ((w_qT, wq_t), (w_kT, wk_t), (w_vT, wv_t))):
                        for f in range(NFC):
                            s_wt[wi].inc(g.dma_start(
                                out=wsb[:, f * 512:(f + 1) * 512],
                                in_=wsrc[f * 128:(f + 1) * 128, :]), 16)
                    bg = []
                    for gu in range(2):
                        for ig in range(INTER // 128):
                            gidx = ig * 2 + gu
                            i0 = gu * INTER + ig * 128
                            for half in range(2):
                                srcv = v1[half * 512:(half + 1) * 512,
                                          i0:i0 + 128]
                                srcv = srcv.rearrange("(rc p) i -> p rc i",
                                                      p=128)
                                dstv = v1_p[gidx, :,
                                            half * 512:(half + 1) * 512]
                                dstv = dstv.rearrange("p (rc i) -> p rc i",
                                                      i=128)
                                bg.append((s_bgv, dstv, srcv))
                    bgi = [0]

                    def bg_pump(k):
                        for _ in range(k):
                            if bgi[0] < len(bg):
                                s_, dst, srcb = bg[bgi[0]]
                                s_.inc(g.dma_start(out=dst, in_=srcb), 16)
                                bgi[0] += 1

                    for c in range(NCH):
                        g.wait_ge(s_bce.h, 2 * (c + 1))
                        for h in range(2):
                            gix = 2 * c + h
                            g.wait_ge(s_xf[gix % 2].h, (gix // 2 + 1) * 16 * 16)
                            xf3 = xfb[gix % 2][:, :].rearrange(
                                "p (f t) -> p f t", t=CHK)
                            mslice = mbc[c % 2][:, :]
                            m3 = bass.AP(mslice.tensor, mslice.offset,
                                         [list(mslice.ap[0]), [0, NFC // 2],
                                          [1, CHK]])
                            rslice = rbc1[c % 2][:, :]
                            r3 = bass.AP(rslice.tensor, rslice.offset,
                                         [list(rslice.ap[0]), [0, NFC // 2],
                                          [1, CHK]])
                            xn3 = xnb[c % 2][:, h * 16 * CHK:
                                             (h + 1) * 16 * CHK].rearrange(
                                "p (f t) -> p f t", t=CHK)
                            g.tensor_tensor(xf3, xf3, m3, op=ALU.subtract)
                            g.drain()
                            s_xn.inc(g.tensor_tensor(xn3, xf3, r3,
                                                     op=ALU.mult))
                        bg_pump(12)
                    bg_pump(len(bg))
                    g.wait_ge(s_bgv.h, s_bgv.n)

                @blk.sync
                def _(sy):
                    s_bv.inc(sy.dma_start(out=sb_bv3[:, :], in_=bq[:, :]), 16)
                    sy.wait_ge(s_bprep.h, 1)
                    s_bv.inc(sy.dma_start(out=sb_bv3[:, :], in_=bk[:, :]), 16)
                    sy.wait_ge(s_bprep.h, 2)
                    s_bv.inc(sy.dma_start(out=sb_bv3[:, :], in_=bv[:, :]), 16)
                    for q in range(4):
                        s_ldx[q % 4].inc(sy.dma_start(out=xb[q % 4][:, :],
                                                      in_=half_src(q)), 16)
                    for c in range(NCHA):
                        if c < NCH:
                            for h in range(2):
                                gix = 2 * c + h
                                if gix >= 2:
                                    sy.wait_ge(s_xn.h, gix - 1)
                                for fq in range(NFC // 2):
                                    f = h * 16 + fq
                                    s_xf[gix % 2].inc(sy.dma_start(
                                        out=xfb[gix % 2][:, fq * CHK:
                                                         (fq + 1) * CHK],
                                        in_=x_q[c, f * 128:(f + 1) * 128, :]),
                                        16)
                        for i in range(TPC * c + 2, TPC * c + 2 + TPC):
                            if i < NTILE:
                                for q in (2 * i, 2 * i + 1):
                                    if q >= 4:
                                        sy.wait_ge(s_st.h, q - 3)
                                    s_ldx[q % 4].inc(sy.dma_start(
                                        out=xb[q % 4][:, :],
                                        in_=half_src(q)), 16)

                @blk.vector
                def _(v):
                    def v_evacs(c):
                        if c >= 1:
                            v.wait_ge(s_qstv.h, 2 * c * 16)
                        for j in range(2):
                            gt = c * OTC + 8 + j
                            v.wait_ge(s_mm1.h, gt + 1)
                            s_evv.inc(v.tensor_copy(stgv[j][:, :],
                                                    ps[gt % 4][:, :]))

                    for p in range(NCHA):
                        i0, i1 = 2 * p, 2 * p + 1
                        if p >= 1:
                            v.wait_ge(s_ln2.h, i0)
                        if p >= 2:
                            v.wait_ge(s_mrt.h, p - 1)
                        for i in (i0, i1):
                            sl = i % 2
                            for hh in range(2):
                                q = 2 * i + hh
                                v.wait_ge(s_ldx[q % 4].h, (q // 4 + 1) * 16)
                                ins = None
                                for j in range(4):
                                    ins = v.bn_stats(
                                        stats[sl][:, 24 * hh + 6 * j:
                                                  24 * hh + 6 * j + 6],
                                        xb[q % 4][:, 512 * j:512 * (j + 1)])
                                s_st.inc(ins)
                        v.drain()
                        for i in (i0, i1):
                            v.bn_aggr(mv[i % 4][:, :], stats[i % 2][:, :])
                        v.drain()
                        for i in (i0, i1):
                            v.tensor_scalar(veps[i % 2][:, :],
                                            mv[i % 4][:, 1:2],
                                            EPS, None, ALU.add)
                        v.drain()
                        s_ln1.inc(v.reciprocal(veps[i0 % 2][:, :],
                                               veps[i0 % 2][:, :]))
                        s_ln1.inc(v.reciprocal(veps[i1 % 2][:, :],
                                               veps[i1 % 2][:, :]))
                        if p >= 2 and p - 2 < NCH:
                            v_evacs(p - 2)
                    for c in range(NCHA - 2, NCH):
                        v_evacs(c)

                @blk.tensor
                def _(t):
                    t.wait_ge(s_const.h, 48)
                    for wi in range(3):
                        t.wait_ge(s_bv.h, (wi + 1) * 16)
                        ins = None
                        for dc in range(4):
                            ins = t.matmul(
                                ps_b[:, wi * 4 + dc:wi * 4 + dc + 1],
                                sb_bv3[:, dc * 128:(dc + 1) * 128],
                                sb_ones_f[:, 0:1],
                                start=True, stop=True)
                        s_bprep.inc(ins)
                    t.wait_ge(s_id32.h, 32)

                    def mr_transp(c):
                        t.wait_ge(s_ln1.h, 2 * c + 2)
                        t.wait_ge(s_ln2.h, 2 * c + 2)
                        if c >= 1:
                            t.wait_ge(s_mre.h, c)
                        t.transpose(ps_mr[0:1, 0:128], mv[(2 * c) % 4][:, 0:1],
                                    sb_id32[:, :])
                        t.transpose(ps_mr[0:1, 128:256],
                                    mv[(2 * c + 1) % 4][:, 0:1], sb_id32[:, :])
                        t.transpose(ps_mr[0:1, 256:384], rstd[0][:, :],
                                    sb_id32[:, :])
                        ins = t.transpose(ps_mr[0:1, 384:512], rstd[1][:, :],
                                          sb_id32[:, :])
                        s_mrt.inc(ins)

                    def mr_bcast(c):
                        t.wait_ge(s_mre.h, c + 1)
                        if c >= 1:
                            t.wait_ge(s_bce.h, 2 * c)
                        t.matmul(ps_bc[:, 0:256], sb_onesrb[:, :],
                                 mr_sb[c % 2][0:1, 0:256],
                                 start=True, stop=True)
                        ins = t.matmul(ps_bc[:, 256:512], sb_onesrb[:, :],
                                       mr_sb[c % 2][0:1, 256:512],
                                       start=True, stop=True)
                        s_bcmm.inc(ins)

                    def qkv(c):
                        t.wait_ge(s_xn.h, 2 * (c + 1))
                        for j in range(OTC):
                            gt = c * OTC + j
                            b = gt % 4
                            if c == 0 and j in (0, 4, 8):
                                t.wait_ge(s_wt[j // 4].h, 32 * 16)
                            if gt >= 4:
                                pt_ = gt - 4
                                if evac_engine[pt_] == 'A':
                                    t.wait_ge(s_evq.h, int(actc[pt_]))
                                else:
                                    t.wait_ge(s_evv.h, int(dvec[pt_]))
                            if j < 8:
                                wsb = w_qT if j < 4 else w_kT
                                dc = j % 4
                                for f in range(NFC):
                                    ins = t.matmul(
                                        ps[b][:, 0:CHK],
                                        wsb[:, f * 512 + dc * 128:
                                            f * 512 + (dc + 1) * 128],
                                        xnb[c % 2][:, f * CHK:(f + 1) * CHK],
                                        start=(f == 0), stop=(f == NFC - 1))
                            else:
                                ts = j - 8
                                for f in range(NFC):
                                    ins = t.matmul(
                                        ps[b][:, :],
                                        xnb[c % 2][:, f * CHK + ts * 128:
                                                   f * CHK + (ts + 1) * 128],
                                        w_vT[:, f * 512:(f + 1) * 512],
                                        start=(f == 0), stop=(f == NFC - 1))
                            s_mm1.inc(ins)

                    for c in range(NCHA):
                        mr_transp(c)
                        if c >= 1 and c - 1 < NCH:
                            mr_bcast(c - 1)
                        if c >= 2 and c - 2 < NCH:
                            qkv(c - 2)
                    for c in range(NCHA - 2, NCH):
                        qkv(c)

                @blk.scalar
                def _(sc):
                    def qk_out(c):
                        sc.wait_ge(s_bcols.h, 1)
                        if c >= 1:
                            sc.wait_ge(s_qstk.h, 8 * c * 16)
                        for j in range(8):
                            gt = c * OTC + j
                            sc.wait_ge(s_mm1.h, gt + 1)
                            bias = sb_bqkv[:, j:j + 1]
                            n = s_evq.inc(sc.activation(
                                stg[j][:, 0:CHK], ps[gt % 4][:, 0:CHK],
                                AF.Identity, bias=bias))
                            sc.wait_ge(s_evq.h, n)
                            dst = (q_d if j < 4 else k_d)
                            dc = j % 4
                            s_qstk.inc(sc.dma_start(
                                out=dst[dc * 128:(dc + 1) * 128,
                                        c * CHK:(c + 1) * CHK],
                                in_=stg[j][:, 0:CHK]), 16)

                    def v_out(c):
                        for j in range(2):
                            gt = c * OTC + 8 + j
                            sc.wait_ge(s_evv.h, int(dvec[gt]))
                            s_qstv.inc(sc.dma_start(
                                out=v_d[c * CHK + j * 128:
                                        c * CHK + (j + 1) * 128, :],
                                in_=stgv[j][:, :]), 16)

                    sc.wait_ge(s_bprep.h, 3)
                    s_bcols.inc(sc.copy(sb_bqkv[:, :], ps_b[:, 0:12]))
                    for c in range(NCHA):
                        for i in (2 * c, 2 * c + 1):
                            sl = i % 2
                            sc.wait_ge(s_ln1.h, i + 1)
                            if i >= 4:
                                sc.wait_ge(s_mrt.h, i // 2)
                            s_ln2.inc(sc.activation(rstd[sl][:, :],
                                                    veps[sl][:, :], AF.Sqrt))
                        sc.wait_ge(s_mrt.h, c + 1)
                        if c < NCH:
                            n = s_mre.inc(sc.copy(mr_sb[c % 2][:, :],
                                                  ps_mr[0:1, :]))
                            if c >= 1:
                                sc.wait_ge(s_bcmm.h, c)
                                if c >= 3:
                                    sc.wait_ge(s_xn.h, 2 * (c - 2))
                                s_bce.inc(sc.copy(mbc[(c - 1) % 2][:, :],
                                                  ps_bc[:, 0:256]))
                                s_bce.inc(sc.copy(rbc1[(c - 1) % 2][:, :],
                                                  ps_bc[:, 256:512]))
                        else:
                            co = c - NCH
                            sc.wait_ge(s_bcmm.h, NCH)
                            if co == 0:
                                sc.wait_ge(s_xn.h, 2 * (c - 2))
                                s_bce.inc(sc.copy(mbc[(NCH - 1) % 2][:, :],
                                                  ps_bc[:, 0:256]))
                                s_bce.inc(sc.copy(rbc1[(NCH - 1) % 2][:, :],
                                                  ps_bc[:, 256:512]))
                            if co >= 2:
                                sc.wait_ge(s_mrst.h, co * 32)
                            n = s_mre.inc(sc.copy(mr_sb[c % 2][:, :],
                                                  ps_mr[0:1, :]))
                            sc.wait_ge(s_mre.h, n)
                            s_mrst.inc(sc.dma_start(
                                out=mrown_d[0:1, co * 256:(co + 1) * 256],
                                in_=mr_sb[c % 2][0:1, 0:256]), 16)
                            s_mrst.inc(sc.dma_start(
                                out=mrown_d[0:1, OWN + co * 256:
                                            OWN + (co + 1) * 256],
                                in_=mr_sb[c % 2][0:1, 256:512]), 16)
                        if c >= 3 and c - 3 < NCH:
                            v_out(c - 3)
                        if c >= 2 and c - 2 < NCH:
                            qk_out(c - 2)
                    for c in range(NCHA - 3, NCH):
                        v_out(c)
                    for c in range(NCHA - 2, NCH):
                        qk_out(c)
                    sc.wait_ge(s_qstk.h, s_qstk.n)
                    sc.wait_ge(s_qstv.h, s_qstv.n)
                    sc.wait_ge(s_mrst.h, s_mrst.n)

        s_qkvld = [sem("s_qkvld0"), sem("s_qkvld1")]
        s_mmA = sem("s_mmA")
        s_maxA = sem("s_maxA")
        s_cred = sem("s_cred")
        s_ctr = sem("s_ctr")
        s_cneg = sem("s_cneg")
        s_cbc = sem("s_cbc")
        s_cbias = sem("s_cbias")
        s_mmB = sem("s_mmB")
        s_expB = sem("s_expB")
        s_mskB = sem("s_mskB")
        s_pvmm = sem("s_pvmm")
        s_rcp = sem("s_rcp")
        s_rbcp = sem("s_rbcp")
        s_rbcs = sem("s_rbcs")
        s_mul = sem("s_mul")
        s_ustg = sem("s_ustg")
        s_uin = [sem("s_uin0"), sem("s_uin1")]
        s_a2a = sem("s_a2a")


        NIT = 8
        NQG = SEQ // 512
        KBS = SEQ // 128
        CPI = sum(4 * qg + 4 for qg in range(NQG))
        _chk_tot = []
        _acc = 0
        for _it in range(NIT):
            row = []
            for _qg in range(NQG):
                _acc += 4 * _qg + 4
                row.append(_acc)
            _chk_tot.append(row)

        _n_passA = sum(4 * (qg + 1) for qg in range(NQG))
        with ExitStack() as ph2:
            sb = lambda n, s, d: ph2.enter_context(nc.sbuf_tensor(n, s, d))
            qh = [sb(f"qh{i}", [128, SEQ], BF16) for i in range(2)]
            kh = [sb(f"kh{i}", [128, SEQ], BF16) for i in range(2)]
            vh = [sb(f"vh{i}", [128, SEQ], BF16) for i in range(2)]
            tmpmax = [sb(f"tmpmax{i}", [128, 160], F32) for i in range(2)]
            maxred = [sb(f"maxred{i}", [128, 1], F32) for i in range(2)]
            csc = [sb(f"csc{i}", [1, 2], F32) for i in range(2)]
            cb = [sb(f"cb{i}", [128, 1], F32) for i in range(2)]
            pbuf = [sb(f"pbuf{i}", [128, 512], BF16) for i in range(4)]
            rsb = [sb(f"rsb{i}", [1, 512], F32) for i in range(2)]
            rbc = [sb(f"rbc{i}", [128, 512], F32) for i in range(2)]
            umul = [sb(f"umul{i}", [128, 512], F32) for i in range(2)]
            ustg = [sb(f"ustg{i}", [128, 512], BF16) for i in range(2)]
            SA = ph2.enter_context(nc.psum_tensor("SA", [128, 1024], F32))
            SB = ph2.enter_context(nc.psum_tensor("SBp", [128, 1024], F32))
            PV = [ph2.enter_context(nc.psum_tensor(f"PV{i}", [128, 512], F32))
                  for i in range(2)]
            SUM = ph2.enter_context(nc.psum_tensor("SUMp", [1, 512], F32))
            RBC = ph2.enter_context(nc.psum_tensor("RBCp", [128, 512], F32))

            _msk_tot = {}
            _mcnt = 0
            _gc = 0
            for _it in range(NIT):
                for _qg in range(NQG):
                    for _r in range(4 * _qg + 4):
                        if _r >= 4 * _qg:
                            _j = _r - 4 * _qg
                            _mcnt += _j + 1
                            _msk_tot[_gc] = _mcnt
                        _gc += 1

            with nc.Block() as blk:
                @blk.sync
                def _(sy):
                    for it in range(NIT):
                        s, h = it // 4, it % 4
                        if it >= 2:
                            sy.wait_ge(s_pvmm.h, CPI * (it - 1))
                        sl = it % 2
                        s_qkvld[sl].inc(sy.dma_start(
                            out=qh[sl][:, :],
                            in_=q_d[h * 128:(h + 1) * 128,
                                    s * SEQ:(s + 1) * SEQ]), 16)
                        s_qkvld[sl].inc(sy.dma_start(
                            out=kh[sl][:, :],
                            in_=k_d[h * 128:(h + 1) * 128,
                                    s * SEQ:(s + 1) * SEQ]), 16)
                        for kb in range(KBS):
                            s_qkvld[sl].inc(sy.dma_start(
                                out=vh[sl][:, kb * 128:(kb + 1) * 128],
                                in_=v_d[s * SEQ + kb * 128:
                                        s * SEQ + (kb + 1) * 128,
                                        h * 128:(h + 1) * 128]), 16)

                @blk.tensor
                def _(t):
                    tA = 0
                    gc = 0
                    for it in range(NIT):
                        sl = it % 2
                        t.wait_ge(s_qkvld[sl].h, (it // 2 + 1) * (2 + KBS) * 16)
                        for qg in range(NQG):
                            for kb5 in range(qg + 1):
                                for qs in range(4):
                                    if tA >= 2:
                                        t.wait_ge(s_maxA.h, tA - 1)
                                    ins = t.matmul(
                                        SA[:, (tA % 2) * 512:(tA % 2 + 1) * 512],
                                        qh[sl][:, qg * 512 + qs * 128:
                                               qg * 512 + (qs + 1) * 128],
                                        kh[sl][:, kb5 * 512:(kb5 + 1) * 512],
                                        start=True, stop=True)
                                    s_mmA.inc(ins)
                                    tA += 1
                        t.wait_ge(s_id32.h, 32)
                        t.wait_ge(s_cred.h, it + 1)
                        s_ctr.inc(t.transpose(RBC[0:1, 0:128],
                                              maxred[sl][:, :], sb_id32[:, :]))
                        t.wait_ge(s_cneg.h, it + 1)
                        if it >= 1:
                            t.wait_ge(s_rbcs.h, NQG * it)
                        s_cbc.inc(t.matmul(RBC[:, 0:1], sb_ones_f[:, :],
                                           csc[sl][0:1, 1:2],
                                           start=True, stop=True))
                        for qg in range(NQG):
                            qgg = it * NQG + qg
                            nkb = 4 * qg + 4

                            def s_mm(r, gc_r):
                                if gc_r >= 2:
                                    t.wait_ge(s_expB.h, gc_r - 1)
                                ins = t.matmul(
                                    SB[:, (gc_r % 2) * 512:(gc_r % 2 + 1) * 512],
                                    kh[sl][:, r * 128:(r + 1) * 128],
                                    qh[sl][:, qg * 512:(qg + 1) * 512],
                                    start=True, stop=True)
                                s_mmB.inc(ins)

                            def pv_sum(r, gc_r):
                                if r >= 4 * qg:
                                    t.wait_ge(s_mskB.h, _msk_tot[gc_r])
                                else:
                                    t.wait_ge(s_expB.h, gc_r + 1)
                                if r == 0 and qgg >= 2:
                                    t.wait_ge(s_mul.h, qgg - 1)
                                t.matmul(PV[qg % 2][:, :],
                                         vh[sl][:, r * 128:(r + 1) * 128],
                                         pbuf[gc_r % 4][:, :],
                                         start=(r == 0), stop=(r == nkb - 1))
                                if r == 0 and qgg >= 1:
                                    t.wait_ge(s_rcp.h, qgg)
                                ins = t.matmul(SUM[0:1, :], sb_ones_b[:, :],
                                               pbuf[gc_r % 4][:, :],
                                               start=(r == 0), stop=(r == nkb - 1))
                                s_pvmm.inc(ins)

                            for r in range(nkb):
                                s_mm(r, gc + r)
                                if r >= 1:
                                    pv_sum(r - 1, gc + r - 1)
                            pv_sum(nkb - 1, gc + nkb - 1)
                            gc += nkb
                            t.wait_ge(s_rcp.h, qgg + 1)
                            t.wait_ge(s_cbias.h, it + 1)
                            if qgg >= 1:
                                t.wait_ge(s_rbcs.h, qgg)
                            s_rbcp.inc(t.matmul(RBC[:, :], sb_ones_f[:, :],
                                                rsb[qgg % 2][:, :],
                                                start=True, stop=True))

                @blk.vector
                def _(v):
                    tA = 0
                    for it in range(NIT):
                        sl = it % 2
                        for ti in range(_n_passA):
                            v.wait_ge(s_mmA.h, tA + 1)
                            s_maxA.inc(v.reduce_max(
                                tmpmax[sl][:, ti:ti + 1],
                                SA[:, (tA % 2) * 512:(tA % 2 + 1) * 512],
                                axis=AX.X))
                            tA += 1
                        v.drain()
                        s_cred.inc(v.reduce_max(maxred[sl][:, :],
                                                tmpmax[sl][:, 0:_n_passA],
                                                axis=AX.X))
                        v.drain()
                        v.wait_ge(s_ctr.h, it + 1)
                        v.reduce_max(csc[sl][0:1, 0:1], RBC[0:1, 0:128],
                                     axis=AX.X)
                        v.drain()
                        s_cneg.inc(v.tensor_scalar(
                            csc[sl][0:1, 1:2], csc[sl][0:1, 0:1],
                            -INV_SQRT_HD, None, ALU.mult))
                        v.drain()
                        for qg in range(NQG):
                            qgg = it * NQG + qg
                            v.wait_ge(s_pvmm.h, _chk_tot[it][qg])
                            s_rcp.inc(v.reciprocal(rsb[qgg % 2][:, :],
                                                   SUM[0:1, :]))
                            v.drain()
                            v.wait_ge(s_rbcs.h, qgg + 1)
                            if qgg >= 2:
                                v.wait_ge(s_ustg.h, qgg - 1)
                            s_mul.inc(v.tensor_tensor(
                                umul[qgg % 2][:, :], PV[qg % 2][:, :],
                                rbc[qgg % 2][:, :], op=ALU.mult))

                @blk.scalar
                def _(sc):
                    gc = 0
                    for it in range(NIT):
                        sl = it % 2
                        s, h = it // 4, it % 4
                        sc.wait_ge(s_cbc.h, it + 1)
                        n = s_cbias.inc(sc.copy(cb[sl][:, :], RBC[:, 0:1]))
                        sc.wait_ge(s_cbias.h, n)
                        for qg in range(NQG):
                            qgg = it * NQG + qg
                            nkb = 4 * qg + 4
                            for r in range(nkb):
                                sc.wait_ge(s_mmB.h, gc + 1)
                                if gc >= 4:
                                    sc.wait_ge(s_pvmm.h, gc - 3)
                                s_expB.inc(sc.activation(
                                    pbuf[gc % 4][:, :],
                                    SB[:, (gc % 2) * 512:(gc % 2 + 1) * 512],
                                    AF.Exp, bias=cb[sl][:, :],
                                    scale=INV_SQRT_HD))
                                gc += 1
                            sc.wait_ge(s_rbcp.h, qgg + 1)
                            s_rbcs.inc(sc.copy(rbc[qgg % 2][:, :], RBC[:, :]))
                            sc.wait_ge(s_mul.h, qgg + 1)
                            if qgg >= 2:
                                sc.wait_ge(s_uin[qgg % 2].h,
                                           (qgg // 2) * 16)
                            n = s_ustg.inc(sc.activation(
                                ustg[qgg % 2][:, :], umul[qgg % 2][:, :],
                                AF.Identity, bias=sb_bqkv[:, 8 + h:9 + h]))
                            sc.wait_ge(s_ustg.h, n)
                            tok0 = s * SEQ + qg * 512
                            jj = tok0 // OWN
                            off = tok0 % OWN
                            s_uin[qgg % 2].inc(sc.dma_start(
                                out=u_in[jj, h * 128:(h + 1) * 128,
                                         off:off + 512],
                                in_=ustg[qgg % 2][:, :]), 16)
                    sc.wait_ge(s_uin[0].h, s_uin[0].n)
                    sc.wait_ge(s_uin[1].h, s_uin[1].n)

                @blk.gpsimd
                def _(g):
                    gc = 0
                    mcnt = 0
                    for it in range(NIT):
                        for qg in range(NQG):
                            for r in range(4 * qg + 4):
                                if r >= 4 * qg:
                                    j = r - 4 * qg
                                    g.wait_ge(s_expB.h, gc + 1)
                                    for i in range(j):
                                        s_mskB.inc(g.memset(
                                            pbuf[gc % 4][:, i * 128:(i + 1) * 128],
                                            0.0))
                                        mcnt += 1
                                    s_mskB.inc(g.tensor_tensor(
                                        pbuf[gc % 4][:, j * 128:(j + 1) * 128],
                                        pbuf[gc % 4][:, j * 128:(j + 1) * 128],
                                        sb_tri[:, :], op=ALU.mult))
                                    mcnt += 1
                                    assert _msk_tot[gc] == mcnt
                                gc += 1
                    g.wait_ge(s_uin[0].h, s_uin[0].n)
                    g.wait_ge(s_uin[1].h, s_uin[1].n)
                    with allow_ici_all_to_all():
                        s_a2a.inc(g.collective_compute(
                            "AllToAll", ALU.bypass,
                            replica_groups=[[0, 1, 2, 3, 4, 5, 6, 7]],
                            ins=[u_in.ap().opt()],
                            outs=[u_out.ap().opt()]))

        NTH = OWN // 512
        NIC = INTER // 128
        NRC = RANK // 128
        s3_b1l = [sem("s3_b1l0"), sem("s3_b1l1")]
        s3_b1m = sem("s3_b1m")
        s3_b1e = sem("s3_b1e")
        s3_xno = sem("s3_xno")
        s3_xfo = [sem("s3_xfo0"), sem("s3_xfo1")]
        s3_mr = sem("s3_mr")
        s3_mrb = sem("s3_mrb")
        s3_mrbe = sem("s3_mrbe")
        s3_u1 = [sem("s3_u1a"), sem("s3_u1b")]
        s3_v1 = [sem(f"s3_v1{i}") for i in range(4)]
        s3_u2 = [sem("s3_u2a"), sem("s3_u2b")]
        s3_v2 = [sem("s3_v2a"), sem("s3_v2b")]
        s3_mm = sem("s3_mm")
        s3_evd = sem("s3_evd")
        s3_eva = sem("s3_eva")
        s3_glu = sem("s3_glu")
        s3_fst = [sem(f"s3_fst{i}") for i in range(4)]

        _b1loads = []
        _o = 0
        while _o < 2 * INTER:
            _w = min(2048, 2 * INTER - _o)
            _b1loads.append((_o, _w))
            _o += _w

        _cnt = 0
        _p_last = []
        for _th in range(NTH):
            for _f in range(NFC):
                _cnt += 1
        _cnt = NTH * NFC
        _hgrp_flat = []
        _h_mm = []
        _u2grp = []
        _z_mm = []
        _f_mm = []
        for _th in range(NTH):
            for _ig in range(NIC):
                _cnt += 1
                _g = _cnt
                _cnt += 1
                _u = _cnt
                _h_mm.append((_g, _u))
                _hgrp_flat += [_g, _u]
            for _i in range(NIC):
                _cnt += 1
                _u2grp.append(_cnt)
            _z_mm.append(_cnt)
            _row = []
            for _j in range(8):
                for _ts in range(4):
                    _cnt += 1
                    _row.append(_cnt)
            _f_mm.append(_row)
        _v1gate = {g: _hgrp_flat[g - 4] for g in range(4, 2 * NTH * NIC)}
        _u2gate = {g: _u2grp[g - 2] for g in range(2, NTH * NIC)}
        _v2gate = {g: _f_mm[(g - 2) // 8][((g - 2) % 8) * 4 + 3]
                   for g in range(2, NTH * 8)}
        _fevd = {}
        for _m in range(32 * NTH):
            _tm = _m // 32
            _fevd[_m] = 8 * NTH + 40 * _tm + 8 + (_m - 32 * _tm) + 1

        with ExitStack() as ph3:
            sb = lambda n, s, d: ph3.enter_context(nc.sbuf_tensor(n, s, d))
            pp = [ph3.enter_context(nc.psum_tensor(f"pp{i}", [128, 512], F32))
                  for i in range(8)]
            b1row = [sb(f"b1row{i}", [1, 2048], F32) for i in range(2)]
            b1c = sb("b1c", [128, 2 * NIC], F32)
            xnoT = [sb(f"xnoT{i}", [128, 512], BF16) for i in range(2)]
            xfo = [sb(f"xfo{i}", [128, 512], F32) for i in range(2)]
            mr3row = sb("mr3row", [1, 2 * OWN], BF16)
            mbc3 = sb("mbc3", [128, 512], BF16)
            rbc3 = sb("rbc3", [128, 512], BF16)
            u1sb = [sb(f"u1sb{i}", [128, RANK], BF16) for i in range(2)]
            v1sb = [sb(f"v1sb{i}", [128, RANK], BF16) for i in range(4)]
            u2sb = [sb(f"u2sb{i}", [128, RANK], BF16) for i in range(2)]
            v2sb = [sb(f"v2sb{i}", [128, RANK * 4], BF16) for i in range(2)]
            g_sb = [sb(f"g_sb{i}", [128, 512], F32) for i in range(2)]
            up_sb = [sb(f"up_sb{i}", [128, 512], F32) for i in range(2)]
            P_sb = sb("P_sb", [128, NRC * OWN], BF16)
            act_sb = sb("act_sb", [128, NIC * 512], BF16)
            Z_sb = sb("Z_sb", [128, NRC * 512], BF16)
            ffn_stg = [sb(f"ffn_stg{i}", [128, 512], BF16) for i in range(4)]

            with nc.Block() as blk:
                @blk.gpsimd
                def _(g):
                    for th in range(NTH):
                        g.wait_ge(s3_mrbe.h, 2 * (th + 1))
                        for f in range(NFC):
                            gi = th * NFC + f
                            g.wait_ge(s3_xfo[gi % 2].h, (gi // 2 + 1) * 16)
                            if gi >= 2:
                                g.wait_ge(s3_mm.h, gi - 1)
                            s3_u1[gi % 2].inc(g.dma_start(
                                out=u1sb[gi % 2][:, :],
                                in_=u1[f * 128:(f + 1) * 128, :]), 16)
                            g.tensor_tensor(xfo[gi % 2][:, :],
                                            xfo[gi % 2][:, :],
                                            mbc3[:, :], op=ALU.subtract)
                            g.drain()
                            s3_xno.inc(g.tensor_tensor(
                                xnoT[gi % 2][:, :], xfo[gi % 2][:, :],
                                rbc3[:, :], op=ALU.mult))
                    for th in range(NTH):
                        for i in range(NIC):
                            gi = th * NIC + i
                            if gi >= 2:
                                g.wait_ge(s3_mm.h, _u2gate[gi])
                            s3_u2[gi % 2].inc(g.dma_start(
                                out=u2sb[gi % 2][:, :],
                                in_=u2[i * 128:(i + 1) * 128, :]), 16)
                        for j in range(8):
                            gi = th * 8 + j
                            if gi >= 2:
                                g.wait_ge(s3_mm.h, _v2gate[gi])
                            for rc in range(NRC):
                                s3_v2[gi % 2].inc(g.dma_start(
                                    out=v2sb[gi % 2][:, rc * 512:(rc + 1) * 512],
                                    in_=v2[rc * 128:(rc + 1) * 128,
                                           j * 512:(j + 1) * 512]), 16)

                @blk.sync
                def _(sy):
                    s3_mr.inc(sy.dma_start(out=mr3row[:, :],
                                           in_=mrown_d[:, :]), 16)
                    nch_done = 0
                    for li, (off, wid) in enumerate(_b1loads):
                        if li >= 2:
                            sy.wait_ge(s3_b1m.h, nch_done)
                        s3_b1l[li % 2].inc(sy.dma_start(
                            out=b1row[li % 2][:, 0:wid],
                            in_=b1[:, off:off + wid]), 16)
                        if li >= 1:
                            nch_done += _b1loads[li - 1][1] // 128
                    for th in range(NTH):
                        for f in range(NFC):
                            gi = th * NFC + f
                            if gi >= 2:
                                sy.wait_ge(s3_xno.h, gi - 1)
                                sy.wait_ge(s3_mm.h, gi - 1)
                            s3_xfo[gi % 2].inc(sy.dma_start(
                                out=xfo[gi % 2][:, :],
                                in_=xf_own[f * 128:(f + 1) * 128,
                                           th * 512:(th + 1) * 512]), 16)
                    sy.wait_ge(s_bgv.h, s_bgv.n)
                    for th in range(NTH):
                        for ig in range(NIC):
                            for gu in range(2):
                                gidx = (th * NIC + ig) * 2 + gu
                                if gidx >= 4:
                                    sy.wait_ge(s3_mm.h, _v1gate[gidx])
                                pidx = ig * 2 + gu
                                s3_v1[gidx % 4].inc(sy.dma_start(
                                    out=v1sb[gidx % 4][:, :],
                                    in_=v1_p[pidx, :, :]), 16)

                @blk.tensor
                def _(t):
                    mmn = 0

                    def bcast(th):
                        if th >= 1:
                            t.wait_ge(s3_mrbe.h, 2 * th)
                            t.wait_ge(s3_xno.h, th * NFC)
                        t.matmul(pp[6][:, :], sb_onesrb[:, :],
                                 mr3row[0:1, th * 512:(th + 1) * 512],
                                 start=True, stop=True)
                        ins = t.matmul(pp[7][:, :], sb_onesrb[:, :],
                                       mr3row[0:1, OWN + th * 512:
                                              OWN + (th + 1) * 512],
                                       start=True, stop=True)
                        s3_mrb.inc(ins)

                    t.wait_ge(s3_mr.h, 16)
                    bcast(0)
                    nch = 0
                    for li, (off, wid) in enumerate(_b1loads):
                        t.wait_ge(s3_b1l[li % 2].h, (li // 2 + 1) * 16)
                        ins = None
                        for cc in range(wid // 128):
                            ins = t.matmul(
                                pp[0][:, nch:nch + 1],
                                b1row[li % 2][0:1, cc * 128:(cc + 1) * 128],
                                sb_ones_f[:, 0:1], start=True, stop=True)
                            nch += 1
                        s3_b1m.inc(ins, wid // 128)
                    t.wait_ge(s3_b1e.h, 1)
                    for th in range(NTH):
                        if th >= 1:
                            t.wait_ge(s3_evd.h, 8 * th)
                            bcast(th)
                        for f in range(NFC):
                            gi = th * NFC + f
                            t.wait_ge(s3_xno.h, gi + 1)
                            t.wait_ge(s3_u1[gi % 2].h, (gi // 2 + 1) * 16)
                            ins = None
                            for rc in range(NRC):
                                ins = t.matmul(
                                    pp[rc][:, :],
                                    u1sb[gi % 2][:, rc * 128:(rc + 1) * 128],
                                    xnoT[gi % 2][:, :],
                                    start=(f == 0), stop=(f == NFC - 1))
                            mmn = s3_mm.inc(ins)
                    _Pev = 8 * NTH
                    for th in range(NTH):
                        for ig in range(NIC):
                            gidx2 = (th * NIC + ig) * 2
                            for gu in range(2):
                                gidx = gidx2 + gu
                                slot = gidx % 4
                                bank = pp[2 * (ig % 2) + gu]
                                t.wait_ge(s3_v1[gidx % 4].h,
                                          (gidx // 4 + 1) * 16)
                                if ig < 2:
                                    t.wait_ge(s3_evd.h, 8 * NTH + 40 * th)
                                if gidx >= 4:
                                    t.wait_ge(s3_eva.h, gidx - 3)
                                ins = None
                                for rc in range(NRC):
                                    ins = t.matmul(
                                        bank[:, :],
                                        v1sb[slot][:, rc * 128:(rc + 1) * 128],
                                        P_sb[:, rc * OWN + th * 512:
                                             rc * OWN + (th + 1) * 512],
                                        start=(rc == 0), stop=(rc == NRC - 1))
                                mmn = s3_mm.inc(ins)
                        for i in range(NIC):
                            gi = th * NIC + i
                            t.wait_ge(s3_u2[gi % 2].h, (gi // 2 + 1) * 16)
                            t.wait_ge(s3_glu.h, th * NIC + i + 1)
                            if i == 0:
                                t.wait_ge(s3_eva.h, 2 * NIC * (th + 1))
                            ins = None
                            for rc in range(NRC):
                                ins = t.matmul(
                                    pp[rc][:, :],
                                    u2sb[gi % 2][:, rc * 128:(rc + 1) * 128],
                                    act_sb[:, i * 512:(i + 1) * 512],
                                    start=(i == 0), stop=(i == NIC - 1))
                            mmn = s3_mm.inc(ins)
                        t.wait_ge(s3_evd.h, 8 * NTH + 40 * th + 8)
                        for j in range(8):
                            gi = th * 8 + j
                            t.wait_ge(s3_v2[gi % 2].h, (gi // 2 + 1) * 8 * 16)
                            for ts in range(4):
                                m = gi * 4 + ts
                                if m >= 4:
                                    t.wait_ge(s3_evd.h, _fevd[m - 4])
                                ins = None
                                for rc in range(NRC):
                                    ins = t.matmul(
                                        pp[ts][:, :],
                                        Z_sb[:, rc * 512 + ts * 128:
                                             rc * 512 + (ts + 1) * 128],
                                        v2sb[gi % 2][:, rc * 512:(rc + 1) * 512],
                                        start=(rc == 0), stop=(rc == NRC - 1))
                                mmn = s3_mm.inc(ins)

                @blk.scalar
                def _(sc):
                    sc.wait_ge(s3_mrb.h, 1)
                    s3_mrbe.inc(sc.copy(mbc3[:, :], pp[6][:, :]))
                    s3_mrbe.inc(sc.copy(rbc3[:, :], pp[7][:, :]))
                    sc.wait_ge(s3_b1m.h, 2 * NIC)
                    s3_b1e.inc(sc.copy(b1c[:, :], pp[0][:, 0:2 * NIC]))
                    for th in range(1, NTH):
                        sc.wait_ge(s3_mrb.h, th + 1)
                        s3_mrbe.inc(sc.copy(mbc3[:, :], pp[6][:, :]))
                        s3_mrbe.inc(sc.copy(rbc3[:, :], pp[7][:, :]))
                    for th in range(NTH):
                        for ig in range(NIC):
                            sc.wait_ge(s3_mm.h, _h_mm[th * NIC + ig][0])
                            if th * NIC + ig >= 2:
                                sc.wait_ge(s3_glu.h, th * NIC + ig - 1)
                            s3_eva.inc(sc.activation(
                                g_sb[ig % 2][:, :], pp[2 * (ig % 2)][:, :],
                                AF.Silu, bias=b1c[:, ig:ig + 1]))
                            sc.wait_ge(s3_mm.h, _h_mm[th * NIC + ig][1])
                            s3_eva.inc(sc.activation(
                                up_sb[ig % 2][:, :], pp[2 * (ig % 2) + 1][:, :],
                                AF.Identity, bias=b1c[:, NIC + ig:NIC + ig + 1]))
                        for j in range(8):
                            for ts in range(4):
                                m = (th * 8 + j) * 4 + ts
                                sc.wait_ge(s3_evd.h, _fevd[m])
                                s3_fst[m % 4].inc(sc.dma_start(
                                    out=ffn_d[th * 512 + ts * 128:
                                              th * 512 + (ts + 1) * 128,
                                              j * 512:(j + 1) * 512],
                                    in_=ffn_stg[m % 4][:, :]), 16)
                    for i in range(4):
                        sc.wait_ge(s3_fst[i].h, s3_fst[i].n)

                @blk.vector
                def _(v):
                    for th in range(NTH):
                        v.wait_ge(s3_mm.h, NFC * (th + 1))
                        for rc in range(NRC):
                            s3_evd.inc(v.tensor_copy(
                                P_sb[:, rc * OWN + th * 512:
                                     rc * OWN + (th + 1) * 512],
                                pp[rc][:, :]))
                    for th in range(NTH):
                        for ig in range(NIC):
                            gidx2 = (th * NIC + ig) * 2
                            v.wait_ge(s3_eva.h, gidx2 + 2)
                            s3_glu.inc(v.tensor_tensor(
                                act_sb[:, ig * 512:(ig + 1) * 512],
                                g_sb[ig % 2][:, :], up_sb[ig % 2][:, :],
                                op=ALU.mult))
                        v.wait_ge(s3_mm.h, _z_mm[th])
                        for rc in range(NRC):
                            s3_evd.inc(v.tensor_copy(
                                Z_sb[:, rc * 512:(rc + 1) * 512], pp[rc][:, :]))
                        for j in range(8):
                            for ts in range(4):
                                m = (th * 8 + j) * 4 + ts
                                v.wait_ge(s3_mm.h, _f_mm[th][j * 4 + ts])
                                if m >= 4:
                                    v.wait_ge(s3_fst[m % 4].h,
                                              (m // 4) * 16)
                                s3_evd.inc(v.tensor_copy(
                                    ffn_stg[m % 4][:, :], pp[ts][:, :]))

        s4_u = sem("s4_u")
        s4_wo = [sem(f"s4_wo{i}") for i in range(3)]
        s4_bb = sem("s4_bb")
        s4_bbv = sem("s4_bbv")
        s4_bbm = sem("s4_bbm")
        s4_bbe = sem("s4_bbe")
        s4_mm = sem("s4_mm")
        s4_add = sem("s4_add")
        s4_ffn = [sem("s4_ffn0"), sem("s4_ffn1")]
        s4_x = [sem("s4_x0"), sem("s4_x1")]
        s4_out = [sem("s4_out0"), sem("s4_out1")]
        NTS = OWN // 128

        with ExitStack() as ph4:
            sb = lambda n, s, d: ph4.enter_context(nc.sbuf_tensor(n, s, d))
            pq = [ph4.enter_context(nc.psum_tensor(f"pq{i}", [128, 512], F32))
                  for i in range(5)]
            u_sb = sb("u_sb", [128, 32 * OWN], BF16)
            wo_sb = [sb(f"wo_sb{i}", [128, 32 * 512], BF16) for i in range(3)]
            bbbc = sb("bbbc", [128, D], F32)
            b2_r = sb("b2_r", [1, D], F32)
            t1b = [sb(f"t1b{i}", [128, 512], F32) for i in range(2)]
            toutb = [sb(f"toutb{i}", [128, 512], F32) for i in range(2)]
            ffn_rd = [sb(f"ffn_rd{i}", [128, 512], BF16) for i in range(2)]
            x_rd = [sb(f"x_rd{i}", [128, 512], F32) for i in range(2)]

            with nc.Block() as blk:
                @blk.sync
                def _(sy):
                    s4_bb.inc(sy.dma_start(out=bbbc[0:1, :], in_=bo[:, :]), 16)
                    s4_bb.inc(sy.dma_start(out=b2_r[:, :], in_=b2[:, :]), 16)
                    sy.wait_ge(s_a2a.h, 1)
                    for dc in range(32):
                        src = u_out[dc // 4, (dc % 4) * 128:(dc % 4 + 1) * 128, :]
                        s4_u.inc(sy.dma_start(
                            out=u_sb[:, dc * OWN:(dc + 1) * OWN], in_=src), 16)
                    for j in range(8):
                        for ts in range(NTS):
                            m = j * NTS + ts
                            if m >= 2:
                                sy.wait_ge(s4_add.h, m - 1)
                            s4_x[m % 2].inc(sy.dma_start(
                                out=x_rd[m % 2][:, :],
                                in_=x_own[ts * 128:(ts + 1) * 128,
                                          j * 512:(j + 1) * 512]), 16)
                            s4_ffn[m % 2].inc(sy.dma_start(
                                out=ffn_rd[m % 2][:, :],
                                in_=ffn_d[ts * 128:(ts + 1) * 128,
                                          j * 512:(j + 1) * 512]), 16)

                @blk.gpsimd
                def _(g):
                    for j in range(8):
                        if j >= 3:
                            g.wait_ge(s4_mm.h, NTS * (j - 2))
                        for dc in range(32):
                            s4_wo[j % 3].inc(g.dma_start(
                                out=wo_sb[j % 3][:, dc * 512:(dc + 1) * 512],
                                in_=wo_t[dc * 128:(dc + 1) * 128,
                                         j * 512:(j + 1) * 512]), 16)

                @blk.vector
                def _(v):
                    for j in range(8):
                        for ts in range(NTS):
                            m = j * NTS + ts
                            v.wait_ge(s4_mm.h, m + 1)
                            v.wait_ge(s4_bbe.h, 8)
                            v.tensor_tensor(t1b[m % 2][:, :], pq[ts % 4][:, :],
                                            bbbc[:, j * 512:(j + 1) * 512],
                                            op=ALU.add)
                            v.drain()
                            v.wait_ge(s4_ffn[m % 2].h, (m // 2 + 1) * 16)
                            v.tensor_tensor(t1b[m % 2][:, :], t1b[m % 2][:, :],
                                            ffn_rd[m % 2][:, :], op=ALU.add)
                            v.drain()
                            v.wait_ge(s4_x[m % 2].h, (m // 2 + 1) * 16)
                            if m >= 2:
                                v.wait_ge(s4_out[m % 2].h, (m // 2) * 16)
                            s4_add.inc(v.tensor_tensor(
                                toutb[m % 2][:, :], t1b[m % 2][:, :],
                                x_rd[m % 2][:, :], op=ALU.add))
                            v.drain()

                @blk.tensor
                def _(t):
                    t.wait_ge(s4_bb.h, 32)
                    for j in range(8):
                        t.matmul(pq[4][:, :], sb_ones_f[:, :],
                                 bbbc[0:1, j * 512:(j + 1) * 512],
                                 start=True, stop=False)
                        ins = t.matmul(pq[4][:, :], sb_ones_f[:, :],
                                       b2_r[0:1, j * 512:(j + 1) * 512],
                                       start=False, stop=True)
                        s4_bbm.inc(ins)
                        t.wait_ge(s4_bbe.h, j + 1)
                    t.wait_ge(s4_u.h, 32 * 16)
                    for j in range(8):
                        t.wait_ge(s4_wo[j % 3].h, (j // 3 + 1) * 32 * 16)
                        for ts in range(NTS):
                            m = j * NTS + ts
                            if m >= 4:
                                t.wait_ge(s4_add.h, m - 3)
                            ins = None
                            for dc in range(32):
                                ins = t.matmul(
                                    pq[ts % 4][:, :],
                                    u_sb[:, dc * OWN + ts * 128:
                                         dc * OWN + (ts + 1) * 128],
                                    wo_sb[j % 3][:, dc * 512:(dc + 1) * 512],
                                    start=(dc == 0), stop=(dc == 31))
                            s4_mm.inc(ins)

                @blk.scalar
                def _(sc):
                    for j in range(8):
                        sc.wait_ge(s4_bbm.h, j + 1)
                        s4_bbe.inc(sc.copy(bbbc[:, j * 512:(j + 1) * 512],
                                           pq[4][:, :]))
                    for j in range(8):
                        for ts in range(NTS):
                            m = j * NTS + ts
                            sc.wait_ge(s4_add.h, m + 1)
                            s4_out[m % 2].inc(sc.dma_start(
                                out=out_own[ts * 128:(ts + 1) * 128,
                                            j * 512:(j + 1) * 512],
                                in_=toutb[m % 2][:, :]), 16)
                    sc.wait_ge(s4_out[0].h, s4_out[0].n)
                    sc.wait_ge(s4_out[1].h, s4_out[1].n)

    return nc


def _shard_inputs(inputs):
    f = lambda a: np.ascontiguousarray(np.asarray(a, dtype=np.float32))
    x = f(inputs["L0_attn_x"]).reshape(T, D)
    Wq, Wk, Wv = f(inputs["L0_attn_Wq"]), f(inputs["L0_attn_Wk"]), f(inputs["L0_attn_Wv"])
    bq, bk, bv = f(inputs["L0_attn_bq"]), f(inputs["L0_attn_bk"]), f(inputs["L0_attn_bv"])
    Wo, bo = f(inputs["L0_attn_Wo"]), f(inputs["L0_attn_bo"])
    U1, V1 = f(inputs["L0_ffn_U1"]), f(inputs["L0_ffn_V1"])
    U2, V2 = f(inputs["L0_ffn_U2"]), f(inputs["L0_ffn_V2"])
    b1, b2 = f(inputs["L0_ffn_b1"]), f(inputs["L0_ffn_b2"])
    x_f = np.ascontiguousarray(x.T)
    x_q = np.ascontiguousarray(x_f.reshape(D, NCH, CHK).transpose(1, 0, 2))
    Wo_t = np.ascontiguousarray(Wo.T)
    in_maps = []
    for c in range(8):
        sl = slice(c * HSL, (c + 1) * HSL)
        in_maps.append({
            "x": x,
            "x_own": np.ascontiguousarray(x[c * OWN:(c + 1) * OWN]),
            "x_q": x_q,
            "xf_own": np.ascontiguousarray(x_f[:, c * OWN:(c + 1) * OWN]),
            "wq_t": np.ascontiguousarray(Wq[sl].T),
            "wk_t": np.ascontiguousarray(Wk[sl].T),
            "wv_t": np.ascontiguousarray(Wv[sl].T),
            "bq": np.ascontiguousarray(bq[sl]).reshape(1, HSL),
            "bk": np.ascontiguousarray(bk[sl]).reshape(1, HSL),
            "bv": np.ascontiguousarray(bv[sl]).reshape(1, HSL),
            "wo_t": Wo_t,
            "bo": bo.reshape(1, D),
            "u1": U1, "v1": V1, "u2": U2, "v2": V2,
            "b1": b1.reshape(1, 2 * INTER), "b2": b2.reshape(1, D),
        })
    return in_maps


def kernel(**inputs):
    nc = build()
    in_maps = _shard_inputs(inputs)
    res = run_bass_kernel_spmd(nc, in_maps, core_ids=list(range(8)))
    out = np.concatenate(
        [np.asarray(res.results[c]["out_own"], dtype=np.float32)
         for c in range(8)], axis=0)
    return np.ascontiguousarray(out.reshape(2, SEQ, D))


# version 11
# speedup vs baseline: 1.1334x; 1.0295x over previous
import numpy as np
import ml_dtypes
from contextlib import ExitStack

import concourse.bass as bass
import concourse.mybir as mybir
from concourse.bass_utils import run_bass_kernel_spmd
from concourse.aot_env import allow_ici_all_to_all

F32 = mybir.dt.float32
BF16 = mybir.dt.bfloat16
AF = mybir.ActivationFunctionType
ALU = mybir.AluOpType
AX = mybir.AxisListType

D = 4096
T = 8192
SEQ = 4096
HSL = 512
NHL = 4
HD = 128
OWN = 1024
RANK = 1024
INTER = 11008
NFC = D // 128
CHK = 256
NCH = T // CHK
TPC = CHK // 128
OTC = 10
NLT = T // 128
NTILE = NLT + 8
INV_SQRT_HD = float(1.0 / np.sqrt(128.0))
EPS = 1e-5


class SemC:

    def __init__(self, handle):
        self.h = handle
        self.n = 0

    def inc(self, instr, k=1):
        instr.then_inc(self.h, k)
        self.n += k
        return self.n


def build(dump=()):
    nc = bass.Bass("TRN2", target_bir_lowering=False)

    x = nc.dram_tensor("x", [T, D], F32, kind="ExternalInput")
    x_own = nc.dram_tensor("x_own", [OWN, D], F32, kind="ExternalInput")
    x_q = nc.dram_tensor("x_q", [NCH, D, CHK], F32, kind="ExternalInput")
    xf_own = nc.dram_tensor("xf_own", [D, OWN], F32, kind="ExternalInput")
    wq_t = nc.dram_tensor("wq_t", [D, HSL], F32, kind="ExternalInput")
    wk_t = nc.dram_tensor("wk_t", [D, HSL], F32, kind="ExternalInput")
    wv_t = nc.dram_tensor("wv_t", [D, HSL], F32, kind="ExternalInput")
    bq = nc.dram_tensor("bq", [1, HSL], F32, kind="ExternalInput")
    bk = nc.dram_tensor("bk", [1, HSL], F32, kind="ExternalInput")
    bv = nc.dram_tensor("bv", [1, HSL], F32, kind="ExternalInput")
    wo_t = nc.dram_tensor("wo_t", [D, D], F32, kind="ExternalInput")
    bo = nc.dram_tensor("bo", [1, D], F32, kind="ExternalInput")
    u1 = nc.dram_tensor("u1", [D, RANK], F32, kind="ExternalInput")
    v1 = nc.dram_tensor("v1", [RANK, 2 * INTER], F32, kind="ExternalInput")
    u2 = nc.dram_tensor("u2", [INTER, RANK], F32, kind="ExternalInput")
    v2 = nc.dram_tensor("v2", [RANK, D], F32, kind="ExternalInput")
    b1 = nc.dram_tensor("b1", [1, 2 * INTER], F32, kind="ExternalInput")
    b2 = nc.dram_tensor("b2", [1, D], F32, kind="ExternalInput")
    out_own = nc.dram_tensor("out_own", [OWN, D], F32, kind="ExternalOutput")

    def scratch(name, shape, dtype):
        kind = "ExternalOutput" if name in dump else "Internal"
        return nc.dram_tensor(name, shape, dtype, kind=kind)

    q_d = scratch("q_d", [HSL, T], BF16)
    k_d = scratch("k_d", [HSL, T], BF16)
    v_d = scratch("v_d", [T, HSL], BF16)
    mrown_d = scratch("mrown_d", [1, 2 * OWN], BF16)
    v1_p = scratch("v1_p", [2 * (INTER // 128), 128, RANK], BF16)
    u_in = scratch("u_in", [8, HSL, OWN], BF16)
    u_out = scratch("u_out", [8, HSL, OWN], BF16)
    ffn_d = scratch("ffn_d", [OWN, D], BF16)

    ones_bcol = nc.inline_tensor(np.ones((128, 1), dtype=ml_dtypes.bfloat16),
                                 "ones_bcol")
    ones_frow = nc.inline_tensor(np.ones((1, 128), dtype=np.float32),
                                 "ones_frow")
    ones_frb = nc.inline_tensor(np.ones((1, 128), dtype=ml_dtypes.bfloat16),
                                "ones_frb")
    tri = np.triu(np.ones((128, 128), np.float32)).astype(ml_dtypes.bfloat16)
    tri_t = nc.inline_tensor(tri, "tri_t")
    ident_f32 = nc.inline_tensor(np.eye(128, dtype=np.float32), "ident_f32")

    with ExitStack() as top:
        sem = lambda name: SemC(top.enter_context(nc.semaphore(name)))

        sb_ones_b = nc.alloc_sbuf_tensor("sb_ones_b", [128, 1], BF16)
        sb_ones_f = nc.alloc_sbuf_tensor("sb_ones_f", [1, 128], F32)
        sb_tri = nc.alloc_sbuf_tensor("sb_tri", [128, 128], BF16)
        sb_bqkv = nc.alloc_sbuf_tensor("sb_bqkv", [128, 12], F32)
        sb_id32 = nc.alloc_sbuf_tensor("sb_id32", [128, 128], F32)
        sb_onesrb = nc.alloc_sbuf_tensor("sb_onesrb", [1, 128], BF16)

        s_const = sem("s_const")
        s_id32 = sem("s_id32")

        s_wt = [sem("s_wtq"), sem("s_wtk"), sem("s_wtv")]
        s_bgv = sem("s_bgv")
        s_bv = sem("s_bv")
        s_bprep = sem("s_bprep")
        s_bcols = sem("s_bcols")
        s_ldx = [sem(f"s_ldx{i}") for i in range(4)]
        s_st = sem("s_st")
        s_ln1 = sem("s_ln1")
        s_ln2 = sem("s_ln2")
        s_mrt = sem("s_mrt")
        s_mre = sem("s_mre")
        s_bcmm = sem("s_bcmm")
        s_bce = sem("s_bce")
        s_xf = [sem(f"s_xf{i}") for i in range(3)]
        s_mrst = sem("s_mrst")
        s_xn = sem("s_xn")
        s_mm1 = sem("s_mm1")
        s_evq = sem("s_evq")
        s_evv = sem("s_evv")
        s_qstk = sem("s_qstk")
        s_qstv = sem("s_qstv")

        NCHA = NCH + (OWN // CHK)

        with ExitStack() as ph1:
            sb = lambda n, s, d: ph1.enter_context(nc.sbuf_tensor(n, s, d))
            pt = lambda n, s: ph1.enter_context(nc.psum_tensor(n, s, F32))
            w_qT = sb("w_qT", [128, NFC * 512], BF16)
            w_kT = sb("w_kT", [128, NFC * 512], BF16)
            w_vT = sb("w_vT", [128, NFC * 512], BF16)
            xfb = [sb(f"xfb{i}", [128, NFC * 128], F32) for i in range(3)]
            xnb = [sb(f"xnb{i}", [128, NFC * CHK], BF16) for i in range(2)]
            xb = [sb(f"xb{i}", [128, D // 4], F32) for i in range(4)]
            stg = [sb(f"stg{i}", [128, CHK], BF16) for i in range(8)]
            stgv = [sb(f"stgv{i}", [128, 512], BF16) for i in range(2)]
            stats = [sb(f"stats{i}", [128, 48], F32) for i in range(2)]
            mv = [sb(f"mv{i}", [128, 2], F32) for i in range(4)]
            veps = [sb(f"veps{i}", [128, 1], F32) for i in range(2)]
            rstd = [sb(f"rstd{i}", [128, 1], F32) for i in range(2)]
            mr_sb = [sb(f"mr_sb{i}", [1, 512], BF16) for i in range(2)]
            mbc = [sb(f"mbc1_{i}", [128, 256], BF16) for i in range(2)]
            rbc1 = [sb(f"rbc1_{i}", [128, 256], BF16) for i in range(2)]
            sb_bv3 = sb("sb_bv3", [1, HSL], F32)
            ps = [pt(f"p1_{i}", [128, 512]) for i in range(4)]
            ps_b = pt("p1_b", [128, 512])
            ps_mr = ph1.enter_context(nc.psum_tensor("ps_mr", [1, 512], F32))
            ps_bc = pt("ps_bc", [128, 512])

            def qt_src(q):
                i, qq = q // 4, q % 4
                cs = slice(qq * (D // 4), (qq + 1) * (D // 4))
                if i < NLT:
                    return x[i * 128:(i + 1) * 128, cs]
                j = i - NLT
                return x_own[j * 128:(j + 1) * 128, cs]

            evac_engine = []
            for c in range(NCH):
                evac_engine += ['A'] * 8 + ['V'] * 2
            actc = np.cumsum([1 if t == 'A' else 0 for t in evac_engine])
            dvec = np.cumsum([1 if t == 'V' else 0 for t in evac_engine])

            with nc.Block() as blk:
                @blk.gpsimd
                def _(g):
                    s_const.inc(g.dma_start(out=sb_ones_b[:, :],
                                            in_=ones_bcol.ap()), 16)
                    s_const.inc(g.dma_start(out=sb_ones_f[:, :],
                                            in_=ones_frow.ap()), 16)
                    s_const.inc(g.dma_start(out=sb_tri[:, :],
                                            in_=tri_t.ap()), 16)
                    s_id32.inc(g.dma_start(out=sb_id32[:, :],
                                           in_=ident_f32.ap()), 16)
                    s_id32.inc(g.dma_start(out=sb_onesrb[:, :],
                                           in_=ones_frb.ap()), 16)
                    for wi, (wsb, wsrc) in enumerate(
                            ((w_qT, wq_t), (w_kT, wk_t), (w_vT, wv_t))):
                        for f in range(NFC):
                            s_wt[wi].inc(g.dma_start(
                                out=wsb[:, f * 512:(f + 1) * 512],
                                in_=wsrc[f * 128:(f + 1) * 128, :]), 16)
                    bg = []
                    for gu in range(2):
                        for ig in range(INTER // 128):
                            gidx = ig * 2 + gu
                            i0 = gu * INTER + ig * 128
                            for half in range(2):
                                srcv = v1[half * 512:(half + 1) * 512,
                                          i0:i0 + 128]
                                srcv = srcv.rearrange("(rc p) i -> p rc i",
                                                      p=128)
                                dstv = v1_p[gidx, :,
                                            half * 512:(half + 1) * 512]
                                dstv = dstv.rearrange("p (rc i) -> p rc i",
                                                      i=128)
                                bg.append((s_bgv, dstv, srcv))
                    bgi = [0]

                    def bg_pump(k):
                        for _ in range(k):
                            if bgi[0] < len(bg):
                                s_, dst, srcb = bg[bgi[0]]
                                s_.inc(g.dma_start(out=dst, in_=srcb), 16)
                                bgi[0] += 1

                    for c in range(NCH):
                        g.wait_ge(s_bce.h, 2 * (c + 1))
                        for h in range(2):
                            gix = 2 * c + h
                            g.wait_ge(s_xf[gix % 3].h, (gix // 3 + 1) * 16 * 16)
                            xf3 = xfb[gix % 3][:, :].rearrange(
                                "p (f t) -> p f t", t=CHK)
                            mslice = mbc[c % 2][:, :]
                            m3 = bass.AP(mslice.tensor, mslice.offset,
                                         [list(mslice.ap[0]), [0, NFC // 2],
                                          [1, CHK]])
                            rslice = rbc1[c % 2][:, :]
                            r3 = bass.AP(rslice.tensor, rslice.offset,
                                         [list(rslice.ap[0]), [0, NFC // 2],
                                          [1, CHK]])
                            xn3 = xnb[c % 2][:, h * 16 * CHK:
                                             (h + 1) * 16 * CHK].rearrange(
                                "p (f t) -> p f t", t=CHK)
                            g.tensor_tensor(xf3, xf3, m3, op=ALU.subtract)
                            g.drain()
                            s_xn.inc(g.tensor_tensor(xn3, xf3, r3,
                                                     op=ALU.mult))
                        bg_pump(12)
                    bg_pump(len(bg))
                    g.wait_ge(s_bgv.h, s_bgv.n)

                @blk.sync
                def _(sy):
                    s_bv.inc(sy.dma_start(out=sb_bv3[:, :], in_=bq[:, :]), 16)
                    sy.wait_ge(s_bprep.h, 1)
                    s_bv.inc(sy.dma_start(out=sb_bv3[:, :], in_=bk[:, :]), 16)
                    sy.wait_ge(s_bprep.h, 2)
                    s_bv.inc(sy.dma_start(out=sb_bv3[:, :], in_=bv[:, :]), 16)
                    for q in range(8):
                        if q >= 4:
                            sy.wait_ge(s_st.h, q - 3)
                        s_ldx[q % 4].inc(sy.dma_start(out=xb[q % 4][:, :],
                                                      in_=qt_src(q)), 16)
                    for c in range(NCHA):
                        if c < NCH:
                            for h in range(2):
                                gix = 2 * c + h
                                if gix >= 3:
                                    sy.wait_ge(s_xn.h, gix - 2)
                                for fq in range(NFC // 2):
                                    f = h * 16 + fq
                                    s_xf[gix % 3].inc(sy.dma_start(
                                        out=xfb[gix % 3][:, fq * CHK:
                                                         (fq + 1) * CHK],
                                        in_=x_q[c, f * 128:(f + 1) * 128, :]),
                                        16)
                        for i in range(TPC * c + 2, TPC * c + 2 + TPC):
                            if i < NTILE:
                                for q in range(4 * i, 4 * i + 4):
                                    if q >= 4:
                                        sy.wait_ge(s_st.h, q - 3)
                                    s_ldx[q % 4].inc(sy.dma_start(
                                        out=xb[q % 4][:, :],
                                        in_=qt_src(q)), 16)

                @blk.vector
                def _(v):
                    def v_evacs(c):
                        if c >= 1:
                            v.wait_ge(s_qstv.h, 2 * c * 16)
                        for j in range(2):
                            gt = c * OTC + 8 + j
                            v.wait_ge(s_mm1.h, gt + 1)
                            s_evv.inc(v.tensor_copy(stgv[j][:, :],
                                                    ps[gt % 4][:, :]))

                    for p in range(NCHA):
                        i0, i1 = 2 * p, 2 * p + 1
                        if p >= 1:
                            v.wait_ge(s_ln2.h, i0)
                        if p >= 2:
                            v.wait_ge(s_mrt.h, p - 1)
                        for i in (i0, i1):
                            sl = i % 2
                            for qq in range(4):
                                q = 4 * i + qq
                                v.wait_ge(s_ldx[q % 4].h, (q // 4 + 1) * 16)
                                ins = None
                                for j in range(2):
                                    ins = v.bn_stats(
                                        stats[sl][:, 12 * qq + 6 * j:
                                                  12 * qq + 6 * j + 6],
                                        xb[q % 4][:, 512 * j:512 * (j + 1)])
                                s_st.inc(ins)
                        v.drain()
                        for i in (i0, i1):
                            v.bn_aggr(mv[i % 4][:, :], stats[i % 2][:, :])
                        v.drain()
                        for i in (i0, i1):
                            v.tensor_scalar(veps[i % 2][:, :],
                                            mv[i % 4][:, 1:2],
                                            EPS, None, ALU.add)
                        v.drain()
                        s_ln1.inc(v.reciprocal(veps[i0 % 2][:, :],
                                               veps[i0 % 2][:, :]))
                        s_ln1.inc(v.reciprocal(veps[i1 % 2][:, :],
                                               veps[i1 % 2][:, :]))
                        if p >= 2 and p - 2 < NCH:
                            v_evacs(p - 2)
                    for c in range(NCHA - 2, NCH):
                        v_evacs(c)

                @blk.tensor
                def _(t):
                    t.wait_ge(s_const.h, 48)
                    for wi in range(3):
                        t.wait_ge(s_bv.h, (wi + 1) * 16)
                        ins = None
                        for dc in range(4):
                            ins = t.matmul(
                                ps_b[:, wi * 4 + dc:wi * 4 + dc + 1],
                                sb_bv3[:, dc * 128:(dc + 1) * 128],
                                sb_ones_f[:, 0:1],
                                start=True, stop=True)
                        s_bprep.inc(ins)
                    t.wait_ge(s_id32.h, 32)

                    def mr_transp(c):
                        t.wait_ge(s_ln1.h, 2 * c + 2)
                        t.wait_ge(s_ln2.h, 2 * c + 2)
                        if c >= 1:
                            t.wait_ge(s_mre.h, c)
                        t.transpose(ps_mr[0:1, 0:128], mv[(2 * c) % 4][:, 0:1],
                                    sb_id32[:, :])
                        t.transpose(ps_mr[0:1, 128:256],
                                    mv[(2 * c + 1) % 4][:, 0:1], sb_id32[:, :])
                        t.transpose(ps_mr[0:1, 256:384], rstd[0][:, :],
                                    sb_id32[:, :])
                        ins = t.transpose(ps_mr[0:1, 384:512], rstd[1][:, :],
                                          sb_id32[:, :])
                        s_mrt.inc(ins)

                    def mr_bcast(c):
                        t.wait_ge(s_mre.h, c + 1)
                        if c >= 1:
                            t.wait_ge(s_bce.h, 2 * c)
                        t.matmul(ps_bc[:, 0:256], sb_onesrb[:, :],
                                 mr_sb[c % 2][0:1, 0:256],
                                 start=True, stop=True)
                        ins = t.matmul(ps_bc[:, 256:512], sb_onesrb[:, :],
                                       mr_sb[c % 2][0:1, 256:512],
                                       start=True, stop=True)
                        s_bcmm.inc(ins)

                    def qkv(c):
                        t.wait_ge(s_xn.h, 2 * (c + 1))
                        for j in range(OTC):
                            gt = c * OTC + j
                            b = gt % 4
                            if c == 0 and j in (0, 4, 8):
                                t.wait_ge(s_wt[j // 4].h, 32 * 16)
                            if gt >= 4:
                                pt_ = gt - 4
                                if evac_engine[pt_] == 'A':
                                    t.wait_ge(s_evq.h, int(actc[pt_]))
                                else:
                                    t.wait_ge(s_evv.h, int(dvec[pt_]))
                            if j < 8:
                                wsb = w_qT if j < 4 else w_kT
                                dc = j % 4
                                for f in range(NFC):
                                    ins = t.matmul(
                                        ps[b][:, 0:CHK],
                                        wsb[:, f * 512 + dc * 128:
                                            f * 512 + (dc + 1) * 128],
                                        xnb[c % 2][:, f * CHK:(f + 1) * CHK],
                                        start=(f == 0), stop=(f == NFC - 1))
                            else:
                                ts = j - 8
                                for f in range(NFC):
                                    ins = t.matmul(
                                        ps[b][:, :],
                                        xnb[c % 2][:, f * CHK + ts * 128:
                                                   f * CHK + (ts + 1) * 128],
                                        w_vT[:, f * 512:(f + 1) * 512],
                                        start=(f == 0), stop=(f == NFC - 1))
                            s_mm1.inc(ins)

                    for c in range(NCHA):
                        mr_transp(c)
                        if c >= 1 and c - 1 < NCH:
                            mr_bcast(c - 1)
                        if c >= 2 and c - 2 < NCH:
                            qkv(c - 2)
                    for c in range(NCHA - 2, NCH):
                        qkv(c)

                @blk.scalar
                def _(sc):
                    def qk_out(c):
                        sc.wait_ge(s_bcols.h, 1)
                        if c >= 1:
                            sc.wait_ge(s_qstk.h, 8 * c * 16)
                        for j in range(8):
                            gt = c * OTC + j
                            sc.wait_ge(s_mm1.h, gt + 1)
                            bias = sb_bqkv[:, j:j + 1]
                            n = s_evq.inc(sc.activation(
                                stg[j][:, 0:CHK], ps[gt % 4][:, 0:CHK],
                                AF.Identity, bias=bias))
                            sc.wait_ge(s_evq.h, n)
                            dst = (q_d if j < 4 else k_d)
                            dc = j % 4
                            s_qstk.inc(sc.dma_start(
                                out=dst[dc * 128:(dc + 1) * 128,
                                        c * CHK:(c + 1) * CHK],
                                in_=stg[j][:, 0:CHK]), 16)

                    def v_out(c):
                        for j in range(2):
                            gt = c * OTC + 8 + j
                            sc.wait_ge(s_evv.h, int(dvec[gt]))
                            s_qstv.inc(sc.dma_start(
                                out=v_d[c * CHK + j * 128:
                                        c * CHK + (j + 1) * 128, :],
                                in_=stgv[j][:, :]), 16)

                    sc.wait_ge(s_bprep.h, 3)
                    s_bcols.inc(sc.copy(sb_bqkv[:, :], ps_b[:, 0:12]))
                    for c in range(NCHA):
                        for i in (2 * c, 2 * c + 1):
                            sl = i % 2
                            sc.wait_ge(s_ln1.h, i + 1)
                            if i >= 4:
                                sc.wait_ge(s_mrt.h, i // 2)
                            s_ln2.inc(sc.activation(rstd[sl][:, :],
                                                    veps[sl][:, :], AF.Sqrt))
                        sc.wait_ge(s_mrt.h, c + 1)
                        if c < NCH:
                            n = s_mre.inc(sc.copy(mr_sb[c % 2][:, :],
                                                  ps_mr[0:1, :]))
                            if c >= 1:
                                sc.wait_ge(s_bcmm.h, c)
                                if c >= 3:
                                    sc.wait_ge(s_xn.h, 2 * (c - 2))
                                s_bce.inc(sc.copy(mbc[(c - 1) % 2][:, :],
                                                  ps_bc[:, 0:256]))
                                s_bce.inc(sc.copy(rbc1[(c - 1) % 2][:, :],
                                                  ps_bc[:, 256:512]))
                        else:
                            co = c - NCH
                            sc.wait_ge(s_bcmm.h, NCH)
                            if co == 0:
                                sc.wait_ge(s_xn.h, 2 * (c - 2))
                                s_bce.inc(sc.copy(mbc[(NCH - 1) % 2][:, :],
                                                  ps_bc[:, 0:256]))
                                s_bce.inc(sc.copy(rbc1[(NCH - 1) % 2][:, :],
                                                  ps_bc[:, 256:512]))
                            if co >= 2:
                                sc.wait_ge(s_mrst.h, co * 32)
                            n = s_mre.inc(sc.copy(mr_sb[c % 2][:, :],
                                                  ps_mr[0:1, :]))
                            sc.wait_ge(s_mre.h, n)
                            s_mrst.inc(sc.dma_start(
                                out=mrown_d[0:1, co * 256:(co + 1) * 256],
                                in_=mr_sb[c % 2][0:1, 0:256]), 16)
                            s_mrst.inc(sc.dma_start(
                                out=mrown_d[0:1, OWN + co * 256:
                                            OWN + (co + 1) * 256],
                                in_=mr_sb[c % 2][0:1, 256:512]), 16)
                        if c >= 3 and c - 3 < NCH:
                            v_out(c - 3)
                        if c >= 2 and c - 2 < NCH:
                            qk_out(c - 2)
                    for c in range(NCHA - 3, NCH):
                        v_out(c)
                    for c in range(NCHA - 2, NCH):
                        qk_out(c)
                    sc.wait_ge(s_qstk.h, s_qstk.n)
                    sc.wait_ge(s_qstv.h, s_qstv.n)
                    sc.wait_ge(s_mrst.h, s_mrst.n)

        s_qkvld = [sem("s_qkvld0"), sem("s_qkvld1")]
        s_mmA = sem("s_mmA")
        s_maxA = sem("s_maxA")
        s_cred = sem("s_cred")
        s_ctr = sem("s_ctr")
        s_cneg = sem("s_cneg")
        s_cbc = sem("s_cbc")
        s_cbias = sem("s_cbias")
        s_mmB = sem("s_mmB")
        s_expB = sem("s_expB")
        s_mskB = sem("s_mskB")
        s_pvmm = sem("s_pvmm")
        s_rcp = sem("s_rcp")
        s_rbcp = sem("s_rbcp")
        s_rbcs = sem("s_rbcs")
        s_mul = sem("s_mul")
        s_ustg = sem("s_ustg")
        s_uin = [sem("s_uin0"), sem("s_uin1")]
        s_a2a = sem("s_a2a")


        NIT = 8
        NQG = SEQ // 512
        KBS = SEQ // 128
        CPI = sum(4 * qg + 4 for qg in range(NQG))
        _chk_tot = []
        _acc = 0
        for _it in range(NIT):
            row = []
            for _qg in range(NQG):
                _acc += 4 * _qg + 4
                row.append(_acc)
            _chk_tot.append(row)

        _n_passA = sum(4 * (qg + 1) for qg in range(NQG))
        with ExitStack() as ph2:
            sb = lambda n, s, d: ph2.enter_context(nc.sbuf_tensor(n, s, d))
            qh = [sb(f"qh{i}", [128, SEQ], BF16) for i in range(2)]
            kh = [sb(f"kh{i}", [128, SEQ], BF16) for i in range(2)]
            vh = [sb(f"vh{i}", [128, SEQ], BF16) for i in range(2)]
            tmpmax = [sb(f"tmpmax{i}", [128, 160], F32) for i in range(2)]
            maxred = [sb(f"maxred{i}", [128, 1], F32) for i in range(2)]
            csc = [sb(f"csc{i}", [1, 2], F32) for i in range(2)]
            cb = [sb(f"cb{i}", [128, 1], F32) for i in range(2)]
            pbuf = [sb(f"pbuf{i}", [128, 512], BF16) for i in range(4)]
            rsb = [sb(f"rsb{i}", [1, 512], F32) for i in range(2)]
            rbc = [sb(f"rbc{i}", [128, 512], F32) for i in range(2)]
            umul = [sb(f"umul{i}", [128, 512], F32) for i in range(2)]
            ustg = [sb(f"ustg{i}", [128, 512], BF16) for i in range(2)]
            SA = ph2.enter_context(nc.psum_tensor("SA", [128, 1024], F32))
            SB = ph2.enter_context(nc.psum_tensor("SBp", [128, 1024], F32))
            PV = [ph2.enter_context(nc.psum_tensor(f"PV{i}", [128, 512], F32))
                  for i in range(2)]
            SUM = ph2.enter_context(nc.psum_tensor("SUMp", [1, 512], F32))
            RBC = ph2.enter_context(nc.psum_tensor("RBCp", [128, 512], F32))

            _msk_tot = {}
            _mcnt = 0
            _gc = 0
            for _it in range(NIT):
                for _qg in range(NQG):
                    for _r in range(4 * _qg + 4):
                        if _r >= 4 * _qg:
                            _j = _r - 4 * _qg
                            _mcnt += _j + 1
                            _msk_tot[_gc] = _mcnt
                        _gc += 1

            with nc.Block() as blk:
                @blk.sync
                def _(sy):
                    for it in range(NIT):
                        s, h = it // 4, it % 4
                        if it >= 2:
                            sy.wait_ge(s_pvmm.h, CPI * (it - 1))
                        sl = it % 2
                        s_qkvld[sl].inc(sy.dma_start(
                            out=qh[sl][:, :],
                            in_=q_d[h * 128:(h + 1) * 128,
                                    s * SEQ:(s + 1) * SEQ]), 16)
                        s_qkvld[sl].inc(sy.dma_start(
                            out=kh[sl][:, :],
                            in_=k_d[h * 128:(h + 1) * 128,
                                    s * SEQ:(s + 1) * SEQ]), 16)
                        for kb in range(KBS):
                            s_qkvld[sl].inc(sy.dma_start(
                                out=vh[sl][:, kb * 128:(kb + 1) * 128],
                                in_=v_d[s * SEQ + kb * 128:
                                        s * SEQ + (kb + 1) * 128,
                                        h * 128:(h + 1) * 128]), 16)

                @blk.tensor
                def _(t):
                    tA = 0
                    gc = 0
                    for it in range(NIT):
                        sl = it % 2
                        t.wait_ge(s_qkvld[sl].h, (it // 2 + 1) * (2 + KBS) * 16)
                        for qg in range(NQG):
                            for kb5 in range(qg + 1):
                                for qs in range(4):
                                    if tA >= 2:
                                        t.wait_ge(s_maxA.h, tA - 1)
                                    ins = t.matmul(
                                        SA[:, (tA % 2) * 512:(tA % 2 + 1) * 512],
                                        qh[sl][:, qg * 512 + qs * 128:
                                               qg * 512 + (qs + 1) * 128],
                                        kh[sl][:, kb5 * 512:(kb5 + 1) * 512],
                                        start=True, stop=True)
                                    s_mmA.inc(ins)
                                    tA += 1
                        t.wait_ge(s_id32.h, 32)
                        t.wait_ge(s_cred.h, it + 1)
                        s_ctr.inc(t.transpose(RBC[0:1, 0:128],
                                              maxred[sl][:, :], sb_id32[:, :]))
                        t.wait_ge(s_cneg.h, it + 1)
                        if it >= 1:
                            t.wait_ge(s_rbcs.h, NQG * it)
                        s_cbc.inc(t.matmul(RBC[:, 0:1], sb_ones_f[:, :],
                                           csc[sl][0:1, 1:2],
                                           start=True, stop=True))
                        for qg in range(NQG):
                            qgg = it * NQG + qg
                            nkb = 4 * qg + 4

                            def s_mm(r, gc_r):
                                if gc_r >= 2:
                                    t.wait_ge(s_expB.h, gc_r - 1)
                                ins = t.matmul(
                                    SB[:, (gc_r % 2) * 512:(gc_r % 2 + 1) * 512],
                                    kh[sl][:, r * 128:(r + 1) * 128],
                                    qh[sl][:, qg * 512:(qg + 1) * 512],
                                    start=True, stop=True)
                                s_mmB.inc(ins)

                            def pv_sum(r, gc_r):
                                if r >= 4 * qg:
                                    t.wait_ge(s_mskB.h, _msk_tot[gc_r])
                                else:
                                    t.wait_ge(s_expB.h, gc_r + 1)
                                if r == 0 and qgg >= 2:
                                    t.wait_ge(s_mul.h, qgg - 1)
                                t.matmul(PV[qg % 2][:, :],
                                         vh[sl][:, r * 128:(r + 1) * 128],
                                         pbuf[gc_r % 4][:, :],
                                         start=(r == 0), stop=(r == nkb - 1))
                                if r == 0 and qgg >= 1:
                                    t.wait_ge(s_rcp.h, qgg)
                                ins = t.matmul(SUM[0:1, :], sb_ones_b[:, :],
                                               pbuf[gc_r % 4][:, :],
                                               start=(r == 0), stop=(r == nkb - 1))
                                s_pvmm.inc(ins)

                            for r in range(nkb):
                                s_mm(r, gc + r)
                                if r >= 1:
                                    pv_sum(r - 1, gc + r - 1)
                            pv_sum(nkb - 1, gc + nkb - 1)
                            gc += nkb
                            t.wait_ge(s_rcp.h, qgg + 1)
                            t.wait_ge(s_cbias.h, it + 1)
                            if qgg >= 1:
                                t.wait_ge(s_rbcs.h, qgg)
                            s_rbcp.inc(t.matmul(RBC[:, :], sb_ones_f[:, :],
                                                rsb[qgg % 2][:, :],
                                                start=True, stop=True))

                @blk.vector
                def _(v):
                    tA = 0
                    for it in range(NIT):
                        sl = it % 2
                        for ti in range(_n_passA):
                            v.wait_ge(s_mmA.h, tA + 1)
                            s_maxA.inc(v.reduce_max(
                                tmpmax[sl][:, ti:ti + 1],
                                SA[:, (tA % 2) * 512:(tA % 2 + 1) * 512],
                                axis=AX.X))
                            tA += 1
                        v.drain()
                        s_cred.inc(v.reduce_max(maxred[sl][:, :],
                                                tmpmax[sl][:, 0:_n_passA],
                                                axis=AX.X))
                        v.drain()
                        v.wait_ge(s_ctr.h, it + 1)
                        v.reduce_max(csc[sl][0:1, 0:1], RBC[0:1, 0:128],
                                     axis=AX.X)
                        v.drain()
                        s_cneg.inc(v.tensor_scalar(
                            csc[sl][0:1, 1:2], csc[sl][0:1, 0:1],
                            -INV_SQRT_HD, None, ALU.mult))
                        v.drain()
                        for qg in range(NQG):
                            qgg = it * NQG + qg
                            v.wait_ge(s_pvmm.h, _chk_tot[it][qg])
                            s_rcp.inc(v.reciprocal(rsb[qgg % 2][:, :],
                                                   SUM[0:1, :]))
                            v.drain()
                            v.wait_ge(s_rbcs.h, qgg + 1)
                            if qgg >= 2:
                                v.wait_ge(s_ustg.h, qgg - 1)
                            s_mul.inc(v.tensor_tensor(
                                umul[qgg % 2][:, :], PV[qg % 2][:, :],
                                rbc[qgg % 2][:, :], op=ALU.mult))

                @blk.scalar
                def _(sc):
                    gc = 0
                    for it in range(NIT):
                        sl = it % 2
                        s, h = it // 4, it % 4
                        sc.wait_ge(s_cbc.h, it + 1)
                        n = s_cbias.inc(sc.copy(cb[sl][:, :], RBC[:, 0:1]))
                        sc.wait_ge(s_cbias.h, n)
                        for qg in range(NQG):
                            qgg = it * NQG + qg
                            nkb = 4 * qg + 4
                            for r in range(nkb):
                                sc.wait_ge(s_mmB.h, gc + 1)
                                if gc >= 4:
                                    sc.wait_ge(s_pvmm.h, gc - 3)
                                s_expB.inc(sc.activation(
                                    pbuf[gc % 4][:, :],
                                    SB[:, (gc % 2) * 512:(gc % 2 + 1) * 512],
                                    AF.Exp, bias=cb[sl][:, :],
                                    scale=INV_SQRT_HD))
                                gc += 1
                            sc.wait_ge(s_rbcp.h, qgg + 1)
                            s_rbcs.inc(sc.copy(rbc[qgg % 2][:, :], RBC[:, :]))
                            sc.wait_ge(s_mul.h, qgg + 1)
                            if qgg >= 2:
                                sc.wait_ge(s_uin[qgg % 2].h,
                                           (qgg // 2) * 16)
                            n = s_ustg.inc(sc.activation(
                                ustg[qgg % 2][:, :], umul[qgg % 2][:, :],
                                AF.Identity, bias=sb_bqkv[:, 8 + h:9 + h]))
                            sc.wait_ge(s_ustg.h, n)
                            tok0 = s * SEQ + qg * 512
                            jj = tok0 // OWN
                            off = tok0 % OWN
                            s_uin[qgg % 2].inc(sc.dma_start(
                                out=u_in[jj, h * 128:(h + 1) * 128,
                                         off:off + 512],
                                in_=ustg[qgg % 2][:, :]), 16)
                    sc.wait_ge(s_uin[0].h, s_uin[0].n)
                    sc.wait_ge(s_uin[1].h, s_uin[1].n)

                @blk.gpsimd
                def _(g):
                    gc = 0
                    mcnt = 0
                    for it in range(NIT):
                        for qg in range(NQG):
                            for r in range(4 * qg + 4):
                                if r >= 4 * qg:
                                    j = r - 4 * qg
                                    g.wait_ge(s_expB.h, gc + 1)
                                    for i in range(j):
                                        s_mskB.inc(g.memset(
                                            pbuf[gc % 4][:, i * 128:(i + 1) * 128],
                                            0.0))
                                        mcnt += 1
                                    s_mskB.inc(g.tensor_tensor(
                                        pbuf[gc % 4][:, j * 128:(j + 1) * 128],
                                        pbuf[gc % 4][:, j * 128:(j + 1) * 128],
                                        sb_tri[:, :], op=ALU.mult))
                                    mcnt += 1
                                    assert _msk_tot[gc] == mcnt
                                gc += 1
                    g.wait_ge(s_uin[0].h, s_uin[0].n)
                    g.wait_ge(s_uin[1].h, s_uin[1].n)
                    with allow_ici_all_to_all():
                        s_a2a.inc(g.collective_compute(
                            "AllToAll", ALU.bypass,
                            replica_groups=[[0, 1, 2, 3, 4, 5, 6, 7]],
                            ins=[u_in.ap().opt()],
                            outs=[u_out.ap().opt()]))

        NTH = OWN // 512
        NIC = INTER // 128
        NRC = RANK // 128
        s3_b1l = [sem("s3_b1l0"), sem("s3_b1l1")]
        s3_b1m = sem("s3_b1m")
        s3_b1e = sem("s3_b1e")
        s3_xno = sem("s3_xno")
        s3_xfo = [sem("s3_xfo0"), sem("s3_xfo1")]
        s3_mr = sem("s3_mr")
        s3_mrb = sem("s3_mrb")
        s3_mrbe = sem("s3_mrbe")
        s3_u1 = [sem("s3_u1a"), sem("s3_u1b")]
        s3_v1 = [sem(f"s3_v1{i}") for i in range(4)]
        s3_u2 = [sem("s3_u2a"), sem("s3_u2b")]
        s3_v2 = [sem("s3_v2a"), sem("s3_v2b")]
        s3_mm = sem("s3_mm")
        s3_evd = sem("s3_evd")
        s3_eva = sem("s3_eva")
        s3_glu = sem("s3_glu")
        s3_fst = [sem(f"s3_fst{i}") for i in range(4)]

        _b1loads = []
        _o = 0
        while _o < 2 * INTER:
            _w = min(2048, 2 * INTER - _o)
            _b1loads.append((_o, _w))
            _o += _w

        _cnt = 0
        _p_last = []
        for _th in range(NTH):
            for _f in range(NFC):
                _cnt += 1
        _cnt = NTH * NFC
        _hgrp_flat = []
        _h_mm = []
        _u2grp = []
        _z_mm = []
        _f_mm = []
        for _th in range(NTH):
            for _ig in range(NIC):
                _cnt += 1
                _g = _cnt
                _cnt += 1
                _u = _cnt
                _h_mm.append((_g, _u))
                _hgrp_flat += [_g, _u]
            for _i in range(NIC):
                _cnt += 1
                _u2grp.append(_cnt)
            _z_mm.append(_cnt)
            _row = []
            for _j in range(8):
                for _ts in range(4):
                    _cnt += 1
                    _row.append(_cnt)
            _f_mm.append(_row)
        _v1gate = {g: _hgrp_flat[g - 4] for g in range(4, 2 * NTH * NIC)}
        _u2gate = {g: _u2grp[g - 2] for g in range(2, NTH * NIC)}
        _v2gate = {g: _f_mm[(g - 2) // 8][((g - 2) % 8) * 4 + 3]
                   for g in range(2, NTH * 8)}
        _fevd = {}
        for _m in range(32 * NTH):
            _tm = _m // 32
            _fevd[_m] = 8 * NTH + 40 * _tm + 8 + (_m - 32 * _tm) + 1

        with ExitStack() as ph3:
            sb = lambda n, s, d: ph3.enter_context(nc.sbuf_tensor(n, s, d))
            pp = [ph3.enter_context(nc.psum_tensor(f"pp{i}", [128, 512], F32))
                  for i in range(8)]
            b1row = [sb(f"b1row{i}", [1, 2048], F32) for i in range(2)]
            b1c = sb("b1c", [128, 2 * NIC], F32)
            xnoT = [sb(f"xnoT{i}", [128, 512], BF16) for i in range(2)]
            xfo = [sb(f"xfo{i}", [128, 512], F32) for i in range(2)]
            mr3row = sb("mr3row", [1, 2 * OWN], BF16)
            mbc3 = sb("mbc3", [128, 512], BF16)
            rbc3 = sb("rbc3", [128, 512], BF16)
            u1sb = [sb(f"u1sb{i}", [128, RANK], BF16) for i in range(2)]
            v1sb = [sb(f"v1sb{i}", [128, RANK], BF16) for i in range(4)]
            u2sb = [sb(f"u2sb{i}", [128, RANK], BF16) for i in range(2)]
            v2sb = [sb(f"v2sb{i}", [128, RANK * 4], BF16) for i in range(2)]
            g_sb = [sb(f"g_sb{i}", [128, 512], F32) for i in range(2)]
            up_sb = [sb(f"up_sb{i}", [128, 512], F32) for i in range(2)]
            P_sb = sb("P_sb", [128, NRC * OWN], BF16)
            act_sb = sb("act_sb", [128, NIC * 512], BF16)
            Z_sb = sb("Z_sb", [128, NRC * 512], BF16)
            ffn_stg = [sb(f"ffn_stg{i}", [128, 512], BF16) for i in range(4)]

            with nc.Block() as blk:
                @blk.gpsimd
                def _(g):
                    for th in range(NTH):
                        g.wait_ge(s3_mrbe.h, 2 * (th + 1))
                        for f in range(NFC):
                            gi = th * NFC + f
                            g.wait_ge(s3_xfo[gi % 2].h, (gi // 2 + 1) * 16)
                            if gi >= 2:
                                g.wait_ge(s3_mm.h, gi - 1)
                            s3_u1[gi % 2].inc(g.dma_start(
                                out=u1sb[gi % 2][:, :],
                                in_=u1[f * 128:(f + 1) * 128, :]), 16)
                            g.tensor_tensor(xfo[gi % 2][:, :],
                                            xfo[gi % 2][:, :],
                                            mbc3[:, :], op=ALU.subtract)
                            g.drain()
                            s3_xno.inc(g.tensor_tensor(
                                xnoT[gi % 2][:, :], xfo[gi % 2][:, :],
                                rbc3[:, :], op=ALU.mult))
                    for th in range(NTH):
                        for i in range(NIC):
                            gi = th * NIC + i
                            if gi >= 2:
                                g.wait_ge(s3_mm.h, _u2gate[gi])
                            s3_u2[gi % 2].inc(g.dma_start(
                                out=u2sb[gi % 2][:, :],
                                in_=u2[i * 128:(i + 1) * 128, :]), 16)
                        for j in range(8):
                            gi = th * 8 + j
                            if gi >= 2:
                                g.wait_ge(s3_mm.h, _v2gate[gi])
                            for rc in range(NRC):
                                s3_v2[gi % 2].inc(g.dma_start(
                                    out=v2sb[gi % 2][:, rc * 512:(rc + 1) * 512],
                                    in_=v2[rc * 128:(rc + 1) * 128,
                                           j * 512:(j + 1) * 512]), 16)

                @blk.sync
                def _(sy):
                    s3_mr.inc(sy.dma_start(out=mr3row[:, :],
                                           in_=mrown_d[:, :]), 16)
                    nch_done = 0
                    for li, (off, wid) in enumerate(_b1loads):
                        if li >= 2:
                            sy.wait_ge(s3_b1m.h, nch_done)
                        s3_b1l[li % 2].inc(sy.dma_start(
                            out=b1row[li % 2][:, 0:wid],
                            in_=b1[:, off:off + wid]), 16)
                        if li >= 1:
                            nch_done += _b1loads[li - 1][1] // 128
                    for th in range(NTH):
                        for f in range(NFC):
                            gi = th * NFC + f
                            if gi >= 2:
                                sy.wait_ge(s3_xno.h, gi - 1)
                                sy.wait_ge(s3_mm.h, gi - 1)
                            s3_xfo[gi % 2].inc(sy.dma_start(
                                out=xfo[gi % 2][:, :],
                                in_=xf_own[f * 128:(f + 1) * 128,
                                           th * 512:(th + 1) * 512]), 16)
                    sy.wait_ge(s_bgv.h, s_bgv.n)
                    for th in range(NTH):
                        for ig in range(NIC):
                            for gu in range(2):
                                gidx = (th * NIC + ig) * 2 + gu
                                if gidx >= 4:
                                    sy.wait_ge(s3_mm.h, _v1gate[gidx])
                                pidx = ig * 2 + gu
                                s3_v1[gidx % 4].inc(sy.dma_start(
                                    out=v1sb[gidx % 4][:, :],
                                    in_=v1_p[pidx, :, :]), 16)

                @blk.tensor
                def _(t):
                    mmn = 0

                    def bcast(th):
                        if th >= 1:
                            t.wait_ge(s3_mrbe.h, 2 * th)
                            t.wait_ge(s3_xno.h, th * NFC)
                        t.matmul(pp[6][:, :], sb_onesrb[:, :],
                                 mr3row[0:1, th * 512:(th + 1) * 512],
                                 start=True, stop=True)
                        ins = t.matmul(pp[7][:, :], sb_onesrb[:, :],
                                       mr3row[0:1, OWN + th * 512:
                                              OWN + (th + 1) * 512],
                                       start=True, stop=True)
                        s3_mrb.inc(ins)

                    t.wait_ge(s3_mr.h, 16)
                    bcast(0)
                    nch = 0
                    for li, (off, wid) in enumerate(_b1loads):
                        t.wait_ge(s3_b1l[li % 2].h, (li // 2 + 1) * 16)
                        ins = None
                        for cc in range(wid // 128):
                            ins = t.matmul(
                                pp[0][:, nch:nch + 1],
                                b1row[li % 2][0:1, cc * 128:(cc + 1) * 128],
                                sb_ones_f[:, 0:1], start=True, stop=True)
                            nch += 1
                        s3_b1m.inc(ins, wid // 128)
                    t.wait_ge(s3_b1e.h, 1)
                    for th in range(NTH):
                        if th >= 1:
                            t.wait_ge(s3_evd.h, 8 * th)
                            bcast(th)
                        for f in range(NFC):
                            gi = th * NFC + f
                            t.wait_ge(s3_xno.h, gi + 1)
                            t.wait_ge(s3_u1[gi % 2].h, (gi // 2 + 1) * 16)
                            ins = None
                            for rc in range(NRC):
                                ins = t.matmul(
                                    pp[rc][:, :],
                                    u1sb[gi % 2][:, rc * 128:(rc + 1) * 128],
                                    xnoT[gi % 2][:, :],
                                    start=(f == 0), stop=(f == NFC - 1))
                            mmn = s3_mm.inc(ins)
                    _Pev = 8 * NTH
                    for th in range(NTH):
                        for ig in range(NIC):
                            gidx2 = (th * NIC + ig) * 2
                            for gu in range(2):
                                gidx = gidx2 + gu
                                slot = gidx % 4
                                bank = pp[2 * (ig % 2) + gu]
                                t.wait_ge(s3_v1[gidx % 4].h,
                                          (gidx // 4 + 1) * 16)
                                if ig < 2:
                                    t.wait_ge(s3_evd.h, 8 * NTH + 40 * th)
                                if gidx >= 4:
                                    t.wait_ge(s3_eva.h, gidx - 3)
                                ins = None
                                for rc in range(NRC):
                                    ins = t.matmul(
                                        bank[:, :],
                                        v1sb[slot][:, rc * 128:(rc + 1) * 128],
                                        P_sb[:, rc * OWN + th * 512:
                                             rc * OWN + (th + 1) * 512],
                                        start=(rc == 0), stop=(rc == NRC - 1))
                                mmn = s3_mm.inc(ins)
                        for i in range(NIC):
                            gi = th * NIC + i
                            t.wait_ge(s3_u2[gi % 2].h, (gi // 2 + 1) * 16)
                            t.wait_ge(s3_glu.h, th * NIC + i + 1)
                            if i == 0:
                                t.wait_ge(s3_eva.h, 2 * NIC * (th + 1))
                            ins = None
                            for rc in range(NRC):
                                ins = t.matmul(
                                    pp[rc][:, :],
                                    u2sb[gi % 2][:, rc * 128:(rc + 1) * 128],
                                    act_sb[:, i * 512:(i + 1) * 512],
                                    start=(i == 0), stop=(i == NIC - 1))
                            mmn = s3_mm.inc(ins)
                        t.wait_ge(s3_evd.h, 8 * NTH + 40 * th + 8)
                        for j in range(8):
                            gi = th * 8 + j
                            t.wait_ge(s3_v2[gi % 2].h, (gi // 2 + 1) * 8 * 16)
                            for ts in range(4):
                                m = gi * 4 + ts
                                if m >= 4:
                                    t.wait_ge(s3_evd.h, _fevd[m - 4])
                                ins = None
                                for rc in range(NRC):
                                    ins = t.matmul(
                                        pp[ts][:, :],
                                        Z_sb[:, rc * 512 + ts * 128:
                                             rc * 512 + (ts + 1) * 128],
                                        v2sb[gi % 2][:, rc * 512:(rc + 1) * 512],
                                        start=(rc == 0), stop=(rc == NRC - 1))
                                mmn = s3_mm.inc(ins)

                @blk.scalar
                def _(sc):
                    sc.wait_ge(s3_mrb.h, 1)
                    s3_mrbe.inc(sc.copy(mbc3[:, :], pp[6][:, :]))
                    s3_mrbe.inc(sc.copy(rbc3[:, :], pp[7][:, :]))
                    sc.wait_ge(s3_b1m.h, 2 * NIC)
                    s3_b1e.inc(sc.copy(b1c[:, :], pp[0][:, 0:2 * NIC]))
                    for th in range(1, NTH):
                        sc.wait_ge(s3_mrb.h, th + 1)
                        s3_mrbe.inc(sc.copy(mbc3[:, :], pp[6][:, :]))
                        s3_mrbe.inc(sc.copy(rbc3[:, :], pp[7][:, :]))
                    for th in range(NTH):
                        for ig in range(NIC):
                            sc.wait_ge(s3_mm.h, _h_mm[th * NIC + ig][0])
                            if th * NIC + ig >= 2:
                                sc.wait_ge(s3_glu.h, th * NIC + ig - 1)
                            s3_eva.inc(sc.activation(
                                g_sb[ig % 2][:, :], pp[2 * (ig % 2)][:, :],
                                AF.Silu, bias=b1c[:, ig:ig + 1]))
                            sc.wait_ge(s3_mm.h, _h_mm[th * NIC + ig][1])
                            s3_eva.inc(sc.activation(
                                up_sb[ig % 2][:, :], pp[2 * (ig % 2) + 1][:, :],
                                AF.Identity, bias=b1c[:, NIC + ig:NIC + ig + 1]))
                        for j in range(8):
                            for ts in range(4):
                                m = (th * 8 + j) * 4 + ts
                                sc.wait_ge(s3_evd.h, _fevd[m])
                                s3_fst[m % 4].inc(sc.dma_start(
                                    out=ffn_d[th * 512 + ts * 128:
                                              th * 512 + (ts + 1) * 128,
                                              j * 512:(j + 1) * 512],
                                    in_=ffn_stg[m % 4][:, :]), 16)
                    for i in range(4):
                        sc.wait_ge(s3_fst[i].h, s3_fst[i].n)

                @blk.vector
                def _(v):
                    for th in range(NTH):
                        v.wait_ge(s3_mm.h, NFC * (th + 1))
                        for rc in range(NRC):
                            s3_evd.inc(v.tensor_copy(
                                P_sb[:, rc * OWN + th * 512:
                                     rc * OWN + (th + 1) * 512],
                                pp[rc][:, :]))
                    for th in range(NTH):
                        for ig in range(NIC):
                            gidx2 = (th * NIC + ig) * 2
                            v.wait_ge(s3_eva.h, gidx2 + 2)
                            s3_glu.inc(v.tensor_tensor(
                                act_sb[:, ig * 512:(ig + 1) * 512],
                                g_sb[ig % 2][:, :], up_sb[ig % 2][:, :],
                                op=ALU.mult))
                        v.wait_ge(s3_mm.h, _z_mm[th])
                        for rc in range(NRC):
                            s3_evd.inc(v.tensor_copy(
                                Z_sb[:, rc * 512:(rc + 1) * 512], pp[rc][:, :]))
                        for j in range(8):
                            for ts in range(4):
                                m = (th * 8 + j) * 4 + ts
                                v.wait_ge(s3_mm.h, _f_mm[th][j * 4 + ts])
                                if m >= 4:
                                    v.wait_ge(s3_fst[m % 4].h,
                                              (m // 4) * 16)
                                s3_evd.inc(v.tensor_copy(
                                    ffn_stg[m % 4][:, :], pp[ts][:, :]))

        s4_u = sem("s4_u")
        s4_wo = [sem(f"s4_wo{i}") for i in range(3)]
        s4_bb = sem("s4_bb")
        s4_bbv = sem("s4_bbv")
        s4_bbm = sem("s4_bbm")
        s4_bbe = sem("s4_bbe")
        s4_mm = sem("s4_mm")
        s4_add = sem("s4_add")
        s4_ffn = [sem("s4_ffn0"), sem("s4_ffn1")]
        s4_x = [sem("s4_x0"), sem("s4_x1")]
        s4_out = [sem("s4_out0"), sem("s4_out1")]
        NTS = OWN // 128

        with ExitStack() as ph4:
            sb = lambda n, s, d: ph4.enter_context(nc.sbuf_tensor(n, s, d))
            pq = [ph4.enter_context(nc.psum_tensor(f"pq{i}", [128, 512], F32))
                  for i in range(5)]
            u_sb = sb("u_sb", [128, 32 * OWN], BF16)
            wo_sb = [sb(f"wo_sb{i}", [128, 32 * 512], BF16) for i in range(3)]
            bbbc = sb("bbbc", [128, D], F32)
            b2_r = sb("b2_r", [1, D], F32)
            t1b = [sb(f"t1b{i}", [128, 512], F32) for i in range(2)]
            toutb = [sb(f"toutb{i}", [128, 512], F32) for i in range(2)]
            ffn_rd = [sb(f"ffn_rd{i}", [128, 512], BF16) for i in range(2)]
            x_rd = [sb(f"x_rd{i}", [128, 512], F32) for i in range(2)]

            with nc.Block() as blk:
                @blk.sync
                def _(sy):
                    s4_bb.inc(sy.dma_start(out=bbbc[0:1, :], in_=bo[:, :]), 16)
                    s4_bb.inc(sy.dma_start(out=b2_r[:, :], in_=b2[:, :]), 16)
                    sy.wait_ge(s_a2a.h, 1)
                    for dc in range(32):
                        src = u_out[dc // 4, (dc % 4) * 128:(dc % 4 + 1) * 128, :]
                        s4_u.inc(sy.dma_start(
                            out=u_sb[:, dc * OWN:(dc + 1) * OWN], in_=src), 16)
                    for j in range(8):
                        for ts in range(NTS):
                            m = j * NTS + ts
                            if m >= 2:
                                sy.wait_ge(s4_add.h, m - 1)
                            s4_x[m % 2].inc(sy.dma_start(
                                out=x_rd[m % 2][:, :],
                                in_=x_own[ts * 128:(ts + 1) * 128,
                                          j * 512:(j + 1) * 512]), 16)
                            s4_ffn[m % 2].inc(sy.dma_start(
                                out=ffn_rd[m % 2][:, :],
                                in_=ffn_d[ts * 128:(ts + 1) * 128,
                                          j * 512:(j + 1) * 512]), 16)

                @blk.gpsimd
                def _(g):
                    for j in range(8):
                        if j >= 3:
                            g.wait_ge(s4_mm.h, NTS * (j - 2))
                        for dc in range(32):
                            s4_wo[j % 3].inc(g.dma_start(
                                out=wo_sb[j % 3][:, dc * 512:(dc + 1) * 512],
                                in_=wo_t[dc * 128:(dc + 1) * 128,
                                         j * 512:(j + 1) * 512]), 16)

                @blk.vector
                def _(v):
                    for j in range(8):
                        for ts in range(NTS):
                            m = j * NTS + ts
                            v.wait_ge(s4_mm.h, m + 1)
                            v.wait_ge(s4_bbe.h, 8)
                            v.tensor_tensor(t1b[m % 2][:, :], pq[ts % 4][:, :],
                                            bbbc[:, j * 512:(j + 1) * 512],
                                            op=ALU.add)
                            v.drain()
                            v.wait_ge(s4_ffn[m % 2].h, (m // 2 + 1) * 16)
                            v.tensor_tensor(t1b[m % 2][:, :], t1b[m % 2][:, :],
                                            ffn_rd[m % 2][:, :], op=ALU.add)
                            v.drain()
                            v.wait_ge(s4_x[m % 2].h, (m // 2 + 1) * 16)
                            if m >= 2:
                                v.wait_ge(s4_out[m % 2].h, (m // 2) * 16)
                            s4_add.inc(v.tensor_tensor(
                                toutb[m % 2][:, :], t1b[m % 2][:, :],
                                x_rd[m % 2][:, :], op=ALU.add))
                            v.drain()

                @blk.tensor
                def _(t):
                    t.wait_ge(s4_bb.h, 32)
                    for j in range(8):
                        t.matmul(pq[4][:, :], sb_ones_f[:, :],
                                 bbbc[0:1, j * 512:(j + 1) * 512],
                                 start=True, stop=False)
                        ins = t.matmul(pq[4][:, :], sb_ones_f[:, :],
                                       b2_r[0:1, j * 512:(j + 1) * 512],
                                       start=False, stop=True)
                        s4_bbm.inc(ins)
                        t.wait_ge(s4_bbe.h, j + 1)
                    t.wait_ge(s4_u.h, 32 * 16)
                    for j in range(8):
                        t.wait_ge(s4_wo[j % 3].h, (j // 3 + 1) * 32 * 16)
                        for ts in range(NTS):
                            m = j * NTS + ts
                            if m >= 4:
                                t.wait_ge(s4_add.h, m - 3)
                            ins = None
                            for dc in range(32):
                                ins = t.matmul(
                                    pq[ts % 4][:, :],
                                    u_sb[:, dc * OWN + ts * 128:
                                         dc * OWN + (ts + 1) * 128],
                                    wo_sb[j % 3][:, dc * 512:(dc + 1) * 512],
                                    start=(dc == 0), stop=(dc == 31))
                            s4_mm.inc(ins)

                @blk.scalar
                def _(sc):
                    for j in range(8):
                        sc.wait_ge(s4_bbm.h, j + 1)
                        s4_bbe.inc(sc.copy(bbbc[:, j * 512:(j + 1) * 512],
                                           pq[4][:, :]))
                    for j in range(8):
                        for ts in range(NTS):
                            m = j * NTS + ts
                            sc.wait_ge(s4_add.h, m + 1)
                            s4_out[m % 2].inc(sc.dma_start(
                                out=out_own[ts * 128:(ts + 1) * 128,
                                            j * 512:(j + 1) * 512],
                                in_=toutb[m % 2][:, :]), 16)
                    sc.wait_ge(s4_out[0].h, s4_out[0].n)
                    sc.wait_ge(s4_out[1].h, s4_out[1].n)

    return nc


def _shard_inputs(inputs):
    f = lambda a: np.ascontiguousarray(np.asarray(a, dtype=np.float32))
    x = f(inputs["L0_attn_x"]).reshape(T, D)
    Wq, Wk, Wv = f(inputs["L0_attn_Wq"]), f(inputs["L0_attn_Wk"]), f(inputs["L0_attn_Wv"])
    bq, bk, bv = f(inputs["L0_attn_bq"]), f(inputs["L0_attn_bk"]), f(inputs["L0_attn_bv"])
    Wo, bo = f(inputs["L0_attn_Wo"]), f(inputs["L0_attn_bo"])
    U1, V1 = f(inputs["L0_ffn_U1"]), f(inputs["L0_ffn_V1"])
    U2, V2 = f(inputs["L0_ffn_U2"]), f(inputs["L0_ffn_V2"])
    b1, b2 = f(inputs["L0_ffn_b1"]), f(inputs["L0_ffn_b2"])
    x_f = np.ascontiguousarray(x.T)
    x_q = np.ascontiguousarray(x_f.reshape(D, NCH, CHK).transpose(1, 0, 2))
    Wo_t = np.ascontiguousarray(Wo.T)
    in_maps = []
    for c in range(8):
        sl = slice(c * HSL, (c + 1) * HSL)
        in_maps.append({
            "x": x,
            "x_own": np.ascontiguousarray(x[c * OWN:(c + 1) * OWN]),
            "x_q": x_q,
            "xf_own": np.ascontiguousarray(x_f[:, c * OWN:(c + 1) * OWN]),
            "wq_t": np.ascontiguousarray(Wq[sl].T),
            "wk_t": np.ascontiguousarray(Wk[sl].T),
            "wv_t": np.ascontiguousarray(Wv[sl].T),
            "bq": np.ascontiguousarray(bq[sl]).reshape(1, HSL),
            "bk": np.ascontiguousarray(bk[sl]).reshape(1, HSL),
            "bv": np.ascontiguousarray(bv[sl]).reshape(1, HSL),
            "wo_t": Wo_t,
            "bo": bo.reshape(1, D),
            "u1": U1, "v1": V1, "u2": U2, "v2": V2,
            "b1": b1.reshape(1, 2 * INTER), "b2": b2.reshape(1, D),
        })
    return in_maps


def kernel(**inputs):
    nc = build()
    in_maps = _shard_inputs(inputs)
    res = run_bass_kernel_spmd(nc, in_maps, core_ids=list(range(8)))
    out = np.concatenate(
        [np.asarray(res.results[c]["out_own"], dtype=np.float32)
         for c in range(8)], axis=0)
    return np.ascontiguousarray(out.reshape(2, SEQ, D))


# version 13 (final)
# speedup vs baseline: 1.1347x; 1.0012x over previous
import numpy as np
import ml_dtypes
from contextlib import ExitStack

import concourse.bass as bass
import concourse.mybir as mybir
from concourse.bass_utils import run_bass_kernel_spmd
from concourse.aot_env import allow_ici_all_to_all

F32 = mybir.dt.float32
BF16 = mybir.dt.bfloat16
AF = mybir.ActivationFunctionType
ALU = mybir.AluOpType
AX = mybir.AxisListType

D = 4096
T = 8192
SEQ = 4096
HSL = 512
NHL = 4
HD = 128
OWN = 1024
RANK = 1024
INTER = 11008
NFC = D // 128
CHK = 256
NCH = T // CHK
TPC = CHK // 128
OTC = 10
NLT = T // 128
NTILE = NLT + 8
INV_SQRT_HD = float(1.0 / np.sqrt(128.0))
EPS = 1e-5


class SemC:

    def __init__(self, handle):
        self.h = handle
        self.n = 0

    def inc(self, instr, k=1):
        instr.then_inc(self.h, k)
        self.n += k
        return self.n


def build(dump=()):
    nc = bass.Bass("TRN2", target_bir_lowering=False)

    x = nc.dram_tensor("x", [T, D], F32, kind="ExternalInput")
    x_own = nc.dram_tensor("x_own", [OWN, D], F32, kind="ExternalInput")
    x_q = nc.dram_tensor("x_q", [NCH, D, CHK], F32, kind="ExternalInput")
    xf_own = nc.dram_tensor("xf_own", [D, OWN], F32, kind="ExternalInput")
    wq_t = nc.dram_tensor("wq_t", [D, HSL], F32, kind="ExternalInput")
    wk_t = nc.dram_tensor("wk_t", [D, HSL], F32, kind="ExternalInput")
    wv_t = nc.dram_tensor("wv_t", [D, HSL], F32, kind="ExternalInput")
    bq = nc.dram_tensor("bq", [1, HSL], F32, kind="ExternalInput")
    bk = nc.dram_tensor("bk", [1, HSL], F32, kind="ExternalInput")
    bv = nc.dram_tensor("bv", [1, HSL], F32, kind="ExternalInput")
    wo_t = nc.dram_tensor("wo_t", [D, D], F32, kind="ExternalInput")
    bo = nc.dram_tensor("bo", [1, D], F32, kind="ExternalInput")
    u1 = nc.dram_tensor("u1", [D, RANK], F32, kind="ExternalInput")
    v1 = nc.dram_tensor("v1", [RANK, 2 * INTER], F32, kind="ExternalInput")
    u2 = nc.dram_tensor("u2", [INTER, RANK], F32, kind="ExternalInput")
    v2 = nc.dram_tensor("v2", [RANK, D], F32, kind="ExternalInput")
    b1 = nc.dram_tensor("b1", [1, 2 * INTER], F32, kind="ExternalInput")
    b2 = nc.dram_tensor("b2", [1, D], F32, kind="ExternalInput")
    out_own = nc.dram_tensor("out_own", [OWN, D], F32, kind="ExternalOutput")

    def scratch(name, shape, dtype):
        kind = "ExternalOutput" if name in dump else "Internal"
        return nc.dram_tensor(name, shape, dtype, kind=kind)

    q_d = scratch("q_d", [HSL, T], BF16)
    k_d = scratch("k_d", [HSL, T], BF16)
    v_d = scratch("v_d", [T, HSL], BF16)
    mrown_d = scratch("mrown_d", [1, 2 * OWN], BF16)
    v1_p = scratch("v1_p", [2 * (INTER // 128), 128, RANK], BF16)
    u_in = scratch("u_in", [8, HSL, OWN], BF16)
    u_out = scratch("u_out", [8, HSL, OWN], BF16)
    ffn_d = scratch("ffn_d", [OWN, D], BF16)

    ones_bcol = nc.inline_tensor(np.ones((128, 1), dtype=ml_dtypes.bfloat16),
                                 "ones_bcol")
    ones_frow = nc.inline_tensor(np.ones((1, 128), dtype=np.float32),
                                 "ones_frow")
    ones_frb = nc.inline_tensor(np.ones((1, 128), dtype=ml_dtypes.bfloat16),
                                "ones_frb")
    tri = np.triu(np.ones((128, 128), np.float32)).astype(ml_dtypes.bfloat16)
    tri_t = nc.inline_tensor(tri, "tri_t")
    ident_f32 = nc.inline_tensor(np.eye(128, dtype=np.float32), "ident_f32")

    with ExitStack() as top:
        sem = lambda name: SemC(top.enter_context(nc.semaphore(name)))

        sb_ones_b = nc.alloc_sbuf_tensor("sb_ones_b", [128, 1], BF16)
        sb_ones_f = nc.alloc_sbuf_tensor("sb_ones_f", [1, 128], F32)
        sb_tri = nc.alloc_sbuf_tensor("sb_tri", [128, 128], BF16)
        sb_bqkv = nc.alloc_sbuf_tensor("sb_bqkv", [128, 12], F32)
        sb_id32 = nc.alloc_sbuf_tensor("sb_id32", [128, 128], F32)
        sb_onesrb = nc.alloc_sbuf_tensor("sb_onesrb", [1, 128], BF16)

        s_const = sem("s_const")
        s_id32 = sem("s_id32")

        s_wt = [sem("s_wtq"), sem("s_wtk"), sem("s_wtv")]
        s_bgv = sem("s_bgv")
        s_bv = sem("s_bv")
        s_bprep = sem("s_bprep")
        s_bcols = sem("s_bcols")
        s_ldx = [sem(f"s_ldx{i}") for i in range(4)]
        s_st = sem("s_st")
        s_ln1 = sem("s_ln1")
        s_ln2 = sem("s_ln2")
        s_mrt = sem("s_mrt")
        s_mre = sem("s_mre")
        s_bcmm = sem("s_bcmm")
        s_bce = sem("s_bce")
        s_xf = [sem(f"s_xf{i}") for i in range(3)]
        s_mrst = sem("s_mrst")
        s_xn = sem("s_xn")
        s_mm1 = sem("s_mm1")
        s_evq = sem("s_evq")
        s_evv = sem("s_evv")
        s_qstk = sem("s_qstk")
        s_qstv = sem("s_qstv")

        NCHA = NCH + (OWN // CHK)

        with ExitStack() as ph1:
            sb = lambda n, s, d: ph1.enter_context(nc.sbuf_tensor(n, s, d))
            pt = lambda n, s: ph1.enter_context(nc.psum_tensor(n, s, F32))
            w_qT = sb("w_qT", [128, NFC * 512], BF16)
            w_kT = sb("w_kT", [128, NFC * 512], BF16)
            w_vT = sb("w_vT", [128, NFC * 512], BF16)
            xfb = [sb(f"xfb{i}", [128, NFC * 128], F32) for i in range(3)]
            xnb = [sb(f"xnb{i}", [128, NFC * CHK], BF16) for i in range(2)]
            xb = [sb(f"xb{i}", [128, D // 4], F32) for i in range(4)]
            stg = [sb(f"stg{i}", [128, CHK], BF16) for i in range(8)]
            stgv = [sb(f"stgv{i}", [128, 512], BF16) for i in range(2)]
            stats = [sb(f"stats{i}", [128, 48], F32) for i in range(2)]
            mv = [sb(f"mv{i}", [128, 2], F32) for i in range(4)]
            veps = [sb(f"veps{i}", [128, 1], F32) for i in range(2)]
            rstd = [sb(f"rstd{i}", [128, 1], F32) for i in range(2)]
            mr_sb = [sb(f"mr_sb{i}", [1, 512], BF16) for i in range(2)]
            mbc = [sb(f"mbc1_{i}", [128, 256], BF16) for i in range(2)]
            rbc1 = [sb(f"rbc1_{i}", [128, 256], BF16) for i in range(2)]
            sb_bv3 = sb("sb_bv3", [1, HSL], F32)
            ps = [pt(f"p1_{i}", [128, 512]) for i in range(4)]
            ps_b = pt("p1_b", [128, 512])
            ps_mr = ph1.enter_context(nc.psum_tensor("ps_mr", [1, 512], F32))
            ps_bc = pt("ps_bc", [128, 512])

            def qt_src(q):
                i, qq = q // 4, q % 4
                cs = slice(qq * (D // 4), (qq + 1) * (D // 4))
                if i < NLT:
                    return x[i * 128:(i + 1) * 128, cs]
                j = i - NLT
                return x_own[j * 128:(j + 1) * 128, cs]

            evac_engine = []
            for c in range(NCH):
                evac_engine += ['A'] * 8 + ['V'] * 2
            actc = np.cumsum([1 if t == 'A' else 0 for t in evac_engine])
            dvec = np.cumsum([1 if t == 'V' else 0 for t in evac_engine])

            with nc.Block() as blk:
                @blk.gpsimd
                def _(g):
                    s_const.inc(g.dma_start(out=sb_ones_b[:, :],
                                            in_=ones_bcol.ap()), 16)
                    s_const.inc(g.dma_start(out=sb_ones_f[:, :],
                                            in_=ones_frow.ap()), 16)
                    s_const.inc(g.dma_start(out=sb_tri[:, :],
                                            in_=tri_t.ap()), 16)
                    s_id32.inc(g.dma_start(out=sb_id32[:, :],
                                           in_=ident_f32.ap()), 16)
                    s_id32.inc(g.dma_start(out=sb_onesrb[:, :],
                                           in_=ones_frb.ap()), 16)
                    for wi, (wsb, wsrc) in enumerate(
                            ((w_qT, wq_t), (w_kT, wk_t), (w_vT, wv_t))):
                        for f in range(NFC):
                            s_wt[wi].inc(g.dma_start(
                                out=wsb[:, f * 512:(f + 1) * 512],
                                in_=wsrc[f * 128:(f + 1) * 128, :]), 16)
                    bg = []
                    for gu in range(2):
                        for ig in range(INTER // 128):
                            gidx = ig * 2 + gu
                            i0 = gu * INTER + ig * 128
                            for half in range(2):
                                srcv = v1[half * 512:(half + 1) * 512,
                                          i0:i0 + 128]
                                srcv = srcv.rearrange("(rc p) i -> p rc i",
                                                      p=128)
                                dstv = v1_p[gidx, :,
                                            half * 512:(half + 1) * 512]
                                dstv = dstv.rearrange("p (rc i) -> p rc i",
                                                      i=128)
                                bg.append((s_bgv, dstv, srcv))
                    bgi = [0]

                    def bg_pump(k):
                        for _ in range(k):
                            if bgi[0] < len(bg):
                                s_, dst, srcb = bg[bgi[0]]
                                s_.inc(g.dma_start(out=dst, in_=srcb), 16)
                                bgi[0] += 1

                    for c in range(NCH):
                        g.wait_ge(s_bce.h, 2 * (c + 1))
                        for h in range(2):
                            gix = 2 * c + h
                            g.wait_ge(s_xf[gix % 3].h, (gix // 3 + 1) * 16 * 16)
                            xf3 = xfb[gix % 3][:, :].rearrange(
                                "p (f t) -> p f t", t=CHK)
                            mslice = mbc[c % 2][:, :]
                            m3 = bass.AP(mslice.tensor, mslice.offset,
                                         [list(mslice.ap[0]), [0, NFC // 2],
                                          [1, CHK]])
                            rslice = rbc1[c % 2][:, :]
                            r3 = bass.AP(rslice.tensor, rslice.offset,
                                         [list(rslice.ap[0]), [0, NFC // 2],
                                          [1, CHK]])
                            xn3 = xnb[c % 2][:, h * 16 * CHK:
                                             (h + 1) * 16 * CHK].rearrange(
                                "p (f t) -> p f t", t=CHK)
                            g.tensor_tensor(xf3, xf3, m3, op=ALU.subtract)
                            g.drain()
                            s_xn.inc(g.tensor_tensor(xn3, xf3, r3,
                                                     op=ALU.mult))
                        bg_pump(12)
                    bg_pump(len(bg))
                    g.wait_ge(s_bgv.h, s_bgv.n)

                @blk.sync
                def _(sy):
                    s_bv.inc(sy.dma_start(out=sb_bv3[:, :], in_=bq[:, :]), 16)
                    sy.wait_ge(s_bprep.h, 1)
                    s_bv.inc(sy.dma_start(out=sb_bv3[:, :], in_=bk[:, :]), 16)
                    sy.wait_ge(s_bprep.h, 2)
                    s_bv.inc(sy.dma_start(out=sb_bv3[:, :], in_=bv[:, :]), 16)
                    for q in range(8):
                        if q >= 4:
                            sy.wait_ge(s_st.h, q - 3)
                        s_ldx[q % 4].inc(sy.dma_start(out=xb[q % 4][:, :],
                                                      in_=qt_src(q)), 16)
                    for c in range(NCHA):
                        if c < NCH:
                            for h in range(2):
                                gix = 2 * c + h
                                if gix >= 3:
                                    sy.wait_ge(s_xn.h, gix - 2)
                                for fq in range(NFC // 2):
                                    f = h * 16 + fq
                                    s_xf[gix % 3].inc(sy.dma_start(
                                        out=xfb[gix % 3][:, fq * CHK:
                                                         (fq + 1) * CHK],
                                        in_=x_q[c, f * 128:(f + 1) * 128, :]),
                                        16)
                        for i in range(TPC * c + 2, TPC * c + 2 + TPC):
                            if i < NTILE:
                                for q in range(4 * i, 4 * i + 4):
                                    if q >= 4:
                                        sy.wait_ge(s_st.h, q - 3)
                                    s_ldx[q % 4].inc(sy.dma_start(
                                        out=xb[q % 4][:, :],
                                        in_=qt_src(q)), 16)

                @blk.vector
                def _(v):
                    def v_evacs(c):
                        if c >= 1:
                            v.wait_ge(s_qstv.h, 2 * c * 16)
                        for j in range(2):
                            gt = c * OTC + 8 + j
                            v.wait_ge(s_mm1.h, gt + 1)
                            s_evv.inc(v.tensor_copy(stgv[j][:, :],
                                                    ps[gt % 4][:, :]))

                    for p in range(NCHA):
                        i0, i1 = 2 * p, 2 * p + 1
                        if p >= 1:
                            v.wait_ge(s_ln2.h, i0)
                        if p >= 2:
                            v.wait_ge(s_mrt.h, p - 1)
                        for i in (i0, i1):
                            sl = i % 2
                            for qq in range(4):
                                q = 4 * i + qq
                                v.wait_ge(s_ldx[q % 4].h, (q // 4 + 1) * 16)
                                ins = None
                                for j in range(2):
                                    ins = v.bn_stats(
                                        stats[sl][:, 12 * qq + 6 * j:
                                                  12 * qq + 6 * j + 6],
                                        xb[q % 4][:, 512 * j:512 * (j + 1)])
                                s_st.inc(ins)
                        v.drain()
                        for i in (i0, i1):
                            v.bn_aggr(mv[i % 4][:, :], stats[i % 2][:, :])
                        v.drain()
                        for i in (i0, i1):
                            v.tensor_scalar(veps[i % 2][:, :],
                                            mv[i % 4][:, 1:2],
                                            EPS, None, ALU.add)
                        v.drain()
                        s_ln1.inc(v.reciprocal(veps[i0 % 2][:, :],
                                               veps[i0 % 2][:, :]))
                        s_ln1.inc(v.reciprocal(veps[i1 % 2][:, :],
                                               veps[i1 % 2][:, :]))
                        if p >= 2 and p - 2 < NCH:
                            v_evacs(p - 2)
                    for c in range(NCHA - 2, NCH):
                        v_evacs(c)

                @blk.tensor
                def _(t):
                    t.wait_ge(s_const.h, 48)
                    for wi in range(3):
                        t.wait_ge(s_bv.h, (wi + 1) * 16)
                        ins = None
                        for dc in range(4):
                            ins = t.matmul(
                                ps_b[:, wi * 4 + dc:wi * 4 + dc + 1],
                                sb_bv3[:, dc * 128:(dc + 1) * 128],
                                sb_ones_f[:, 0:1],
                                start=True, stop=True)
                        s_bprep.inc(ins)
                    t.wait_ge(s_id32.h, 32)

                    def mr_transp(c):
                        t.wait_ge(s_ln1.h, 2 * c + 2)
                        t.wait_ge(s_ln2.h, 2 * c + 2)
                        if c >= 1:
                            t.wait_ge(s_mre.h, c)
                        t.transpose(ps_mr[0:1, 0:128], mv[(2 * c) % 4][:, 0:1],
                                    sb_id32[:, :])
                        t.transpose(ps_mr[0:1, 128:256],
                                    mv[(2 * c + 1) % 4][:, 0:1], sb_id32[:, :])
                        t.transpose(ps_mr[0:1, 256:384], rstd[0][:, :],
                                    sb_id32[:, :])
                        ins = t.transpose(ps_mr[0:1, 384:512], rstd[1][:, :],
                                          sb_id32[:, :])
                        s_mrt.inc(ins)

                    def mr_bcast(c):
                        t.wait_ge(s_mre.h, c + 1)
                        if c >= 1:
                            t.wait_ge(s_bce.h, 2 * c)
                        t.matmul(ps_bc[:, 0:256], sb_onesrb[:, :],
                                 mr_sb[c % 2][0:1, 0:256],
                                 start=True, stop=True)
                        ins = t.matmul(ps_bc[:, 256:512], sb_onesrb[:, :],
                                       mr_sb[c % 2][0:1, 256:512],
                                       start=True, stop=True)
                        s_bcmm.inc(ins)

                    def qkv(c):
                        t.wait_ge(s_xn.h, 2 * (c + 1))
                        for j in range(OTC):
                            gt = c * OTC + j
                            b = gt % 4
                            if c == 0 and j in (0, 4, 8):
                                t.wait_ge(s_wt[j // 4].h, 32 * 16)
                            if gt >= 4:
                                pt_ = gt - 4
                                if evac_engine[pt_] == 'A':
                                    t.wait_ge(s_evq.h, int(actc[pt_]))
                                else:
                                    t.wait_ge(s_evv.h, int(dvec[pt_]))
                            if j < 8:
                                wsb = w_qT if j < 4 else w_kT
                                dc = j % 4
                                for f in range(NFC):
                                    ins = t.matmul(
                                        ps[b][:, 0:CHK],
                                        wsb[:, f * 512 + dc * 128:
                                            f * 512 + (dc + 1) * 128],
                                        xnb[c % 2][:, f * CHK:(f + 1) * CHK],
                                        start=(f == 0), stop=(f == NFC - 1))
                            else:
                                ts = j - 8
                                for f in range(NFC):
                                    ins = t.matmul(
                                        ps[b][:, :],
                                        xnb[c % 2][:, f * CHK + ts * 128:
                                                   f * CHK + (ts + 1) * 128],
                                        w_vT[:, f * 512:(f + 1) * 512],
                                        start=(f == 0), stop=(f == NFC - 1))
                            s_mm1.inc(ins)

                    for c in range(NCHA):
                        if c >= 1 and c - 1 < NCH:
                            mr_bcast(c - 1)
                        mr_transp(c)
                        if c >= 2 and c - 2 < NCH:
                            qkv(c - 2)
                    for c in range(NCHA - 2, NCH):
                        qkv(c)

                @blk.scalar
                def _(sc):
                    def qk_out(c):
                        sc.wait_ge(s_bcols.h, 1)
                        if c >= 1:
                            sc.wait_ge(s_qstk.h, 8 * c * 16)
                        for j in range(8):
                            gt = c * OTC + j
                            sc.wait_ge(s_mm1.h, gt + 1)
                            bias = sb_bqkv[:, j:j + 1]
                            n = s_evq.inc(sc.activation(
                                stg[j][:, 0:CHK], ps[gt % 4][:, 0:CHK],
                                AF.Identity, bias=bias))
                            sc.wait_ge(s_evq.h, n)
                            dst = (q_d if j < 4 else k_d)
                            dc = j % 4
                            s_qstk.inc(sc.dma_start(
                                out=dst[dc * 128:(dc + 1) * 128,
                                        c * CHK:(c + 1) * CHK],
                                in_=stg[j][:, 0:CHK]), 16)

                    def v_out(c):
                        for j in range(2):
                            gt = c * OTC + 8 + j
                            sc.wait_ge(s_evv.h, int(dvec[gt]))
                            s_qstv.inc(sc.dma_start(
                                out=v_d[c * CHK + j * 128:
                                        c * CHK + (j + 1) * 128, :],
                                in_=stgv[j][:, :]), 16)

                    sc.wait_ge(s_bprep.h, 3)
                    s_bcols.inc(sc.copy(sb_bqkv[:, :], ps_b[:, 0:12]))
                    for c in range(NCHA):
                        if 1 <= c < NCH:
                            sc.wait_ge(s_bcmm.h, c)
                            if c >= 3:
                                sc.wait_ge(s_xn.h, 2 * (c - 2))
                            s_bce.inc(sc.copy(mbc[(c - 1) % 2][:, :],
                                              ps_bc[:, 0:256]))
                            s_bce.inc(sc.copy(rbc1[(c - 1) % 2][:, :],
                                              ps_bc[:, 256:512]))
                        for i in (2 * c, 2 * c + 1):
                            sl = i % 2
                            sc.wait_ge(s_ln1.h, i + 1)
                            if i >= 4:
                                sc.wait_ge(s_mrt.h, i // 2)
                            s_ln2.inc(sc.activation(rstd[sl][:, :],
                                                    veps[sl][:, :], AF.Sqrt))
                        sc.wait_ge(s_mrt.h, c + 1)
                        if c < NCH:
                            n = s_mre.inc(sc.copy(mr_sb[c % 2][:, :],
                                                  ps_mr[0:1, :]))
                        else:
                            co = c - NCH
                            sc.wait_ge(s_bcmm.h, NCH)
                            if co == 0:
                                sc.wait_ge(s_xn.h, 2 * (c - 2))
                                s_bce.inc(sc.copy(mbc[(NCH - 1) % 2][:, :],
                                                  ps_bc[:, 0:256]))
                                s_bce.inc(sc.copy(rbc1[(NCH - 1) % 2][:, :],
                                                  ps_bc[:, 256:512]))
                            if co >= 2:
                                sc.wait_ge(s_mrst.h, co * 32)
                            n = s_mre.inc(sc.copy(mr_sb[c % 2][:, :],
                                                  ps_mr[0:1, :]))
                            sc.wait_ge(s_mre.h, n)
                            s_mrst.inc(sc.dma_start(
                                out=mrown_d[0:1, co * 256:(co + 1) * 256],
                                in_=mr_sb[c % 2][0:1, 0:256]), 16)
                            s_mrst.inc(sc.dma_start(
                                out=mrown_d[0:1, OWN + co * 256:
                                            OWN + (co + 1) * 256],
                                in_=mr_sb[c % 2][0:1, 256:512]), 16)
                        if c >= 3 and c - 3 < NCH:
                            v_out(c - 3)
                        if c >= 2 and c - 2 < NCH:
                            qk_out(c - 2)
                    for c in range(NCHA - 3, NCH):
                        v_out(c)
                    for c in range(NCHA - 2, NCH):
                        qk_out(c)
                    sc.wait_ge(s_qstk.h, s_qstk.n)
                    sc.wait_ge(s_qstv.h, s_qstv.n)
                    sc.wait_ge(s_mrst.h, s_mrst.n)

        s_qkvld = [sem("s_qkvld0"), sem("s_qkvld1")]
        s_mmA = sem("s_mmA")
        s_maxA = sem("s_maxA")
        s_cred = sem("s_cred")
        s_ctr = sem("s_ctr")
        s_cneg = sem("s_cneg")
        s_cbc = sem("s_cbc")
        s_cbias = sem("s_cbias")
        s_mmB = sem("s_mmB")
        s_expB = sem("s_expB")
        s_mskB = sem("s_mskB")
        s_pvmm = sem("s_pvmm")
        s_rcp = sem("s_rcp")
        s_rbcp = sem("s_rbcp")
        s_rbcs = sem("s_rbcs")
        s_mul = sem("s_mul")
        s_ustg = sem("s_ustg")
        s_uin = [sem("s_uin0"), sem("s_uin1")]
        s_a2a = sem("s_a2a")


        NIT = 8
        NQG = SEQ // 512
        KBS = SEQ // 128
        CPI = sum(4 * qg + 4 for qg in range(NQG))
        _chk_tot = []
        _acc = 0
        for _it in range(NIT):
            row = []
            for _qg in range(NQG):
                _acc += 4 * _qg + 4
                row.append(_acc)
            _chk_tot.append(row)

        _n_passA = sum(4 * (qg + 1) for qg in range(NQG))
        with ExitStack() as ph2:
            sb = lambda n, s, d: ph2.enter_context(nc.sbuf_tensor(n, s, d))
            qh = [sb(f"qh{i}", [128, SEQ], BF16) for i in range(2)]
            kh = [sb(f"kh{i}", [128, SEQ], BF16) for i in range(2)]
            vh = [sb(f"vh{i}", [128, SEQ], BF16) for i in range(2)]
            tmpmax = [sb(f"tmpmax{i}", [128, 160], F32) for i in range(2)]
            maxred = [sb(f"maxred{i}", [128, 1], F32) for i in range(2)]
            csc = [sb(f"csc{i}", [1, 2], F32) for i in range(2)]
            cb = [sb(f"cb{i}", [128, 1], F32) for i in range(2)]
            pbuf = [sb(f"pbuf{i}", [128, 512], BF16) for i in range(4)]
            rsb = [sb(f"rsb{i}", [1, 512], F32) for i in range(2)]
            rbc = [sb(f"rbc{i}", [128, 512], F32) for i in range(2)]
            umul = [sb(f"umul{i}", [128, 512], F32) for i in range(2)]
            ustg = [sb(f"ustg{i}", [128, 512], BF16) for i in range(2)]
            SA = ph2.enter_context(nc.psum_tensor("SA", [128, 1024], F32))
            SB = ph2.enter_context(nc.psum_tensor("SBp", [128, 1024], F32))
            PV = [ph2.enter_context(nc.psum_tensor(f"PV{i}", [128, 512], F32))
                  for i in range(2)]
            SUM = ph2.enter_context(nc.psum_tensor("SUMp", [1, 512], F32))
            RBC = ph2.enter_context(nc.psum_tensor("RBCp", [128, 512], F32))

            _msk_tot = {}
            _mcnt = 0
            _gc = 0
            for _it in range(NIT):
                for _qg in range(NQG):
                    for _r in range(4 * _qg + 4):
                        if _r >= 4 * _qg:
                            _j = _r - 4 * _qg
                            _mcnt += _j + 1
                            _msk_tot[_gc] = _mcnt
                        _gc += 1

            with nc.Block() as blk:
                @blk.sync
                def _(sy):
                    for it in range(NIT):
                        s, h = it // 4, it % 4
                        if it >= 2:
                            sy.wait_ge(s_pvmm.h, CPI * (it - 1))
                        sl = it % 2
                        s_qkvld[sl].inc(sy.dma_start(
                            out=qh[sl][:, :],
                            in_=q_d[h * 128:(h + 1) * 128,
                                    s * SEQ:(s + 1) * SEQ]), 16)
                        s_qkvld[sl].inc(sy.dma_start(
                            out=kh[sl][:, :],
                            in_=k_d[h * 128:(h + 1) * 128,
                                    s * SEQ:(s + 1) * SEQ]), 16)
                        for kb in range(KBS):
                            s_qkvld[sl].inc(sy.dma_start(
                                out=vh[sl][:, kb * 128:(kb + 1) * 128],
                                in_=v_d[s * SEQ + kb * 128:
                                        s * SEQ + (kb + 1) * 128,
                                        h * 128:(h + 1) * 128]), 16)

                @blk.tensor
                def _(t):
                    tA = 0
                    gc = 0
                    for it in range(NIT):
                        sl = it % 2
                        t.wait_ge(s_qkvld[sl].h, (it // 2 + 1) * (2 + KBS) * 16)
                        for qg in range(NQG):
                            for kb5 in range(qg + 1):
                                for qs in range(4):
                                    if tA >= 2:
                                        t.wait_ge(s_maxA.h, tA - 1)
                                    ins = t.matmul(
                                        SA[:, (tA % 2) * 512:(tA % 2 + 1) * 512],
                                        qh[sl][:, qg * 512 + qs * 128:
                                               qg * 512 + (qs + 1) * 128],
                                        kh[sl][:, kb5 * 512:(kb5 + 1) * 512],
                                        start=True, stop=True)
                                    s_mmA.inc(ins)
                                    tA += 1
                        t.wait_ge(s_id32.h, 32)
                        t.wait_ge(s_cred.h, it + 1)
                        s_ctr.inc(t.transpose(RBC[0:1, 0:128],
                                              maxred[sl][:, :], sb_id32[:, :]))
                        t.wait_ge(s_cneg.h, it + 1)
                        if it >= 1:
                            t.wait_ge(s_rbcs.h, NQG * it)
                        s_cbc.inc(t.matmul(RBC[:, 0:1], sb_ones_f[:, :],
                                           csc[sl][0:1, 1:2],
                                           start=True, stop=True))
                        for qg in range(NQG):
                            qgg = it * NQG + qg
                            nkb = 4 * qg + 4

                            def s_mm(r, gc_r):
                                if gc_r >= 2:
                                    t.wait_ge(s_expB.h, gc_r - 1)
                                ins = t.matmul(
                                    SB[:, (gc_r % 2) * 512:(gc_r % 2 + 1) * 512],
                                    kh[sl][:, r * 128:(r + 1) * 128],
                                    qh[sl][:, qg * 512:(qg + 1) * 512],
                                    start=True, stop=True)
                                s_mmB.inc(ins)

                            def pv_sum(r, gc_r):
                                if r >= 4 * qg:
                                    t.wait_ge(s_mskB.h, _msk_tot[gc_r])
                                else:
                                    t.wait_ge(s_expB.h, gc_r + 1)
                                if r == 0 and qgg >= 2:
                                    t.wait_ge(s_mul.h, qgg - 1)
                                t.matmul(PV[qg % 2][:, :],
                                         vh[sl][:, r * 128:(r + 1) * 128],
                                         pbuf[gc_r % 4][:, :],
                                         start=(r == 0), stop=(r == nkb - 1))
                                if r == 0 and qgg >= 1:
                                    t.wait_ge(s_rcp.h, qgg)
                                ins = t.matmul(SUM[0:1, :], sb_ones_b[:, :],
                                               pbuf[gc_r % 4][:, :],
                                               start=(r == 0), stop=(r == nkb - 1))
                                s_pvmm.inc(ins)

                            for r in range(nkb):
                                s_mm(r, gc + r)
                                if r >= 1:
                                    pv_sum(r - 1, gc + r - 1)
                            pv_sum(nkb - 1, gc + nkb - 1)
                            gc += nkb
                            t.wait_ge(s_rcp.h, qgg + 1)
                            t.wait_ge(s_cbias.h, it + 1)
                            if qgg >= 1:
                                t.wait_ge(s_rbcs.h, qgg)
                            s_rbcp.inc(t.matmul(RBC[:, :], sb_ones_f[:, :],
                                                rsb[qgg % 2][:, :],
                                                start=True, stop=True))

                @blk.vector
                def _(v):
                    tA = 0
                    for it in range(NIT):
                        sl = it % 2
                        for ti in range(_n_passA):
                            v.wait_ge(s_mmA.h, tA + 1)
                            s_maxA.inc(v.reduce_max(
                                tmpmax[sl][:, ti:ti + 1],
                                SA[:, (tA % 2) * 512:(tA % 2 + 1) * 512],
                                axis=AX.X))
                            tA += 1
                        v.drain()
                        s_cred.inc(v.reduce_max(maxred[sl][:, :],
                                                tmpmax[sl][:, 0:_n_passA],
                                                axis=AX.X))
                        v.drain()
                        v.wait_ge(s_ctr.h, it + 1)
                        v.reduce_max(csc[sl][0:1, 0:1], RBC[0:1, 0:128],
                                     axis=AX.X)
                        v.drain()
                        s_cneg.inc(v.tensor_scalar(
                            csc[sl][0:1, 1:2], csc[sl][0:1, 0:1],
                            -INV_SQRT_HD, None, ALU.mult))
                        v.drain()
                        for qg in range(NQG):
                            qgg = it * NQG + qg
                            v.wait_ge(s_pvmm.h, _chk_tot[it][qg])
                            s_rcp.inc(v.reciprocal(rsb[qgg % 2][:, :],
                                                   SUM[0:1, :]))
                            v.drain()
                            v.wait_ge(s_rbcs.h, qgg + 1)
                            if qgg >= 2:
                                v.wait_ge(s_ustg.h, qgg - 1)
                            s_mul.inc(v.tensor_tensor(
                                umul[qgg % 2][:, :], PV[qg % 2][:, :],
                                rbc[qgg % 2][:, :], op=ALU.mult))

                @blk.scalar
                def _(sc):
                    gc = 0
                    for it in range(NIT):
                        sl = it % 2
                        s, h = it // 4, it % 4
                        sc.wait_ge(s_cbc.h, it + 1)
                        n = s_cbias.inc(sc.copy(cb[sl][:, :], RBC[:, 0:1]))
                        sc.wait_ge(s_cbias.h, n)
                        for qg in range(NQG):
                            qgg = it * NQG + qg
                            nkb = 4 * qg + 4
                            for r in range(nkb):
                                sc.wait_ge(s_mmB.h, gc + 1)
                                if gc >= 4:
                                    sc.wait_ge(s_pvmm.h, gc - 3)
                                s_expB.inc(sc.activation(
                                    pbuf[gc % 4][:, :],
                                    SB[:, (gc % 2) * 512:(gc % 2 + 1) * 512],
                                    AF.Exp, bias=cb[sl][:, :],
                                    scale=INV_SQRT_HD))
                                gc += 1
                            sc.wait_ge(s_rbcp.h, qgg + 1)
                            s_rbcs.inc(sc.copy(rbc[qgg % 2][:, :], RBC[:, :]))
                            sc.wait_ge(s_mul.h, qgg + 1)
                            if qgg >= 2:
                                sc.wait_ge(s_uin[qgg % 2].h,
                                           (qgg // 2) * 16)
                            n = s_ustg.inc(sc.activation(
                                ustg[qgg % 2][:, :], umul[qgg % 2][:, :],
                                AF.Identity, bias=sb_bqkv[:, 8 + h:9 + h]))
                            sc.wait_ge(s_ustg.h, n)
                            tok0 = s * SEQ + qg * 512
                            jj = tok0 // OWN
                            off = tok0 % OWN
                            s_uin[qgg % 2].inc(sc.dma_start(
                                out=u_in[jj, h * 128:(h + 1) * 128,
                                         off:off + 512],
                                in_=ustg[qgg % 2][:, :]), 16)
                    sc.wait_ge(s_uin[0].h, s_uin[0].n)
                    sc.wait_ge(s_uin[1].h, s_uin[1].n)

                @blk.gpsimd
                def _(g):
                    gc = 0
                    mcnt = 0
                    for it in range(NIT):
                        for qg in range(NQG):
                            for r in range(4 * qg + 4):
                                if r >= 4 * qg:
                                    j = r - 4 * qg
                                    g.wait_ge(s_expB.h, gc + 1)
                                    for i in range(j):
                                        s_mskB.inc(g.memset(
                                            pbuf[gc % 4][:, i * 128:(i + 1) * 128],
                                            0.0))
                                        mcnt += 1
                                    s_mskB.inc(g.tensor_tensor(
                                        pbuf[gc % 4][:, j * 128:(j + 1) * 128],
                                        pbuf[gc % 4][:, j * 128:(j + 1) * 128],
                                        sb_tri[:, :], op=ALU.mult))
                                    mcnt += 1
                                    assert _msk_tot[gc] == mcnt
                                gc += 1
                    g.wait_ge(s_uin[0].h, s_uin[0].n)
                    g.wait_ge(s_uin[1].h, s_uin[1].n)
                    with allow_ici_all_to_all():
                        s_a2a.inc(g.collective_compute(
                            "AllToAll", ALU.bypass,
                            replica_groups=[[0, 1, 2, 3, 4, 5, 6, 7]],
                            ins=[u_in.ap().opt()],
                            outs=[u_out.ap().opt()]))

        NTH = OWN // 512
        NIC = INTER // 128
        NRC = RANK // 128
        s3_b1l = [sem("s3_b1l0"), sem("s3_b1l1")]
        s3_b1m = sem("s3_b1m")
        s3_b1e = sem("s3_b1e")
        s3_xno = sem("s3_xno")
        s3_xfo = [sem("s3_xfo0"), sem("s3_xfo1")]
        s3_mr = sem("s3_mr")
        s3_mrb = sem("s3_mrb")
        s3_mrbe = sem("s3_mrbe")
        s3_u1 = [sem("s3_u1a"), sem("s3_u1b")]
        s3_v1 = [sem(f"s3_v1{i}") for i in range(4)]
        s3_u2 = [sem("s3_u2a"), sem("s3_u2b")]
        s3_v2 = [sem("s3_v2a"), sem("s3_v2b")]
        s3_mm = sem("s3_mm")
        s3_evd = sem("s3_evd")
        s3_eva = sem("s3_eva")
        s3_glu = sem("s3_glu")
        s3_fst = [sem(f"s3_fst{i}") for i in range(4)]

        _b1loads = []
        _o = 0
        while _o < 2 * INTER:
            _w = min(2048, 2 * INTER - _o)
            _b1loads.append((_o, _w))
            _o += _w

        _cnt = 0
        _p_last = []
        for _th in range(NTH):
            for _f in range(NFC):
                _cnt += 1
        _cnt = NTH * NFC
        _hgrp_flat = []
        _h_mm = []
        _u2grp = []
        _z_mm = []
        _f_mm = []
        for _th in range(NTH):
            for _ig in range(NIC):
                _cnt += 1
                _g = _cnt
                _cnt += 1
                _u = _cnt
                _h_mm.append((_g, _u))
                _hgrp_flat += [_g, _u]
            for _i in range(NIC):
                _cnt += 1
                _u2grp.append(_cnt)
            _z_mm.append(_cnt)
            _row = []
            for _j in range(8):
                for _ts in range(4):
                    _cnt += 1
                    _row.append(_cnt)
            _f_mm.append(_row)
        _v1gate = {g: _hgrp_flat[g - 4] for g in range(4, 2 * NTH * NIC)}
        _u2gate = {g: _u2grp[g - 2] for g in range(2, NTH * NIC)}
        _v2gate = {g: _f_mm[(g - 2) // 8][((g - 2) % 8) * 4 + 3]
                   for g in range(2, NTH * 8)}
        _fevd = {}
        for _m in range(32 * NTH):
            _tm = _m // 32
            _fevd[_m] = 8 * NTH + 40 * _tm + 8 + (_m - 32 * _tm) + 1

        with ExitStack() as ph3:
            sb = lambda n, s, d: ph3.enter_context(nc.sbuf_tensor(n, s, d))
            pp = [ph3.enter_context(nc.psum_tensor(f"pp{i}", [128, 512], F32))
                  for i in range(8)]
            b1row = [sb(f"b1row{i}", [1, 2048], F32) for i in range(2)]
            b1c = sb("b1c", [128, 2 * NIC], F32)
            xnoT = [sb(f"xnoT{i}", [128, 512], BF16) for i in range(2)]
            xfo = [sb(f"xfo{i}", [128, 512], F32) for i in range(2)]
            mr3row = sb("mr3row", [1, 2 * OWN], BF16)
            mbc3 = sb("mbc3", [128, 512], BF16)
            rbc3 = sb("rbc3", [128, 512], BF16)
            u1sb = [sb(f"u1sb{i}", [128, RANK], BF16) for i in range(2)]
            v1sb = [sb(f"v1sb{i}", [128, RANK], BF16) for i in range(4)]
            u2sb = [sb(f"u2sb{i}", [128, RANK], BF16) for i in range(2)]
            v2sb = [sb(f"v2sb{i}", [128, RANK * 4], BF16) for i in range(2)]
            g_sb = [sb(f"g_sb{i}", [128, 512], F32) for i in range(2)]
            up_sb = [sb(f"up_sb{i}", [128, 512], F32) for i in range(2)]
            P_sb = sb("P_sb", [128, NRC * OWN], BF16)
            act_sb = sb("act_sb", [128, NIC * 512], BF16)
            Z_sb = sb("Z_sb", [128, NRC * 512], BF16)
            ffn_stg = [sb(f"ffn_stg{i}", [128, 512], BF16) for i in range(4)]

            with nc.Block() as blk:
                @blk.gpsimd
                def _(g):
                    for th in range(NTH):
                        g.wait_ge(s3_mrbe.h, 2 * (th + 1))
                        for f in range(NFC):
                            gi = th * NFC + f
                            g.wait_ge(s3_xfo[gi % 2].h, (gi // 2 + 1) * 16)
                            if gi >= 2:
                                g.wait_ge(s3_mm.h, gi - 1)
                            s3_u1[gi % 2].inc(g.dma_start(
                                out=u1sb[gi % 2][:, :],
                                in_=u1[f * 128:(f + 1) * 128, :]), 16)
                            g.tensor_tensor(xfo[gi % 2][:, :],
                                            xfo[gi % 2][:, :],
                                            mbc3[:, :], op=ALU.subtract)
                            g.drain()
                            s3_xno.inc(g.tensor_tensor(
                                xnoT[gi % 2][:, :], xfo[gi % 2][:, :],
                                rbc3[:, :], op=ALU.mult))
                    for th in range(NTH):
                        for i in range(NIC):
                            gi = th * NIC + i
                            if gi >= 2:
                                g.wait_ge(s3_mm.h, _u2gate[gi])
                            s3_u2[gi % 2].inc(g.dma_start(
                                out=u2sb[gi % 2][:, :],
                                in_=u2[i * 128:(i + 1) * 128, :]), 16)
                        for j in range(8):
                            gi = th * 8 + j
                            if gi >= 2:
                                g.wait_ge(s3_mm.h, _v2gate[gi])
                            for rc in range(NRC):
                                s3_v2[gi % 2].inc(g.dma_start(
                                    out=v2sb[gi % 2][:, rc * 512:(rc + 1) * 512],
                                    in_=v2[rc * 128:(rc + 1) * 128,
                                           j * 512:(j + 1) * 512]), 16)

                @blk.sync
                def _(sy):
                    s3_mr.inc(sy.dma_start(out=mr3row[:, :],
                                           in_=mrown_d[:, :]), 16)
                    nch_done = 0
                    for li, (off, wid) in enumerate(_b1loads):
                        if li >= 2:
                            sy.wait_ge(s3_b1m.h, nch_done)
                        s3_b1l[li % 2].inc(sy.dma_start(
                            out=b1row[li % 2][:, 0:wid],
                            in_=b1[:, off:off + wid]), 16)
                        if li >= 1:
                            nch_done += _b1loads[li - 1][1] // 128
                    for th in range(NTH):
                        for f in range(NFC):
                            gi = th * NFC + f
                            if gi >= 2:
                                sy.wait_ge(s3_xno.h, gi - 1)
                                sy.wait_ge(s3_mm.h, gi - 1)
                            s3_xfo[gi % 2].inc(sy.dma_start(
                                out=xfo[gi % 2][:, :],
                                in_=xf_own[f * 128:(f + 1) * 128,
                                           th * 512:(th + 1) * 512]), 16)
                    sy.wait_ge(s_bgv.h, s_bgv.n)
                    for th in range(NTH):
                        for ig in range(NIC):
                            for gu in range(2):
                                gidx = (th * NIC + ig) * 2 + gu
                                if gidx >= 4:
                                    sy.wait_ge(s3_mm.h, _v1gate[gidx])
                                pidx = ig * 2 + gu
                                s3_v1[gidx % 4].inc(sy.dma_start(
                                    out=v1sb[gidx % 4][:, :],
                                    in_=v1_p[pidx, :, :]), 16)

                @blk.tensor
                def _(t):
                    mmn = 0

                    def bcast(th):
                        if th >= 1:
                            t.wait_ge(s3_mrbe.h, 2 * th)
                            t.wait_ge(s3_xno.h, th * NFC)
                        t.matmul(pp[6][:, :], sb_onesrb[:, :],
                                 mr3row[0:1, th * 512:(th + 1) * 512],
                                 start=True, stop=True)
                        ins = t.matmul(pp[7][:, :], sb_onesrb[:, :],
                                       mr3row[0:1, OWN + th * 512:
                                              OWN + (th + 1) * 512],
                                       start=True, stop=True)
                        s3_mrb.inc(ins)

                    t.wait_ge(s3_mr.h, 16)
                    bcast(0)
                    nch = 0
                    for li, (off, wid) in enumerate(_b1loads):
                        t.wait_ge(s3_b1l[li % 2].h, (li // 2 + 1) * 16)
                        ins = None
                        for cc in range(wid // 128):
                            ins = t.matmul(
                                pp[0][:, nch:nch + 1],
                                b1row[li % 2][0:1, cc * 128:(cc + 1) * 128],
                                sb_ones_f[:, 0:1], start=True, stop=True)
                            nch += 1
                        s3_b1m.inc(ins, wid // 128)
                    t.wait_ge(s3_b1e.h, 1)
                    for th in range(NTH):
                        if th >= 1:
                            t.wait_ge(s3_evd.h, 8 * th)
                            bcast(th)
                        for f in range(NFC):
                            gi = th * NFC + f
                            t.wait_ge(s3_xno.h, gi + 1)
                            t.wait_ge(s3_u1[gi % 2].h, (gi // 2 + 1) * 16)
                            ins = None
                            for rc in range(NRC):
                                ins = t.matmul(
                                    pp[rc][:, :],
                                    u1sb[gi % 2][:, rc * 128:(rc + 1) * 128],
                                    xnoT[gi % 2][:, :],
                                    start=(f == 0), stop=(f == NFC - 1))
                            mmn = s3_mm.inc(ins)
                    _Pev = 8 * NTH
                    for th in range(NTH):
                        for ig in range(NIC):
                            gidx2 = (th * NIC + ig) * 2
                            for gu in range(2):
                                gidx = gidx2 + gu
                                slot = gidx % 4
                                bank = pp[2 * (ig % 2) + gu]
                                t.wait_ge(s3_v1[gidx % 4].h,
                                          (gidx // 4 + 1) * 16)
                                if ig < 2:
                                    t.wait_ge(s3_evd.h, 8 * NTH + 40 * th)
                                if gidx >= 4:
                                    t.wait_ge(s3_eva.h, gidx - 3)
                                ins = None
                                for rc in range(NRC):
                                    ins = t.matmul(
                                        bank[:, :],
                                        v1sb[slot][:, rc * 128:(rc + 1) * 128],
                                        P_sb[:, rc * OWN + th * 512:
                                             rc * OWN + (th + 1) * 512],
                                        start=(rc == 0), stop=(rc == NRC - 1))
                                mmn = s3_mm.inc(ins)
                        for i in range(NIC):
                            gi = th * NIC + i
                            t.wait_ge(s3_u2[gi % 2].h, (gi // 2 + 1) * 16)
                            t.wait_ge(s3_glu.h, th * NIC + i + 1)
                            if i == 0:
                                t.wait_ge(s3_eva.h, 2 * NIC * (th + 1))
                            ins = None
                            for rc in range(NRC):
                                ins = t.matmul(
                                    pp[rc][:, :],
                                    u2sb[gi % 2][:, rc * 128:(rc + 1) * 128],
                                    act_sb[:, i * 512:(i + 1) * 512],
                                    start=(i == 0), stop=(i == NIC - 1))
                            mmn = s3_mm.inc(ins)
                        t.wait_ge(s3_evd.h, 8 * NTH + 40 * th + 8)
                        for j in range(8):
                            gi = th * 8 + j
                            t.wait_ge(s3_v2[gi % 2].h, (gi // 2 + 1) * 8 * 16)
                            for ts in range(4):
                                m = gi * 4 + ts
                                if m >= 4:
                                    t.wait_ge(s3_evd.h, _fevd[m - 4])
                                ins = None
                                for rc in range(NRC):
                                    ins = t.matmul(
                                        pp[ts][:, :],
                                        Z_sb[:, rc * 512 + ts * 128:
                                             rc * 512 + (ts + 1) * 128],
                                        v2sb[gi % 2][:, rc * 512:(rc + 1) * 512],
                                        start=(rc == 0), stop=(rc == NRC - 1))
                                mmn = s3_mm.inc(ins)

                @blk.scalar
                def _(sc):
                    sc.wait_ge(s3_mrb.h, 1)
                    s3_mrbe.inc(sc.copy(mbc3[:, :], pp[6][:, :]))
                    s3_mrbe.inc(sc.copy(rbc3[:, :], pp[7][:, :]))
                    sc.wait_ge(s3_b1m.h, 2 * NIC)
                    s3_b1e.inc(sc.copy(b1c[:, :], pp[0][:, 0:2 * NIC]))
                    for th in range(1, NTH):
                        sc.wait_ge(s3_mrb.h, th + 1)
                        s3_mrbe.inc(sc.copy(mbc3[:, :], pp[6][:, :]))
                        s3_mrbe.inc(sc.copy(rbc3[:, :], pp[7][:, :]))
                    for th in range(NTH):
                        for ig in range(NIC):
                            sc.wait_ge(s3_mm.h, _h_mm[th * NIC + ig][0])
                            if th * NIC + ig >= 2:
                                sc.wait_ge(s3_glu.h, th * NIC + ig - 1)
                            s3_eva.inc(sc.activation(
                                g_sb[ig % 2][:, :], pp[2 * (ig % 2)][:, :],
                                AF.Silu, bias=b1c[:, ig:ig + 1]))
                            sc.wait_ge(s3_mm.h, _h_mm[th * NIC + ig][1])
                            s3_eva.inc(sc.activation(
                                up_sb[ig % 2][:, :], pp[2 * (ig % 2) + 1][:, :],
                                AF.Identity, bias=b1c[:, NIC + ig:NIC + ig + 1]))
                        for j in range(8):
                            for ts in range(4):
                                m = (th * 8 + j) * 4 + ts
                                sc.wait_ge(s3_evd.h, _fevd[m])
                                s3_fst[m % 4].inc(sc.dma_start(
                                    out=ffn_d[th * 512 + ts * 128:
                                              th * 512 + (ts + 1) * 128,
                                              j * 512:(j + 1) * 512],
                                    in_=ffn_stg[m % 4][:, :]), 16)
                    for i in range(4):
                        sc.wait_ge(s3_fst[i].h, s3_fst[i].n)

                @blk.vector
                def _(v):
                    for th in range(NTH):
                        v.wait_ge(s3_mm.h, NFC * (th + 1))
                        for rc in range(NRC):
                            s3_evd.inc(v.tensor_copy(
                                P_sb[:, rc * OWN + th * 512:
                                     rc * OWN + (th + 1) * 512],
                                pp[rc][:, :]))
                    for th in range(NTH):
                        for ig in range(NIC):
                            gidx2 = (th * NIC + ig) * 2
                            v.wait_ge(s3_eva.h, gidx2 + 2)
                            s3_glu.inc(v.tensor_tensor(
                                act_sb[:, ig * 512:(ig + 1) * 512],
                                g_sb[ig % 2][:, :], up_sb[ig % 2][:, :],
                                op=ALU.mult))
                        v.wait_ge(s3_mm.h, _z_mm[th])
                        for rc in range(NRC):
                            s3_evd.inc(v.tensor_copy(
                                Z_sb[:, rc * 512:(rc + 1) * 512], pp[rc][:, :]))
                        for j in range(8):
                            for ts in range(4):
                                m = (th * 8 + j) * 4 + ts
                                v.wait_ge(s3_mm.h, _f_mm[th][j * 4 + ts])
                                if m >= 4:
                                    v.wait_ge(s3_fst[m % 4].h,
                                              (m // 4) * 16)
                                s3_evd.inc(v.tensor_copy(
                                    ffn_stg[m % 4][:, :], pp[ts][:, :]))

        s4_u = sem("s4_u")
        s4_wo = [sem(f"s4_wo{i}") for i in range(3)]
        s4_bb = sem("s4_bb")
        s4_bbv = sem("s4_bbv")
        s4_bbm = sem("s4_bbm")
        s4_bbe = sem("s4_bbe")
        s4_mm = sem("s4_mm")
        s4_add = sem("s4_add")
        s4_ffn = [sem("s4_ffn0"), sem("s4_ffn1")]
        s4_x = [sem("s4_x0"), sem("s4_x1")]
        s4_out = [sem("s4_out0"), sem("s4_out1")]
        NTS = OWN // 128

        with ExitStack() as ph4:
            sb = lambda n, s, d: ph4.enter_context(nc.sbuf_tensor(n, s, d))
            pq = [ph4.enter_context(nc.psum_tensor(f"pq{i}", [128, 512], F32))
                  for i in range(5)]
            u_sb = sb("u_sb", [128, 32 * OWN], BF16)
            wo_sb = [sb(f"wo_sb{i}", [128, 32 * 512], BF16) for i in range(3)]
            bbbc = sb("bbbc", [128, D], F32)
            b2_r = sb("b2_r", [1, D], F32)
            t1b = [sb(f"t1b{i}", [128, 512], F32) for i in range(2)]
            toutb = [sb(f"toutb{i}", [128, 512], F32) for i in range(2)]
            ffn_rd = [sb(f"ffn_rd{i}", [128, 512], BF16) for i in range(2)]
            x_rd = [sb(f"x_rd{i}", [128, 512], F32) for i in range(2)]

            with nc.Block() as blk:
                @blk.sync
                def _(sy):
                    s4_bb.inc(sy.dma_start(out=bbbc[0:1, :], in_=bo[:, :]), 16)
                    s4_bb.inc(sy.dma_start(out=b2_r[:, :], in_=b2[:, :]), 16)
                    sy.wait_ge(s_a2a.h, 1)
                    for dc in range(32):
                        src = u_out[dc // 4, (dc % 4) * 128:(dc % 4 + 1) * 128, :]
                        s4_u.inc(sy.dma_start(
                            out=u_sb[:, dc * OWN:(dc + 1) * OWN], in_=src), 16)
                    for j in range(8):
                        for ts in range(NTS):
                            m = j * NTS + ts
                            if m >= 2:
                                sy.wait_ge(s4_add.h, m - 1)
                            s4_x[m % 2].inc(sy.dma_start(
                                out=x_rd[m % 2][:, :],
                                in_=x_own[ts * 128:(ts + 1) * 128,
                                          j * 512:(j + 1) * 512]), 16)
                            s4_ffn[m % 2].inc(sy.dma_start(
                                out=ffn_rd[m % 2][:, :],
                                in_=ffn_d[ts * 128:(ts + 1) * 128,
                                          j * 512:(j + 1) * 512]), 16)

                @blk.gpsimd
                def _(g):
                    for j in range(8):
                        if j >= 3:
                            g.wait_ge(s4_mm.h, NTS * (j - 2))
                        for dc in range(32):
                            s4_wo[j % 3].inc(g.dma_start(
                                out=wo_sb[j % 3][:, dc * 512:(dc + 1) * 512],
                                in_=wo_t[dc * 128:(dc + 1) * 128,
                                         j * 512:(j + 1) * 512]), 16)

                @blk.vector
                def _(v):
                    for j in range(8):
                        for ts in range(NTS):
                            m = j * NTS + ts
                            v.wait_ge(s4_mm.h, m + 1)
                            v.wait_ge(s4_bbe.h, 8)
                            v.tensor_tensor(t1b[m % 2][:, :], pq[ts % 4][:, :],
                                            bbbc[:, j * 512:(j + 1) * 512],
                                            op=ALU.add)
                            v.drain()
                            v.wait_ge(s4_ffn[m % 2].h, (m // 2 + 1) * 16)
                            v.tensor_tensor(t1b[m % 2][:, :], t1b[m % 2][:, :],
                                            ffn_rd[m % 2][:, :], op=ALU.add)
                            v.drain()
                            v.wait_ge(s4_x[m % 2].h, (m // 2 + 1) * 16)
                            if m >= 2:
                                v.wait_ge(s4_out[m % 2].h, (m // 2) * 16)
                            s4_add.inc(v.tensor_tensor(
                                toutb[m % 2][:, :], t1b[m % 2][:, :],
                                x_rd[m % 2][:, :], op=ALU.add))
                            v.drain()

                @blk.tensor
                def _(t):
                    t.wait_ge(s4_bb.h, 32)
                    for j in range(8):
                        t.matmul(pq[4][:, :], sb_ones_f[:, :],
                                 bbbc[0:1, j * 512:(j + 1) * 512],
                                 start=True, stop=False)
                        ins = t.matmul(pq[4][:, :], sb_ones_f[:, :],
                                       b2_r[0:1, j * 512:(j + 1) * 512],
                                       start=False, stop=True)
                        s4_bbm.inc(ins)
                        t.wait_ge(s4_bbe.h, j + 1)
                    t.wait_ge(s4_u.h, 32 * 16)
                    for j in range(8):
                        t.wait_ge(s4_wo[j % 3].h, (j // 3 + 1) * 32 * 16)
                        for ts in range(NTS):
                            m = j * NTS + ts
                            if m >= 4:
                                t.wait_ge(s4_add.h, m - 3)
                            ins = None
                            for dc in range(32):
                                ins = t.matmul(
                                    pq[ts % 4][:, :],
                                    u_sb[:, dc * OWN + ts * 128:
                                         dc * OWN + (ts + 1) * 128],
                                    wo_sb[j % 3][:, dc * 512:(dc + 1) * 512],
                                    start=(dc == 0), stop=(dc == 31))
                            s4_mm.inc(ins)

                @blk.scalar
                def _(sc):
                    for j in range(8):
                        sc.wait_ge(s4_bbm.h, j + 1)
                        s4_bbe.inc(sc.copy(bbbc[:, j * 512:(j + 1) * 512],
                                           pq[4][:, :]))
                    for j in range(8):
                        for ts in range(NTS):
                            m = j * NTS + ts
                            sc.wait_ge(s4_add.h, m + 1)
                            s4_out[m % 2].inc(sc.dma_start(
                                out=out_own[ts * 128:(ts + 1) * 128,
                                            j * 512:(j + 1) * 512],
                                in_=toutb[m % 2][:, :]), 16)
                    sc.wait_ge(s4_out[0].h, s4_out[0].n)
                    sc.wait_ge(s4_out[1].h, s4_out[1].n)

    return nc


def _shard_inputs(inputs):
    f = lambda a: np.ascontiguousarray(np.asarray(a, dtype=np.float32))
    x = f(inputs["L0_attn_x"]).reshape(T, D)
    Wq, Wk, Wv = f(inputs["L0_attn_Wq"]), f(inputs["L0_attn_Wk"]), f(inputs["L0_attn_Wv"])
    bq, bk, bv = f(inputs["L0_attn_bq"]), f(inputs["L0_attn_bk"]), f(inputs["L0_attn_bv"])
    Wo, bo = f(inputs["L0_attn_Wo"]), f(inputs["L0_attn_bo"])
    U1, V1 = f(inputs["L0_ffn_U1"]), f(inputs["L0_ffn_V1"])
    U2, V2 = f(inputs["L0_ffn_U2"]), f(inputs["L0_ffn_V2"])
    b1, b2 = f(inputs["L0_ffn_b1"]), f(inputs["L0_ffn_b2"])
    x_f = np.ascontiguousarray(x.T)
    x_q = np.ascontiguousarray(x_f.reshape(D, NCH, CHK).transpose(1, 0, 2))
    Wo_t = np.ascontiguousarray(Wo.T)
    in_maps = []
    for c in range(8):
        sl = slice(c * HSL, (c + 1) * HSL)
        in_maps.append({
            "x": x,
            "x_own": np.ascontiguousarray(x[c * OWN:(c + 1) * OWN]),
            "x_q": x_q,
            "xf_own": np.ascontiguousarray(x_f[:, c * OWN:(c + 1) * OWN]),
            "wq_t": np.ascontiguousarray(Wq[sl].T),
            "wk_t": np.ascontiguousarray(Wk[sl].T),
            "wv_t": np.ascontiguousarray(Wv[sl].T),
            "bq": np.ascontiguousarray(bq[sl]).reshape(1, HSL),
            "bk": np.ascontiguousarray(bk[sl]).reshape(1, HSL),
            "bv": np.ascontiguousarray(bv[sl]).reshape(1, HSL),
            "wo_t": Wo_t,
            "bo": bo.reshape(1, D),
            "u1": U1, "v1": V1, "u2": U2, "v2": V2,
            "b1": b1.reshape(1, 2 * INTER), "b2": b2.reshape(1, D),
        })
    return in_maps


def kernel(**inputs):
    nc = build()
    in_maps = _shard_inputs(inputs)
    res = run_bass_kernel_spmd(nc, in_maps, core_ids=list(range(8)))
    out = np.concatenate(
        [np.asarray(res.results[c]["out_own"], dtype=np.float32)
         for c in range(8)], axis=0)
    return np.ascontiguousarray(out.reshape(2, SEQ, D))
